# version 01
# baseline (speedup 1.0000x reference)
import sys
import numpy as np
from contextlib import ExitStack

sys.path.insert(0, "/opt/trn_rl_repo")
import concourse.bass as bass
import concourse.mybir as mybir
from concourse.bass_utils import run_bass_kernel_spmd

import ml_dtypes

F32 = mybir.dt.float32
BF16 = mybir.dt.bfloat16
I32 = mybir.dt.int32
AF = mybir.ActivationFunctionType
ALU = mybir.AluOpType
AX = mybir.AxisListType

ENGINES = ("tensor", "vector", "scalar", "gpsimd", "sync")

B, S, D = 2, 4096, 4096
PER = 1024
NQT = PER // 128
NK = D // 128


class Sched:

    def __init__(self, nc, es, block):
        self.nc, self.block = nc, block
        self.sems = {e: es.enter_context(nc.semaphore(f"s_{e}")) for e in ENGINES}
        self.cnt = {e: 0 for e in ENGINES}
        self.hw = {e: {} for e in ENGINES}
        self.last_dma = {e: None for e in ENGINES}
        self.nops = 0

    SERIAL = False
    SERIAL_ENGS = ()

    def op(self, eng, fn, deps=(), dma=False):
        if Sched.SERIAL or eng in Sched.SERIAL_ENGS:
            deps = list(deps) + [(e, self.cnt[e]) for e in ENGINES
                                 if self.cnt[e] > 0]
        if dma:
            deps = list(deps) + [self.last_dma[eng]]
        hw = self.hw[eng]
        waits = []
        for d in deps:
            if d is None:
                continue
            seng, val = d
            if seng == eng:
                if eng == "tensor":
                    continue
                if val > self.cnt[eng]:
                    raise RuntimeError("self-dep on future op")
            if hw.get(seng, 0) < val:
                waits.append((seng, val))
                hw[seng] = val
        inc = 16 if dma else 1
        self.cnt[eng] += inc
        tok = (eng, self.cnt[eng])
        if dma:
            self.last_dma[eng] = tok
        sems = self.sems
        self.nops += 1

        def body(q):
            for seng, val in waits:
                q.wait_ge(sems[seng], val)
            fn(q).then_inc(sems[eng], inc)
        getattr(self.block, eng)(body)
        return tok

    def finish(self, deps):
        self.op("sync", lambda q: q.nop(), deps=deps)


_POS_IDX = [np.asarray([p for p in range(S) if (p // 32) % 4 == j])
            for j in range(4)]


def core_positions(j):
    return _POS_IDX[j]


def shard_rows(arr):
    out = []
    for c in range(8):
        b, j = divmod(c, 4)
        out.append(np.ascontiguousarray(arr[b, _POS_IDX[j]]))
    return out


def unshard_rows(shards):
    sh = shards[0].shape[1:]
    out = np.empty((B, S) + sh, dtype=shards[0].dtype)
    for c in range(8):
        b, j = divmod(c, 4)
        out[b, _POS_IDX[j]] = shards[c]
    return out


def tile_w(w_do_di):
    do, di = w_do_di.shape
    nm, nk = do // 128, di // 128
    t = w_do_di.reshape(nm, 128, nk, 128).transpose(0, 3, 2, 1)
    return np.ascontiguousarray(t.reshape(nm, 128, nk * 128))


def bias_cols(b_vec):
    nm = b_vec.shape[0] // 128
    return np.ascontiguousarray(b_vec.reshape(nm, 128).T)


def emit_ln(s, P, x_dram, n_tok=PER, eps=1e-5):
    nqt = n_tok // 128
    lnt = P["lnt"]
    eps_t = P["eps"]
    t_eps = s.op("vector", lambda q: q.memset(eps_t[:], eps))
    evac = [[None] * 4 for _ in range(nqt)]
    t_norm_prev = [None, None]
    t_evac_prev = [None, None]
    t_tp_last = [None]
    for qt in range(nqt):
        xb_t = (P["xa"], P["xb"])[qt % 2]
        t_x = s.op("sync", lambda q, qt=qt, xb_t=xb_t: q.dma_start(
            out=xb_t[:], in_=x_dram[qt * 128:(qt + 1) * 128, :]),
            deps=[t_norm_prev[qt % 2]], dma=True)
        par = qt % 2
        t_sts = []
        for g in range(8):
            st = P["stats"][:, par * 48 + g * 6:par * 48 + (g + 1) * 6]
            t_sts.append(s.op("vector", lambda q, g=g, st=st, xb_t=xb_t: q.bn_stats(
                st, xb_t[:, g * 512:(g + 1) * 512]), deps=[t_x]))
        mvv = P["mv"][:, par * 4:par * 4 + 2]
        lnvv = P["mv"][:, par * 4 + 2:par * 4 + 3]
        nmv = P["mv"][:, par * 4 + 3:par * 4 + 4]
        rstdv = P["rstd"][:, par:par + 1]
        t_mv = s.op("vector", lambda q, par=par, mvv=mvv: q.bn_aggr(
            mvv, P["stats"][:, par * 48:par * 48 + 48]), deps=t_sts)
        t_lnv = s.op("scalar", lambda q, mvv=mvv, lnvv=lnvv: q.activation(
            out=lnvv, in_=mvv[:, 1:2], func=AF.Ln, bias=eps_t[:]),
            deps=[t_mv, t_eps])
        t_rstd = s.op("scalar", lambda q, lnvv=lnvv, rstdv=rstdv: q.activation(
            out=rstdv, in_=lnvv, func=AF.Exp, scale=-0.5), deps=[t_lnv])
        t_nm = s.op("vector", lambda q, mvv=mvv, nmv=nmv: q.tensor_scalar(
            nmv, mvv[:, 0:1], -1.0, None, op0=ALU.mult), deps=[t_mv])
        t_norm = s.op("vector", lambda q, xb_t=xb_t, nmv=nmv, rstdv=rstdv:
                      q.tensor_scalar(P["lntok"][:], xb_t[:], nmv, rstdv,
                                      op0=ALU.add, op1=ALU.mult),
                      deps=[t_x, t_nm, t_rstd, t_tp_last[0]])
        t_norm_prev[qt % 2] = t_norm
        for g in range(4):
            p = P["pLN"][g % 2]
            tps = []
            for kk in range(8):
                k = g * 8 + kk
                tps.append(s.op("tensor", lambda q, k=k, kk=kk, p=p: q.transpose(
                    p[:, kk * 128:(kk + 1) * 128],
                    P["lntok"][:, k * 128:(k + 1) * 128], P["id_bf"][:]),
                    deps=[t_norm, t_evac_prev[g % 2], P.get("t_idbf")]))
            t_tp_last[0] = tps[-1]
            dst = lnt[:].rearrange("p (k t) -> p k t", t=n_tok)[
                :, g * 8:(g + 1) * 8, qt * 128:(qt + 1) * 128]
            src = p[:].rearrange("p (k t) -> p k t", t=128)
            t_e = s.op("vector", lambda q, src=src, dst=dst: q.tensor_copy(dst, src),
                       deps=tps)
            t_evac_prev[g % 2] = t_e
            evac[qt][g] = t_e
    kdep = {k: [evac[qt][k // 8] for qt in range(nqt)] for k in range(32)}
    return kdep


def emit_residual_group(s, P, x_dram, y_dram, m_group, dfeat_deps, prev_guard,
                        n_tok=PER):
    nqt = n_tok // 128
    g0 = m_group[0]
    ng = len(m_group)
    pR = P["pR"]
    stores = []
    rs = P.setdefault("resid_state", {"add": [None, None], "store": [None, None]})
    t_add_prev = rs["add"]
    t_store_prev = rs["store"]
    t_tp_prev = prev_guard
    t_add_last = None
    for qt in range(nqt):
        xs = (P["xs0"], P["xs1"])[qt % 2]
        ys = (P["ys0"], P["ys1"])[qt % 2]
        t_x = s.op("sync", lambda q, qt=qt, xs=xs: q.dma_start(
            out=xs[:, 0:ng * 128],
            in_=x_dram[qt * 128:(qt + 1) * 128, g0 * 128:g0 * 128 + ng * 128]),
            deps=[t_add_prev[qt % 2]], dma=True)
        tps = []
        for i in range(ng):
            tps.append(s.op("tensor", lambda q, i=i, qt=qt: q.transpose(
                pR[:, i * 128:(i + 1) * 128],
                P["dfeat"][:, i * 1024 + qt * 128:i * 1024 + (qt + 1) * 128],
                P["id_f32"][:]),
                deps=dfeat_deps + [t_tp_prev, t_add_last, P.get("t_idf")]))
        t_tp_prev = tps[-1]
        t_add = s.op("vector", lambda q, xs=xs, ys=ys: q.tensor_tensor(
            ys[:, 0:ng * 128], pR[:, 0:ng * 128], xs[:, 0:ng * 128], op=ALU.add),
            deps=tps + [t_x, t_store_prev[qt % 2]])
        t_add_prev[qt % 2] = t_add
        t_add_last = t_add
        tst = s.op("sync", lambda q, qt=qt, ys=ys: q.dma_start(
            out=y_dram[qt * 128:(qt + 1) * 128, g0 * 128:g0 * 128 + ng * 128],
            in_=ys[:, 0:ng * 128]), deps=[t_add], dma=True)
        t_store_prev[qt % 2] = tst
        stores.append(tst)
    return stores


def ts_op(s, out, in0, s1, s2, op0, op1=None, deps=(), eng="vector"):
    if op1 is None:
        return s.op(eng, lambda q: q.tensor_scalar(out, in0, s1, None, op0=op0),
                    deps=deps)
    return s.op(eng, lambda q: q.tensor_scalar(out, in0, s1, s2, op0=op0, op1=op1),
                deps=deps)


def tt_op(s, out, a, b, op, deps=(), eng="vector"):
    return s.op(eng, lambda q: q.tensor_tensor(out, a, b, op=op), deps=deps)


def stt_op(s, out, in0, sc, in1, op0, op1, deps=(), eng="vector"):
    return s.op(eng, lambda q: q.scalar_tensor_tensor(out, in0, sc, in1,
                                                      op0=op0, op1=op1), deps=deps)


def emit_searchsorted_wm(s, P, lsi_dram, bvec_sb, t_pr_free):
    sc = P["sc"]
    t0 = s.op("sync", lambda q: q.dma_start(out=P["lsi_i"][:], in_=lsi_dram[:]),
              dma=True)
    lsi_f = sc[:, 0:4]
    t1 = s.op("vector", lambda q: q.tensor_copy(lsi_f, P["lsi_i"][:]), deps=[t0])
    iota3 = P["cvec_sb"][:, 0:3]
    c = sc[:, 4:8]
    nb = sc[:, 28:29]
    d0 = ts_op(s, nb, bvec_sb, -1.0, None, ALU.mult, deps=[P["t_bvec"]])
    d = ts_op(s, c, lsi_f, nb, None, ALU.add, deps=[t1, d0])
    d = ts_op(s, c, c, 0.0, 1.0, ALU.max, ALU.min, deps=[d])
    c0, c1, c2, c3 = (c[:, j:j + 1] for j in range(4))
    u, t1v, t2v, v, w = (sc[:, 8 + j:9 + j] for j in range(5))
    r1, nc2, r2, res = (sc[:, 13 + j:14 + j] for j in range(4))
    d = ts_op(s, u, c0, -1.0, 1.0, ALU.mult, ALU.add, deps=[d])
    d = tt_op(s, t1v, c1, u, ALU.mult, deps=[d])
    d = ts_op(s, t2v, c1, -2.0, 2.0, ALU.mult, ALU.add, deps=[d])
    d = tt_op(s, v, t1v, t2v, ALU.add, deps=[d])
    d = ts_op(s, w, c3, -1.0, 4.0, ALU.mult, ALU.add, deps=[d])
    d = tt_op(s, r1, c2, v, ALU.mult, deps=[d])
    d = ts_op(s, nc2, c2, -1.0, 1.0, ALU.mult, ALU.add, deps=[d])
    d = tt_op(s, r2, nc2, w, ALU.mult, deps=[d])
    d = tt_op(s, res, r1, r2, ALU.add, deps=[d])
    seg, aidx, use, eq3 = sc[:, 17:18], sc[:, 18:19], sc[:, 19:20], sc[:, 20:23]
    d = ts_op(s, seg, res, -1.0, None, ALU.add, deps=[d])
    d2 = ts_op(s, aidx, seg, -1.0, 0.0, ALU.add, ALU.max, deps=[d])
    d2 = ts_op(s, aidx, aidx, 2.0, None, ALU.min, deps=[d2])
    d3 = ts_op(s, use, seg, 0.0, 1.0, ALU.max, ALU.min, deps=[d])
    d4 = ts_op(s, eq3, iota3, aidx, None, ALU.is_equal, deps=[d2, P["t_cvec"]])
    d4 = ts_op(s, P["wm3"][:], eq3, use, None, ALU.mult, deps=[d4, d3])
    t_o = s.op("vector", lambda q: q.memset(P["ones11"][:], 1.0))
    ps = P["pR"]
    t_m1 = s.op("tensor", lambda q: q.matmul(ps[0:3, 0:1], P["wm3"][0:1, :],
                                             P["ones11"][:], start=True, stop=True),
                deps=[d4, t_o, t_pr_free])
    t_e1 = s.op("vector", lambda q: q.tensor_copy(P["wm3t"][:], ps[0:3, 0:1]),
                deps=[t_m1])
    t_m2 = s.op("tensor", lambda q: q.matmul(ps[0:48, 1:2], P["sel_sb"][:],
                                             P["wm3t"][:], start=True, stop=True),
                deps=[t_e1])
    t_e2 = s.op("vector", lambda q: q.tensor_copy(P["wm48"][:], ps[0:48, 1:2]),
                deps=[t_m2])
    return t_e2, P["wm48"]


def alloc_common(nc, es, P, n_tok=PER):
    sb, ps = nc.sbuf_tensor, nc.psum_tensor
    P["xa"] = es.enter_context(sb("xa", [128, 4096], F32))
    P["xb"] = es.enter_context(sb("xb", [128, 4096], F32))
    P["lntok"] = es.enter_context(sb("lntok", [128, 4096], BF16))
    P["stats"] = es.enter_context(sb("stats", [128, 96], F32))
    P["mv"] = es.enter_context(sb("mv", [128, 8], F32))
    P["rstd"] = es.enter_context(sb("rstd", [128, 2], F32))
    P["eps"] = es.enter_context(sb("eps", [128, 1], F32))
    P["lnt"] = es.enter_context(sb("lnt", [128, 32 * n_tok], BF16))
    P["id_bf"] = es.enter_context(sb("id_bf", [128, 128], BF16))
    P["id_f32"] = es.enter_context(sb("id_f32", [128, 128], F32))
    P["pLN"] = [es.enter_context(ps(f"pLN{i}", [128, 1024], BF16)) for i in range(2)]
    P["pD"] = [es.enter_context(ps(f"pD{i}", [128, 1024], F32)) for i in range(2)]
    P["pR"] = es.enter_context(ps("pR", [128, 1024], F32))
    P["xs0"] = es.enter_context(sb("xs0", [128, 1024], F32))
    P["xs1"] = es.enter_context(sb("xs1", [128, 1024], F32))
    P["ys0"] = es.enter_context(sb("ys0", [128, 1024], F32))
    P["ys1"] = es.enter_context(sb("ys1", [128, 1024], F32))


def load_ident(s, P, idf_dram):
    t1 = s.op("sync", lambda q: q.dma_start(out=P["id_f32"][:], in_=idf_dram[:]),
              dma=True)
    t2 = s.op("gpsimd", lambda q: q.dma_start(out=P["id_bf"][:], in_=idf_dram[:]),
              dma=True)
    P["t_idf"] = t1
    P["t_idbf"] = t2
    return [t1, t2]


def emit_router(s, P, kdep, t_wr, t_rep, t_he):
    lnt = P["lnt"]
    pL = P["pD"][0]
    pA = P["pR"]
    comb_done = []
    t_prev_tp = t_he
    for qt in range(8):
        t_mm = None
        for k in range(32):
            t_mm = s.op("tensor", lambda q, k=k, qt=qt: q.matmul(
                pL[:, 0:8],
                lnt[:, k * PER + qt * 128:k * PER + (qt + 1) * 128],
                P["wr_sb"][:, k * 8:(k + 1) * 8],
                start=(k == 0), stop=(k == 31)),
                deps=[t_wr, comb_done[-1] if comb_done else None] + kdep[k])
        c = P["rt"]
        m1n = c[:, 0:1]
        d = s.op("vector", lambda q: q.reduce_max(m1n, pL[:, 0:8], axis=AX.X,
                                                  negate=True), deps=[t_mm])
        ee = P["comb"][:, 0:8]
        d = s.op("scalar", lambda q, ee=ee: q.activation(
            out=ee, in_=pL[:, 0:8], func=AF.Exp, bias=m1n), deps=[d])
        m1e, mask1, keep = c[:, 1:2], c[:, 2:10], c[:, 10:18]
        p2m, m2, den, rec = c[:, 18:26], c[:, 26:27], c[:, 27:28], c[:, 28:29]
        d = s.op("vector", lambda q, ee=ee: q.reduce_max(m1e, ee, axis=AX.X),
                 deps=[d])
        d = ts_op(s, mask1, ee, m1e, None, ALU.is_ge, deps=[d])
        d = ts_op(s, keep, mask1, -1.0, 1.0, ALU.mult, ALU.add, deps=[d])
        d = tt_op(s, p2m, ee, keep, ALU.mult, deps=[d])
        d = s.op("vector", lambda q: q.reduce_max(m2, p2m, axis=AX.X), deps=[d])
        d = ts_op(s, mask1, ee, m2, None, ALU.is_ge, deps=[d])
        d = tt_op(s, p2m, ee, mask1, ALU.mult, deps=[d])
        d2 = tt_op(s, den, m1e, m2, ALU.add, deps=[d])
        d2 = s.op("vector", lambda q: q.reciprocal(rec, den), deps=[d2])
        d2 = ts_op(s, rec, rec, 2.0, None, ALU.mult, deps=[d2])
        d2 = ts_op(s, p2m, p2m, rec, None, ALU.mult, deps=[d2])
        t_tp = s.op("tensor", lambda q, qt=qt: q.transpose(
            pA[0:8, qt * 128:(qt + 1) * 128], p2m, P["id_f32"][:]),
            deps=[d2, t_prev_tp, P.get("t_idf")])
        t_prev_tp = t_tp
        comb_done.append(t_tp)
    t_ct = s.op("vector", lambda q: q.tensor_copy(P["combT"][:], pA[0:8, 0:1024]),
                deps=comb_done)
    t_ex = []
    for h in range(2):
        t_ex.append(s.op("tensor", lambda q, h=h: q.matmul(
            pA[:, h * 512:(h + 1) * 512], P["rep_sb"][:],
            P["combT"][:, h * 512:(h + 1) * 512], start=True, stop=True),
            deps=[t_ct, t_rep]))
    t_c128 = s.op("vector", lambda q: q.tensor_copy(P["comb128"][:], pA[:, 0:1024]),
                  deps=t_ex)
    t_hw = s.op("vector", lambda q: q.tensor_tensor(
        P["hw"][:], P["hfeat"][:], P["comb128"][:], op=ALU.mult),
        deps=[t_c128, t_he])
    return t_hw, t_c128


def build_ffn(kind, core, dbg=False):
    nc = bass.Bass("TRN2", target_bir_lowering=False)
    x = nc.dram_tensor("x", [PER, 4096], F32, kind="ExternalInput").ap()
    wt = nc.dram_tensor("wt", [32, 128, 4096], F32, kind="ExternalInput").ap()
    bias_t = nc.dram_tensor("bias_t", [128, 32], F32, kind="ExternalInput").ap()
    idf = nc.dram_tensor("ident", [128, 128], F32, kind="ExternalInput").ap()
    cvec = nc.dram_tensor("cvec", [128, 8], F32, kind="ExternalInput").ap()
    R = 48 if kind == "l0" else 128
    bvec = nc.dram_tensor("bvec", [128, 1], F32, kind="ExternalInput").ap()
    acat = nc.dram_tensor("acat", [4096, R], F32, kind="ExternalInput").ap()
    bcat = nc.dram_tensor("bcat", [R, 4096], F32, kind="ExternalInput").ap()
    if kind == "l0":
        lsi = nc.dram_tensor("lsi", [128, 4], I32, kind="ExternalInput").ap()
        sel = nc.dram_tensor("sel", [3, 48], F32, kind="ExternalInput").ap()
    else:
        wr = nc.dram_tensor("wr", [4096, 8], F32, kind="ExternalInput").ap()
        rep = nc.dram_tensor("rep", [8, 128], F32, kind="ExternalInput").ap()
    y = nc.dram_tensor("y", [PER, 4096], F32, kind="ExternalOutput").ap()
    if dbg:
        lnt_d = nc.dram_tensor("lnt_d", [128, 32 * PER], BF16,
                               kind="ExternalOutput").ap()
        sc_d = nc.dram_tensor("sc_d", [128, 32], F32, kind="ExternalOutput").ap()
        wm_d = nc.dram_tensor("wm_d", [48, 1], F32, kind="ExternalOutput").ap()

    with ExitStack() as es:
        P = {}
        sb = nc.sbuf_tensor
        alloc_common(nc, es, P)
        P["wa"] = es.enter_context(sb("wa", [128, 4096], BF16))
        P["wb"] = es.enter_context(sb("wb", [128, 4096], BF16))
        P["bias_sb"] = es.enter_context(sb("bias_sb", [128, 32], F32))
        P["acat_sb"] = es.enter_context(sb("acat_sb", [128, 32 * R], BF16))
        P["bcat_sb"] = es.enter_context(sb("bcat_sb", [R, 4096], BF16))
        P["hfeat"] = es.enter_context(sb("hfeat", [R, 1024], BF16))
        P["dfeat"] = es.enter_context(sb("dfeat", [128, 8 * 1024], F32))
        P["cvec_sb"] = es.enter_context(sb("cvec_sb", [128, 8], F32))
        if kind == "l0":
            P["bvec_sb"] = es.enter_context(sb("bvec_sb", [128, 1], F32))
            P["sc"] = es.enter_context(sb("sc", [128, 32], F32))
            P["lsi_i"] = es.enter_context(sb("lsi_i", [128, 4], I32))
            P["wm3"] = es.enter_context(sb("wm3", [128, 3], BF16))
            P["wm3t"] = es.enter_context(sb("wm3t", [3, 1], BF16))
            P["wm48"] = es.enter_context(sb("wm48", [48, 1], F32))
            P["ones11"] = es.enter_context(sb("ones11", [1, 1], BF16))
            P["sel_sb"] = es.enter_context(sb("sel_sb", [3, 48], BF16))
            P["bcat_sc"] = es.enter_context(sb("bcat_sc", [R, 4096], BF16))
        else:
            P["wr_sb"] = es.enter_context(sb("wr_sb", [128, 32 * 8], BF16))
            P["rep_sb"] = es.enter_context(sb("rep_sb", [8, 128], BF16))
            P["rt"] = es.enter_context(sb("rt", [128, 32], F32))
            P["comb"] = es.enter_context(sb("comb", [128, 8], F32))
            P["combT"] = es.enter_context(sb("combT", [8, 1024], BF16))
            P["comb128"] = es.enter_context(sb("comb128", [128, 1024], BF16))
            P["hw"] = es.enter_context(sb("hw", [128, 1024], BF16))
        block = es.enter_context(nc.Block())
        s = Sched(nc, es, block)

        t_id = load_ident(s, P, idf)
        t_cvec = s.op("sync", lambda q: q.dma_start(out=P["cvec_sb"][:], in_=cvec[:]),
                      dma=True)
        P["t_cvec"] = t_cvec
        t_b = s.op("sync", lambda q: q.dma_start(out=P["bias_sb"][:], in_=bias_t[:]),
                   dma=True)
        acat3 = acat.rearrange("(k p) r -> p k r", p=128)
        t_a = s.op("gpsimd", lambda q: q.dma_start(
            out=P["acat_sb"][:].rearrange("p (k r) -> p k r", r=R), in_=acat3),
            dma=True)
        t_bc = s.op("gpsimd", lambda q: q.dma_start(out=P["bcat_sb"][:], in_=bcat[:]),
                    dma=True)
        if kind == "l0":
            t_sel = s.op("gpsimd", lambda q: q.dma_start(out=P["sel_sb"][:],
                                                         in_=sel[:]), dma=True)
            P["t_bvec"] = s.op("sync", lambda q: q.dma_start(
                out=P["bvec_sb"][:], in_=bvec[:]), dma=True)
        else:
            wr3 = wr.rearrange("(k p) r -> p k r", p=128)
            t_wr = s.op("gpsimd", lambda q: q.dma_start(
                out=P["wr_sb"][:].rearrange("p (k r) -> p k r", r=8), in_=wr3),
                dma=True)
            t_rep = s.op("gpsimd", lambda q: q.dma_start(out=P["rep_sb"][:],
                                                         in_=rep[:]), dma=True)

        kdep = emit_ln(s, P, x)
        t_lntd = None
        if dbg:
            alldep = [kdep[k][qt] for k in range(32) for qt in range(8)]
            t_lntd = s.op("sync", lambda q: q.dma_start(out=lnt_d[:],
                          in_=P["lnt"][:]), deps=alldep, dma=True)

        ph = P["pR"][0:R, :]
        t_h_toks = []
        for h in range(2):
            last = None
            for k in range(32):
                last = s.op("tensor", lambda q, k=k, h=h: q.matmul(
                    ph[:, h * 512:(h + 1) * 512],
                    P["acat_sb"][:, k * R:(k + 1) * R],
                    P["lnt"][:, k * PER + h * 512:k * PER + (h + 1) * 512],
                    start=(k == 0), stop=(k == 31)),
                    deps=[t_a, t_id[1]] + kdep[k])
            t_h_toks.append(last)
        t_he = s.op("vector", lambda q: q.tensor_copy(P["hfeat"][:], ph),
                    deps=t_h_toks)

        if kind == "l0":
            t_wm, wm48 = emit_searchsorted_wm(s, P, lsi, P["bvec_sb"][:], t_he)
            if dbg:
                t_d1 = s.op("sync", lambda q: q.dma_start(out=sc_d[:],
                            in_=P["sc"][:]), deps=[t_wm], dma=True)
                t_d2 = s.op("sync", lambda q: q.dma_start(out=wm_d[:],
                            in_=P["wm48"][:]), deps=[t_wm], dma=True)
            t_lhs = s.op("vector", lambda q: q.tensor_scalar(
                P["bcat_sc"][:], P["bcat_sb"][:], wm48[:], None, op0=ALU.mult),
                deps=[t_bc, t_wm])
            lora_lhs, lora_rhs = P["bcat_sc"], P["hfeat"]
            t_router_done = t_wm
        else:
            t_hw, t_router_done = emit_router(s, P, kdep, t_wr, t_rep, t_he)
            t_lhs = t_hw
            lora_lhs, lora_rhs = P["bcat_sb"], P["hw"]
            t_bc2 = t_bc

        store_toks = []
        t_wprev = [None, None]
        t_evac_prev = [None, None]
        dfeat_deps = [None] * 8
        prev_guard = [t_router_done]
        for m in range(32):
            wb_t = (P["wa"], P["wb"])[m % 2]
            t_w = s.op("gpsimd", lambda q, m=m, wb_t=wb_t: q.dma_start(
                out=wb_t[:], in_=wt[m, :, :]), deps=[t_wprev[m % 2]], dma=True)
            p = P["pD"][m % 2]
            half_toks = []
            for h in range(2):
                last = None
                for k in range(32):
                    last = s.op("tensor", lambda q, k=k, h=h, m=m, p=p, wb_t=wb_t:
                                q.matmul(
                        p[:, h * 512:(h + 1) * 512],
                        wb_t[:, k * 128:(k + 1) * 128],
                        P["lnt"][:, k * PER + h * 512:k * PER + (h + 1) * 512],
                        start=(k == 0), stop=False),
                        deps=[t_w, t_evac_prev[m % 2], t_router_done] + kdep[k])
                last = s.op("tensor", lambda q, h=h, m=m, p=p: q.matmul(
                    p[:, h * 512:(h + 1) * 512],
                    lora_lhs[:, m * 128:(m + 1) * 128],
                    lora_rhs[:, h * 512:(h + 1) * 512],
                    start=False, stop=True),
                    deps=[t_lhs, t_he, t_bc])
                half_toks.append(last)
            t_wprev[m % 2] = half_toks[-1]
            dst = P["dfeat"][:, (m % 8) * 1024:(m % 8 + 1) * 1024]
            t_e = s.op("vector", lambda q, p=p, dst=dst, m=m: q.tensor_scalar(
                dst, p[:], P["bias_sb"][:, m:m + 1], None, op0=ALU.add),
                deps=half_toks + [t_b] +
                     ([store_toks[-1]] if store_toks else []))
            t_evac_prev[m % 2] = t_e
            dfeat_deps[m % 8] = t_e
            if m % 8 == 7:
                sts = emit_residual_group(s, P, x, y, list(range(m - 7, m + 1)),
                                          list(dfeat_deps), prev_guard[0])
                store_toks.extend(sts)
                prev_guard[0] = sts[-1]
        s.finish([store_toks[-1]] + ([t_lntd] if dbg else []))
    return nc


_FFN_CACHE = {}


def run_ffn(kind, x_shards, wargs):
    in_maps = []
    for c in range(8):
        m = dict(wargs)
        m["x"] = x_shards[c]
        in_maps.append(m)
    for c in range(8):
        in_maps[c]["bvec"] = np.full((128, 1), float(c // 4), np.float32)
    nc = build_ffn(kind, 0)
    r = run_bass_kernel_spmd(nc, in_maps, core_ids=list(range(8)))
    return [r.results[c]["y"] for c in range(8)]


def prep_ffn_l0(inp):
    return {
        "wt": tile_w(np.asarray(inp["L0_ffn_weight"], np.float32)),
        "bias_t": bias_cols(np.asarray(inp["L0_ffn_bias"], np.float32)),
        "ident": np.eye(128, dtype=np.float32),
        "cvec": np.tile(np.arange(8, dtype=np.float32), (128, 1)),
        "acat": np.ascontiguousarray(
            np.asarray(inp["L0_ffn_lora_A"], np.float32).reshape(48, 4096).T),
        "bcat": np.ascontiguousarray(
            np.asarray(inp["L0_ffn_lora_B"], np.float32).transpose(0, 2, 1)
            .reshape(48, 4096)),
        "lsi": np.tile(np.asarray(inp["L0_ffn_lora_start_indices"], np.int32).reshape(1, 4), (128, 1)),
        "sel": np.ascontiguousarray(
            np.kron(np.eye(3), np.ones((1, 16))).astype(np.float32)),
    }


def prep_ffn_l1(inp):
    return {
        "wt": tile_w(np.asarray(inp["L1_ffn_W_base"], np.float32)),
        "bias_t": bias_cols(np.asarray(inp["L1_ffn_b_base"], np.float32)),
        "ident": np.eye(128, dtype=np.float32),
        "cvec": np.tile(np.arange(8, dtype=np.float32), (128, 1)),
        "acat": np.ascontiguousarray(
            np.asarray(inp["L1_ffn_A"], np.float32).reshape(128, 4096).T),
        "bcat": np.ascontiguousarray(
            np.asarray(inp["L1_ffn_B"], np.float32).transpose(0, 2, 1)
            .reshape(128, 4096)),
        "wr": np.ascontiguousarray(
            np.asarray(inp["L1_ffn_W_router"], np.float32).T),
        "rep": np.ascontiguousarray(
            np.kron(np.eye(8), np.ones((1, 16))).astype(np.float32)),
    }


def kernel(**inputs):
    inp = {k: np.asarray(v) for k, v in inputs.items()}
    x = np.asarray(inp["L0_attn_a_x"], np.float32)
    xs = shard_rows(x)
    ffn0 = prep_ffn_l0(inp)
    ffn1 = prep_ffn_l1(inp)
    xs = run_attn_l0a(xs, inp)
    xs = run_ffn("l0", xs, ffn0)
    xs = run_attn_l0b(xs, inp)
    xs = run_ffn("l0", xs, ffn0)
    xs = run_attn_l1a(xs, inp)
    xs = run_ffn("l1", xs, ffn1)
    xs = run_attn_l1b(xs, inp)
    xs = run_ffn("l1", xs, ffn1)
    return unshard_rows([np.asarray(s, np.float32) for s in xs])


def alloc_ln_bufs(nc, es, P, n_tok=PER):
    sb, ps = nc.sbuf_tensor, nc.psum_tensor
    P["xa"] = es.enter_context(sb("xa", [128, 4096], F32))
    P["xb"] = es.enter_context(sb("xb", [128, 4096], F32))
    P["lntok"] = es.enter_context(sb("lntok", [128, 4096], BF16))
    P["stats"] = es.enter_context(sb("stats", [128, 96], F32))
    P["mv"] = es.enter_context(sb("mv", [128, 8], F32))
    P["rstd"] = es.enter_context(sb("rstd", [128, 2], F32))
    P["eps"] = es.enter_context(sb("eps", [128, 1], F32))
    P["lnt"] = es.enter_context(sb("lnt", [128, 32 * n_tok], BF16))


def alloc_resid_bufs(nc, es, P, gsz=8):
    sb = nc.sbuf_tensor
    P["dfeat"] = es.enter_context(sb("dfeat", [128, gsz * 1024], F32))
    P["xs0"] = es.enter_context(sb("xs0", [128, 1024], F32))
    P["xs1"] = es.enter_context(sb("xs1", [128, 1024], F32))
    P["ys0"] = es.enter_context(sb("ys0", [128, 1024], F32))
    P["ys1"] = es.enter_context(sb("ys1", [128, 1024], F32))


def emit_proj(s, P, wt_dram, nm, nk, rhs_sb, rhs_deps, out_cb, wbufs,
              extra=None, n_tok=PER, first_deps=(), rhs_stride=None):
    if rhs_stride is None:
        rhs_stride = n_tok
    st8 = P.setdefault("proj_state", {"w": [None, None], "out": [None, None]})
    t_wprev = st8["w"]
    t_outprev = st8["out"]
    nh = n_tok // 512
    for m in range(nm):
        wb = wbufs[m % 2]
        t_w = s.op("gpsimd", lambda q, m=m, wb=wb: q.dma_start(
            out=wb[:, 0:nk * 128], in_=wt_dram[m, :, :]),
            deps=[t_wprev[m % 2]], dma=True)
        p = P["pD"][m % 2]
        toks = []
        for h in range(nh):
            last = None
            for ki in range(nk):
                last = s.op("tensor", lambda q, ki=ki, h=h, p=p, wb=wb: q.matmul(
                    p[:, h * 512:(h + 1) * 512],
                    wb[:, ki * 128:(ki + 1) * 128],
                    rhs_sb[:, ki * rhs_stride + h * 512:ki * rhs_stride + (h + 1) * 512],
                    start=(ki == 0), stop=(ki == nk - 1 and extra is None)),
                    deps=[t_w, t_outprev[m % 2]] + list(first_deps) + rhs_deps[ki])
            if extra is not None:
                last = extra(m, h, p, [last])
            toks.append(last)
        t_out = out_cb(m, p, toks)
        t_outprev[m % 2] = t_out
        t_wprev[m % 2] = toks[-1]


def emit_rope_inplace(s, P, hb, cos_t, sin_t, deps, half_rows=64, base=0,
                      width=1024):
    rt = P["rt4"]
    h = half_rows
    x1 = hb[base:base + h, :]
    x2 = hb[base + h:base + 2 * h, :]
    c1 = cos_t[0:h, 0:width]
    s1 = sin_t[0:h, 0:width]
    A = rt[0:h, 0:1024][:, 0:width]
    Bv = rt[0:h, 1024:2048][:, 0:width]
    Cv = rt[0:h, 2048:3072][:, 0:width]
    Dv = rt[0:h, 3072:4096][:, 0:width]
    if base == 0:
        z1 = x1
        tz1 = None
    else:
        z1 = rt[0:h, 2048:3072][:, 0:width]
        z1 = A
        tz1 = s.op("vector", lambda q: q.tensor_copy(z1, x1), deps=deps)
    z2 = Bv
    tz2 = s.op("vector", lambda q: q.tensor_copy(z2, x2), deps=deps)
    pA = Cv
    pD = Dv
    d1 = tt_op(s, pA, z1, c1, ALU.mult, deps=[tz1, tz2] + list(deps))
    d4 = tt_op(s, pD, z1, s1, ALU.mult, deps=[d1])
    d2 = tt_op(s, z1 if base != 0 else A, z2, s1, ALU.mult, deps=[d4])
    x2sin = z1 if base != 0 else A
    d3 = tt_op(s, z2, z2, c1, ALU.mult, deps=[d2])
    e1 = tt_op(s, pA, pA, x2sin, ALU.subtract, deps=[d3])
    e2 = tt_op(s, z2, z2, pD, ALU.add, deps=[e1])
    f1 = s.op("vector", lambda q: q.tensor_copy(x1, pA), deps=[e2])
    f2 = s.op("vector", lambda q: q.tensor_copy(x2, z2), deps=[f1])
    return [f1, f2]


def emit_out_proj_residual(s, P, wo_dram, bias_sb, src_sb, src_deps,
                           x_dram, y_dram, wbufs, extra_out=None, gsz=8):
    store_toks = []
    t_wprev = [None, None]
    t_evac_prev = [None, None]
    dfeat_deps = [None] * gsz
    prev_guard = [None]
    for m in range(32):
        wb = wbufs[m % 2]
        t_w = s.op("gpsimd", lambda q, m=m, wb=wb: q.dma_start(
            out=wb[:], in_=wo_dram[m, :, :]), deps=[t_wprev[m % 2]], dma=True)
        p = P["pD"][m % 2]
        half_toks = []
        for h in range(2):
            last = None
            for k in range(32):
                last = s.op("tensor", lambda q, k=k, h=h, p=p, wb=wb: q.matmul(
                    p[:, h * 512:(h + 1) * 512],
                    wb[:, k * 128:(k + 1) * 128],
                    src_sb[:, k * PER + h * 512:k * PER + (h + 1) * 512],
                    start=(k == 0), stop=True if k == 31 else False),
                    deps=[t_w, t_evac_prev[m % 2]] + src_deps[k])
            half_toks.append(last)
        t_wprev[m % 2] = half_toks[-1]
        dst = P["dfeat"][:, (m % gsz) * 1024:(m % gsz + 1) * 1024]
        if bias_sb is not None:
            t_e = s.op("vector", lambda q, p=p, dst=dst, m=m: q.tensor_scalar(
                dst, p[:], bias_sb[:, m:m + 1], None, op0=ALU.add),
                deps=half_toks + ([store_toks[-1]] if store_toks else []))
        else:
            t_e = s.op("vector", lambda q, p=p, dst=dst: q.tensor_copy(dst, p[:]),
                       deps=half_toks + ([store_toks[-1]] if store_toks else []))
        t_evac_prev[m % 2] = t_e
        dfeat_deps[m % gsz] = t_e
        if m % gsz == gsz - 1:
            sts = emit_residual_group(s, P, x_dram, y_dram,
                                      list(range(m - gsz + 1, m + 1)),
                                      list(dfeat_deps), prev_guard[0])
            store_toks.extend(sts)
            prev_guard[0] = sts[-1]
    return store_toks[-1]


def emit_flash_half(s, P, F, h, st, smm_cb, vbuf, v_dep, ctx_dst,
                    scale, n_q=512):
    nfull = 16 * st
    kcs = list(range(nfull + 16))
    sc = P["fsc"]
    Crun = sc[:, 0:1]
    csb = sc[:, 1:2]
    cnew = sc[:, 2:3]
    aold = sc[:, 3:4]
    anew = sc[:, 4:5]
    mcol = sc[:, 5:9]
    dsc = P["fsd"]
    aoldd, anewd = dsc[0:1, 0:1], dsc[0:1, 1:2]
    ctx_sb = P["ctx_sb"]
    den_sb = P["den_sb"]
    t0 = s.op("vector", lambda q: q.memset(ctx_sb[:], 0.0), deps=[P["t_ctxfree"][0]])
    t0b = s.op("vector", lambda q: q.memset(den_sb[:], 0.0), deps=[t0])
    t0c = s.op("vector", lambda q: q.memset(Crun, -1e30), deps=[t0b])
    t_exp_done = {0: P["t_psfree"][0], 1: P["t_psfree"][1]}
    t_comb_prev = [t0c]
    t_pv_prev = [P["t_pcfree"][0]]
    t_den_prev = [P["t_pdenfree"][0]]
    rounds = [kcs[i:i + 3] for i in range(0, len(kcs), 3)]
    for ri, rk in enumerate(rounds):
        par = ri % 2
        pS = P["pS"][par]
        Eb = P["Eb"][par]
        q0r = max(0, 32 * (rk[0] - nfull))
        t_mx = []
        t_ss = []
        for slot, kc in enumerate(rk):
            q0 = max(0, 32 * (kc - nfull))
            out_ap = pS[:, slot * 512 + q0:slot * 512 + 512]
            t_s = smm_cb(kc, out_ap, q0, [t_exp_done[par]])
            if kc >= nfull:
                t_s = s.op("vector", lambda q, q0=q0, slot=slot:
                           q.tensor_tensor(
                    pS[:, slot * 512 + q0:slot * 512 + q0 + 32],
                    pS[:, slot * 512 + q0:slot * 512 + q0 + 32],
                    P["dmask"][:], op=ALU.add), deps=[t_s])
            t_ss.append(t_s)
            t_mx.append(s.op("vector", lambda q, out_ap=out_ap, slot=slot:
                             q.reduce_max(mcol[:, slot:slot + 1], out_ap,
                                          axis=AX.X), deps=[t_s]))
        t_c = s.op("vector", lambda q, n=len(rk): q.reduce_max(
            mcol[:, 3:4], mcol[:, 0:n], axis=AX.X), deps=t_mx)
        t_tp = s.op("tensor", lambda q: q.transpose(
            P["pCt"][0:1, 0:128], mcol[:, 3:4], P["id_f32"][:]),
            deps=[t_c, t_pv_prev[0], t_comb_prev[0], P.get("t_idf")])
        t_cf = s.op("vector", lambda q: q.reduce_max(
            dsc[0:1, 2:3], P["pCt"][0:1, 0:128], axis=AX.X), deps=[t_tp])
        t_cb = s.op("vector", lambda q: q.tensor_copy(P["cbf"][:], dsc[0:1, 2:3]),
                    deps=[t_cf])
        t_nb = s.op("tensor", lambda q: q.matmul(
            P["pCb"][:, 0:1], P["negscale_row"][:], P["cbf"][:],
            start=True, stop=True), deps=[t_cb])
        t_nbs = s.op("vector", lambda q: q.tensor_copy(P["nb_sb"][:],
                                                       P["pCb"][:, 0:1]),
                     deps=[t_nb])
        t_bc = _bcast128(s, P, dsc[0:1, 2:3], sc[:, 9:10], [t_cf])
        t_cn = s.op("vector", lambda q: q.tensor_tensor(
            cnew, Crun, sc[:, 9:10], op=ALU.max), deps=[t_cf, t_bc])
        t_d1 = tt_op(s, sc[:, 10:11], Crun, cnew, ALU.subtract, deps=[t_cn])
        t_a1 = s.op("scalar", lambda q: q.activation(
            out=aold, in_=sc[:, 10:11], func=AF.Exp, scale=scale), deps=[t_d1])
        t_d2 = tt_op(s, sc[:, 11:12], sc[:, 9:10], cnew, ALU.subtract, deps=[t_cn])
        t_a2 = s.op("scalar", lambda q: q.activation(
            out=anew, in_=sc[:, 11:12], func=AF.Exp, scale=scale), deps=[t_d2])
        t_cr = s.op("vector", lambda q: q.tensor_copy(Crun, cnew), deps=[t_a1, t_a2])
        t_exps = []
        for slot, kc in enumerate(rk):
            q0 = max(0, 32 * (kc - nfull))
            t_exps.append(s.op("scalar", lambda q, slot=slot, q0=q0: q.activation(
                out=Eb[:, slot * 512 + q0:slot * 512 + 512],
                in_=pS[:, slot * 512 + q0:slot * 512 + 512],
                func=AF.Exp, scale=scale, bias=P["nb_sb"][:]),
                deps=[t_nbs] + t_mx))
        t_exp_done[par] = t_exps[-1]
        t_den = None
        t_pv = None
        for slot, kc in enumerate(rk):
            q0 = max(0, 32 * (kc - nfull))
            t_den = s.op("tensor", lambda q, slot=slot, q0=q0, first=(slot == 0):
                         q.matmul(
                P["pDen"][0:1, q0:512], P["onescol"][:],
                Eb[:, slot * 512 + q0:slot * 512 + 512],
                start=first, stop=(slot == len(rk) - 1)),
                deps=[t_exps[slot], t_den_prev[0]])
            t_pv = s.op("tensor", lambda q, slot=slot, kc=kc, q0=q0,
                        first=(slot == 0): q.matmul(
                P["pC"][:, q0:512], vbuf[:, kc * 128:(kc + 1) * 128],
                Eb[:, slot * 512 + q0:slot * 512 + 512],
                start=first, stop=(slot == len(rk) - 1)),
                deps=[t_exps[slot], v_dep, t_pv_prev[0], t_nbs, t_bc])
        t_m1 = s.op("vector", lambda q: q.tensor_scalar(
            ctx_sb[:], ctx_sb[:], aold, None, op0=ALU.mult),
            deps=[t_cr, t_a1])
        t_m2 = s.op("vector", lambda q, q0r=q0r: q.scalar_tensor_tensor(
            ctx_sb[:, q0r:512], P["pC"][:, q0r:512], anew, ctx_sb[:, q0r:512],
            op0=ALU.mult, op1=ALU.add), deps=[t_m1, t_pv, t_a2])
        t_ad = _sc_to_p0(s, P, aold, aoldd, [t_a1])
        t_an = _sc_to_p0(s, P, anew, anewd, [t_a2])
        t_m3 = s.op("vector", lambda q: q.tensor_scalar(
            den_sb[:], den_sb[:], aoldd, None, op0=ALU.mult), deps=[t_ad, t_m2])
        t_m4 = s.op("vector", lambda q, q0r=q0r: q.scalar_tensor_tensor(
            den_sb[0:1, q0r:512], P["pDen"][0:1, q0r:512], anewd,
            den_sb[0:1, q0r:512], op0=ALU.mult, op1=ALU.add),
            deps=[t_m3, t_den, t_an])
        t_comb_prev = [t_m2]
        t_pv_prev = [t_m2]
        t_den_prev = [t_m4]
    t_l = s.op("scalar", lambda q: q.activation(
        out=dsc[0:1, 3:4].broadcast_to((1, 1)) if False else P["dlog"][0:1, 0:512],
        in_=den_sb[:], func=AF.Ln), deps=[t_den_prev[0], t_comb_prev[0]])
    t_r = s.op("scalar", lambda q: q.activation(
        out=P["dlog"][0:1, 512:1024], in_=P["dlog"][0:1, 0:512],
        func=AF.Exp, scale=-1.0), deps=[t_l])
    t_rb = s.op("vector", lambda q: q.tensor_copy(P["recip_bf"][:],
                                                  P["dlog"][0:1, 512:1024]),
                deps=[t_r])
    t_ex = s.op("tensor", lambda q: q.matmul(
        P["pS"][0][:, 0:512], P["ones_row"][:], P["recip_bf"][:],
        start=True, stop=True), deps=[t_rb, t_exp_done[0]])
    t_fin = s.op("vector", lambda q: q.tensor_tensor(
        ctx_dst, ctx_sb[:], P["pS"][0][:, 0:512], op=ALU.mult),
        deps=[t_ex, t_comb_prev[0]])
    P["t_psfree"][0] = t_fin
    P["t_psfree"][1] = t_exp_done[1]
    P["t_pcfree"][0] = t_comb_prev[0]
    P["t_pdenfree"][0] = t_den_prev[0]
    P["t_ctxfree"][0] = t_fin
    return t_fin


def _bcast128(s, P, src11, dst1281, deps):
    t1 = s.op("vector", lambda q: q.tensor_copy(P["cbf2"][:], src11), deps=deps)
    t2 = s.op("tensor", lambda q: q.matmul(P["pCb"][:, 1:2], P["ones_row"][:],
                                           P["cbf2"][:], start=True, stop=True),
              deps=[t1])
    t3 = s.op("vector", lambda q: q.tensor_copy(dst1281, P["pCb"][:, 1:2]),
              deps=[t2])
    return t3


def _sc_to_p0(s, P, src1281, dst11, deps):
    return s.op("vector", lambda q: q.tensor_copy(dst11, src1281[0:1, :]),
                deps=deps)


def build_attn_proj_l1b():
    nc = bass.Bass("TRN2", target_bir_lowering=False)
    x = nc.dram_tensor("x", [PER, 4096], F32, kind="ExternalInput").ap()
    wt = nc.dram_tensor("wt", [96, 128, 4096], F32, kind="ExternalInput").ap()
    bias_t = nc.dram_tensor("bias_t", [128, 96], F32, kind="ExternalInput").ap()
    idf = nc.dram_tensor("ident", [128, 128], F32, kind="ExternalInput").ap()
    cosf = nc.dram_tensor("cosf", [64, 1024], F32, kind="ExternalInput").ap()
    sinf = nc.dram_tensor("sinf", [64, 1024], F32, kind="ExternalInput").ap()
    outs = {}
    for nm in ("q_sh", "k_sh", "v_sh"):
        outs[nm] = nc.dram_tensor(nm, [128, 32 * 1024], BF16,
                                  kind="ExternalOutput").ap()
    with ExitStack() as es:
        P = {}
        sb, ps = nc.sbuf_tensor, nc.psum_tensor
        alloc_ln_bufs(nc, es, P)
        P["id_bf"] = es.enter_context(sb("id_bf", [128, 128], BF16))
        P["id_f32"] = es.enter_context(sb("id_f32", [128, 128], F32))
        P["pLN"] = [es.enter_context(ps(f"pLN{i}", [128, 1024], BF16))
                    for i in range(2)]
        P["pD"] = [es.enter_context(ps(f"pD{i}", [128, 1024], F32))
                   for i in range(2)]
        P["wa"] = es.enter_context(sb("wa", [128, 4096], BF16))
        P["wb"] = es.enter_context(sb("wb", [128, 4096], BF16))
        P["bias_sb"] = es.enter_context(sb("bias_sb", [128, 96], F32))
        P["hb0"] = es.enter_context(sb("hb0", [128, 1024], F32))
        P["hb1"] = es.enter_context(sb("hb1", [128, 1024], F32))
        P["hc0"] = es.enter_context(sb("hc0", [128, 1024], BF16))
        P["hc1"] = es.enter_context(sb("hc1", [128, 1024], BF16))
        P["rt4"] = es.enter_context(sb("rt4", [64, 4096], F32))
        P["cosf"] = es.enter_context(sb("cosf_sb", [64, 1024], F32))
        P["sinf"] = es.enter_context(sb("sinf_sb", [64, 1024], F32))
        block = es.enter_context(nc.Block())
        s = Sched(nc, es, block)
        t_id = load_ident(s, P, idf)
        t_b = s.op("sync", lambda q: q.dma_start(out=P["bias_sb"][:], in_=bias_t[:]),
                   dma=True)
        t_cos = s.op("sync", lambda q: q.dma_start(out=P["cosf"][:], in_=cosf[:]),
                     dma=True)
        t_sin = s.op("sync", lambda q: q.dma_start(out=P["sinf"][:], in_=sinf[:]),
                     dma=True)
        kdep = emit_ln(s, P, x)
        state = {"dma": [None, None], "rope": [None, None]}

        def out_cb(m, p, toks):
            par = m % 2
            hb = (P["hb0"], P["hb1"])[par]
            hc = (P["hc0"], P["hc1"])[par]
            t_e = s.op("vector", lambda q, p=p, hb=hb, m=m: q.tensor_scalar(
                hb[:], p[:], P["bias_sb"][:, m:m + 1], None, op0=ALU.add),
                deps=toks + [t_b, state["dma"][par]])
            deps = [t_e, t_cos, t_sin]
            if m < 64:
                deps = emit_rope_inplace(s, P, hb, P["cosf"][:], P["sinf"][:],
                                         deps + [state["rope"][1 - par]])
                state["rope"][par] = deps[-1]
            t_c = s.op("vector", lambda q, hc=hc, hb=hb: q.tensor_copy(hc[:], hb[:]),
                       deps=deps)
            dst = (outs["q_sh"] if m < 32 else outs["k_sh"] if m < 64
                   else outs["v_sh"])
            t_d = s.op("sync", lambda q, dst=dst, hc=hc, m=m: q.dma_start(
                out=dst[:, (m % 32) * 1024:(m % 32 + 1) * 1024], in_=hc[:]),
                deps=[t_c], dma=True)
            state["dma"][par] = t_d
            return t_e
        emit_proj(s, P, wt, 96, 32, P["lnt"], kdep, out_cb,
                  (P["wa"], P["wb"]), first_deps=[t_id[1]])
        s.finish([state["dma"][0], state["dma"][1]])
    return nc


def alloc_flash_psum(nc, es2, P):
    ps = nc.psum_tensor
    P["pS"] = [es2.enter_context(ps(f"pS{i}", [128, 1536], F32)) for i in range(2)]
    P["pC"] = es2.enter_context(ps("pC", [128, 512], F32))
    P["pDen"] = es2.enter_context(ps("pDen", [1, 512], F32))
    P["pCt"] = P["pC"][0:1, 0:128]
    P["pCb"] = P["pC"][:, 256:264]


def alloc_flash_bufs(nc, es2, P):
    sb = nc.sbuf_tensor
    P["Eb"] = [es2.enter_context(sb(f"Eb{i}", [128, 1536], BF16)) for i in range(2)]
    P["ctx_sb"] = es2.enter_context(sb("ctx_sb", [128, 512], F32))
    P["den_sb"] = es2.enter_context(sb("den_sb", [1, 512], F32))
    P["fsc"] = es2.enter_context(sb("fsc", [128, 32], F32))
    P["fsd"] = es2.enter_context(sb("fsd", [1, 16], F32))
    P["cbf"] = es2.enter_context(sb("cbf", [1, 1], BF16))
    P["cbf2"] = es2.enter_context(sb("cbf2", [1, 1], BF16))
    P["nb_sb"] = es2.enter_context(sb("nb_sb", [128, 1], F32))
    P["dlog"] = es2.enter_context(sb("dlog", [1, 1024], F32))
    P["recip_bf"] = es2.enter_context(sb("recip_bf", [1, 512], BF16))
    P["ones_row"] = es2.enter_context(sb("ones_row", [1, 128], BF16))
    P["negscale_row"] = es2.enter_context(sb("negscale_row", [1, 128], BF16))
    P["onescol"] = es2.enter_context(sb("onescol", [128, 1], BF16))
    P["dmask"] = es2.enter_context(sb("dmask", [128, 32], F32))
    P["t_psfree"] = {0: None, 1: None}
    P["t_pcfree"] = [None]
    P["t_pdenfree"] = [None]
    P["t_ctxfree"] = [None]


def build_attn_flash_mha(n_heads=32, group=1, scale=None,
                         with_bias=True, name="l1b", dbg=False):
    if scale is None:
        scale = 1.0 / float(np.sqrt(128.0))
    n_kv = n_heads // group
    KMAXB = 32
    nc = bass.Bass("TRN2", target_bir_lowering=False)
    x = nc.dram_tensor("x", [PER, 4096], F32, kind="ExternalInput").ap()
    q_sh = nc.dram_tensor("q_sh", [128, 32 * 1024], BF16, kind="ExternalInput").ap()
    kall = nc.dram_tensor("kall", [n_kv, 128, 4096], BF16, kind="ExternalInput").ap()
    vall = nc.dram_tensor("vall", [n_kv, 128, 4096], BF16, kind="ExternalInput").ap()
    wo = nc.dram_tensor("wo", [32, 128, 4096], F32, kind="ExternalInput").ap()
    bias_t = nc.dram_tensor("bias_t", [128, 32], F32, kind="ExternalInput").ap()
    tri = nc.dram_tensor("tri", [128, 32], F32, kind="ExternalInput").ap()
    idf = nc.dram_tensor("ident", [128, 128], F32, kind="ExternalInput").ap()
    y = nc.dram_tensor("y", [PER, 4096], F32, kind="ExternalOutput").ap()
    if dbg:
        ctx_d = nc.dram_tensor("ctx_d", [128, 32 * 1024], BF16,
                               kind="ExternalOutput").ap()
    with ExitStack() as es:
        P = {}
        sb, ps = nc.sbuf_tensor, nc.psum_tensor
        P["id_f32"] = es.enter_context(sb("id_f32", [128, 128], F32))
        P["ctx_all"] = es.enter_context(sb("ctx_all", [128, 32 * 1024], BF16))
        P["kbuf"] = [es.enter_context(sb(f"kbuf{i}", [128, KMAXB * 128], BF16))
                     for i in range(2)]
        P["vbuf"] = [es.enter_context(sb(f"vbuf{i}", [128, KMAXB * 128], BF16))
                     for i in range(2)]
        P["qbuf"] = [es.enter_context(sb(f"qbuf{i}", [128, 1024], BF16))
                     for i in range(2)]
        alloc_flash_bufs(nc, es, P)
        P["wa"] = es.enter_context(sb("wa", [128, 4096], BF16))
        P["wb"] = es.enter_context(sb("wb", [128, 4096], BF16))
        alloc_resid_bufs(nc, es, P)
        if with_bias:
            P["bias_sb"] = es.enter_context(sb("bias_sb", [128, 32], F32))
        block = es.enter_context(nc.Block())
        s = Sched(nc, es, block)
        t_id = s.op("sync", lambda q: q.dma_start(out=P["id_f32"][:], in_=idf[:]),
                    dma=True)
        P["t_idf"] = t_id
        ctx_done = [[None] * 2 for _ in range(n_heads)]
        with ExitStack() as es2:
            alloc_flash_psum(nc, es2, P)
            t_tri = s.op("sync", lambda q: q.dma_start(out=P["dmask"][:],
                                                       in_=tri[:]), dma=True)
            t_c1 = s.op("vector", lambda q: q.memset(P["ones_row"][:], 1.0))
            t_c2 = s.op("vector", lambda q: q.memset(P["negscale_row"][:], -scale))
            t_c3 = s.op("vector", lambda q: q.memset(P["onescol"][:], 1.0))
            P["t_ctxfree"] = [t_c1]
            P["t_psfree"] = {0: t_c2, 1: t_c2}
            P["t_pcfree"] = [t_c3]
            P["t_pdenfree"] = [t_c3]
            kv_loaded = {}
            t_kv_prev = [None, None]
            t_q_prev = [None, None]
            for hh in range(n_heads):
                g = hh // group
                gpar = g % 2
                if g not in kv_loaded:
                    t_k = s.op("sync", lambda q, g=g, gpar=gpar: q.dma_start(
                        out=P["kbuf"][gpar][:], in_=kall[g, :, 0:KMAXB * 128]),
                        deps=[t_kv_prev[gpar]], dma=True)
                    t_v = s.op("gpsimd", lambda q, g=g, gpar=gpar: q.dma_start(
                        out=P["vbuf"][gpar][:], in_=vall[g, :, 0:KMAXB * 128]),
                        deps=[t_kv_prev[gpar]], dma=True)
                    kv_loaded[g] = (t_k, t_v)
                t_k, t_v = kv_loaded[g]
                hpar = hh % 2
                t_q = s.op("sync", lambda q, hh=hh, hpar=hpar: q.dma_start(
                    out=P["qbuf"][hpar][:],
                    in_=q_sh[:, hh * 1024:(hh + 1) * 1024]),
                    deps=[t_q_prev[hpar]], dma=True)
                kbuf, vbuf, qbuf = P["kbuf"][gpar], P["vbuf"][gpar], P["qbuf"][hpar]
                for st in range(2):

                    def smm(kc, out_ap, q0, deps, qbuf=qbuf, kbuf=kbuf,
                            st=st, t_k=t_k, t_q=t_q, t_tri=t_tri):
                        return s.op("tensor", lambda q: q.matmul(
                            out_ap, kbuf[:, kc * 128:(kc + 1) * 128],
                            qbuf[:, st * 512 + q0:st * 512 + 512],
                            start=True, stop=True),
                            deps=deps + [t_k, t_q, t_tri])
                    ctx_dst = P["ctx_all"][:, hh * 1024 + st * 512:
                                           hh * 1024 + (st + 1) * 512]
                    t_fin = emit_flash_half(s, P, None, hh, st, smm,
                                            vbuf, t_v, ctx_dst, scale)
                    ctx_done[hh][st] = t_fin
                t_kv_prev[gpar] = ctx_done[hh][1]
                t_q_prev[hpar] = ctx_done[hh][1]
        P["pD"] = [es.enter_context(ps(f"pD{i}", [128, 1024], F32))
                   for i in range(2)]
        P["pR"] = es.enter_context(ps("pR", [128, 1024], F32))
        bias_sb = None
        if with_bias:
            t_b = s.op("sync", lambda q: q.dma_start(out=P["bias_sb"][:],
                                                     in_=bias_t[:]), dma=True)
            bias_sb = P["bias_sb"]
        src_deps = {k: [ctx_done[hh][hf] for hh in range(n_heads)
                        for hf in range(2)] if k == 0 else
                    [ctx_done[k][0], ctx_done[k][1]] for k in range(32)}
        t_ctxd = None
        if dbg:
            t_ctxd = s.op("sync", lambda q: q.dma_start(
                out=ctx_d[:], in_=P["ctx_all"][:]),
                deps=[ctx_done[hh][hf] for hh in range(n_heads)
                      for hf in range(2)], dma=True)
        t_last = emit_out_proj_residual(s, P, wo, bias_sb, P["ctx_all"],
                                        src_deps, x, y, (P["wa"], P["wb"]))
        s.finish([t_last, t_ctxd])
    return nc


def as_bf16(a):
    a = np.asarray(a)
    if a.dtype == np.uint16:
        return a.view(ml_dtypes.bfloat16)
    return a


def rope_tables_core(j):
    pos = core_positions(j).astype(np.float64)
    f = np.arange(64, dtype=np.float64)
    inv = 1.0 / (10000.0 ** (f / 64.0))
    ang = inv[:, None] * pos[None, :]
    return (np.ascontiguousarray(np.cos(ang).astype(np.float32)),
            np.ascontiguousarray(np.sin(ang).astype(np.float32)))


def diag_mask_core(j):
    kr = np.arange(128)[:, None]
    qc = np.arange(32)[None, :]
    return np.ascontiguousarray(
        np.where(kr > 32 * j + qc, -1e9, 0.0).astype(np.float32))


def reshard_feat_heads(shards, batch, n_heads=32, hd=1024):
    out = np.empty((n_heads, 128, 4096), dtype=shards[0].dtype)
    for j in range(4):
        kf = np.asarray(shards[batch * 4 + j]).reshape(128, n_heads, 1024)
        out[:, :, _POS_IDX[j]] = kf.transpose(1, 0, 2)
    return np.ascontiguousarray(out)


def reshard_v_tokmajor(shards, batch, n_heads=32):
    vfull = np.empty((n_heads, 4096, 128), dtype=shards[0].dtype)
    for j in range(4):
        vf = np.asarray(shards[batch * 4 + j]).reshape(128, n_heads, 1024)
        vfull[:, _POS_IDX[j], :] = vf.transpose(1, 2, 0)
    return np.ascontiguousarray(
        vfull.reshape(n_heads, 32, 128, 128).transpose(0, 2, 1, 3)
        .reshape(n_heads, 128, 4096))


def run_attn_l1b(x_shards, inp):
    qkvw = np.asarray(inp["L1_attn_b_qkv_w"], np.float32)
    ident = np.eye(128, dtype=np.float32)
    wt = tile_w(np.ascontiguousarray(qkvw.T))
    bias96 = bias_cols(np.asarray(inp["L1_attn_b_qkv_b"], np.float32))
    nc = build_attn_proj_l1b()
    in_maps = []
    for c in range(8):
        j = c % 4
        cosf, sinf = rope_tables_core(j)
        in_maps.append({"x": x_shards[c], "wt": wt, "bias_t": bias96,
                        "ident": ident, "cosf": cosf, "sinf": sinf})
    r = run_bass_kernel_spmd(nc, in_maps, core_ids=list(range(8)))
    q_sh = [as_bf16(r.results[c]["q_sh"]) for c in range(8)]
    k_sh = [as_bf16(r.results[c]["k_sh"]) for c in range(8)]
    v_sh = [as_bf16(r.results[c]["v_sh"]) for c in range(8)]
    kall = [reshard_feat_heads(k_sh, b) for b in range(2)]
    vall = [reshard_v_tokmajor(v_sh, b) for b in range(2)]
    wo = tile_w(np.ascontiguousarray(np.asarray(inp["L1_attn_b_out_w"],
                                                np.float32).T))
    bias_o = bias_cols(np.asarray(inp["L1_attn_b_out_b"], np.float32))
    nc2 = build_attn_flash_mha(n_heads=32, group=1, with_bias=True)
    in_maps2 = []
    for c in range(8):
        b, j = divmod(c, 4)
        in_maps2.append({"x": x_shards[c], "q_sh": q_sh[c], "kall": kall[b],
                         "vall": vall[b], "wo": wo, "bias_t": bias_o,
                         "tri": diag_mask_core(j), "ident": ident})
    r2 = run_bass_kernel_spmd(nc2, in_maps2, core_ids=list(range(8)))
    return [r2.results[c]["y"] for c in range(8)]


def build_attn_proj_l0b():
    nc = bass.Bass("TRN2", target_bir_lowering=False)
    x = nc.dram_tensor("x", [PER, 4096], F32, kind="ExternalInput").ap()
    wq = nc.dram_tensor("wq", [32, 128, 4096], F32, kind="ExternalInput").ap()
    wkr = nc.dram_tensor("wkr", [4, 128, 4096], F32, kind="ExternalInput").ap()
    wdk = nc.dram_tensor("wdk", [4, 128, 4096], F32, kind="ExternalInput").ap()
    wupk = nc.dram_tensor("wupk", [4, 128, 512], F32, kind="ExternalInput").ap()
    wupv = nc.dram_tensor("wupv", [8, 128, 512], F32, kind="ExternalInput").ap()
    idf = nc.dram_tensor("ident", [128, 128], F32, kind="ExternalInput").ap()
    cosf = nc.dram_tensor("cosf", [64, 1024], F32, kind="ExternalInput").ap()
    sinf = nc.dram_tensor("sinf", [64, 1024], F32, kind="ExternalInput").ap()
    outs = {}
    for nm, w in (("q_sh", 32), ("kr_sh", 4), ("kc_sh", 4), ("v_sh", 8)):
        outs[nm] = nc.dram_tensor(nm, [128, w * 1024], BF16,
                                  kind="ExternalOutput").ap()
    with ExitStack() as es:
        P = {}
        sb, ps = nc.sbuf_tensor, nc.psum_tensor
        alloc_ln_bufs(nc, es, P)
        P["id_bf"] = es.enter_context(sb("id_bf", [128, 128], BF16))
        P["id_f32"] = es.enter_context(sb("id_f32", [128, 128], F32))
        P["pLN"] = [es.enter_context(ps(f"pLN{i}", [128, 1024], BF16))
                    for i in range(2)]
        P["pD"] = [es.enter_context(ps(f"pD{i}", [128, 1024], F32))
                   for i in range(2)]
        P["wa"] = es.enter_context(sb("wa", [128, 4096], BF16))
        P["wb"] = es.enter_context(sb("wb", [128, 4096], BF16))
        P["hb0"] = es.enter_context(sb("hb0", [128, 1024], F32))
        P["hb1"] = es.enter_context(sb("hb1", [128, 1024], F32))
        P["hc0"] = es.enter_context(sb("hc0", [128, 1024], BF16))
        P["hc1"] = es.enter_context(sb("hc1", [128, 1024], BF16))
        P["rt4"] = es.enter_context(sb("rt4", [64, 4096], F32))
        P["cosf"] = es.enter_context(sb("cosf_sb", [64, 1024], F32))
        P["sinf"] = es.enter_context(sb("sinf_sb", [64, 1024], F32))
        P["ckv_sb"] = es.enter_context(sb("ckv_sb", [128, 4 * 1024], BF16))
        block = es.enter_context(nc.Block())
        s = Sched(nc, es, block)
        t_id = load_ident(s, P, idf)
        t_cos = s.op("sync", lambda q: q.dma_start(out=P["cosf"][:], in_=cosf[:]),
                     dma=True)
        t_sin = s.op("sync", lambda q: q.dma_start(out=P["sinf"][:], in_=sinf[:]),
                     dma=True)
        kdep = emit_ln(s, P, x)
        state = {"dma": [None, None], "rope": [None, None], "ckv": {}}

        def mk_out_cb(dst, kind):
            def out_cb(m, p, toks):
                par = m % 2
                hb = (P["hb0"], P["hb1"])[par]
                hc = (P["hc0"], P["hc1"])[par]
                t_e = s.op("vector", lambda q, p=p, hb=hb: q.tensor_copy(hb[:], p[:]),
                           deps=toks + [state["dma"][par]])
                deps = [t_e, t_cos, t_sin]
                if kind == "q":
                    deps = emit_rope_inplace(s, P, hb, P["cosf"][:], P["sinf"][:],
                                             deps + [state["rope"][1 - par]])
                    state["rope"][par] = deps[-1]
                elif kind == "kr":
                    deps = emit_rope_inplace(s, P, hb, P["cosf"][:], P["sinf"][:],
                                             deps + [state["rope"][1 - par]],
                                             half_rows=32, base=0)
                    deps = emit_rope_inplace(s, P, hb, P["cosf"][:], P["sinf"][:],
                                             deps, half_rows=32, base=64)
                    state["rope"][par] = deps[-1]
                t_c = s.op("vector", lambda q, hc=hc, hb=hb: q.tensor_copy(hc[:], hb[:]),
                           deps=deps)
                if kind == "ckv":
                    t_c2 = s.op("vector", lambda q, m=m, hc=hc: q.tensor_copy(
                        P["ckv_sb"][:, m * 1024:(m + 1) * 1024], hc[:]), deps=[t_c])
                    state["ckv"][m] = t_c2
                    state["dma"][par] = t_c2
                else:
                    t_d = s.op("sync", lambda q, dst=dst, hc=hc, m=m: q.dma_start(
                        out=dst[:, m * 1024:(m + 1) * 1024], in_=hc[:]),
                        deps=[t_c], dma=True)
                    state["dma"][par] = t_d
                return t_e
            return out_cb

        emit_proj(s, P, wq, 32, 32, P["lnt"], kdep, mk_out_cb(outs["q_sh"], "q"),
                  (P["wa"], P["wb"]), first_deps=[t_id[1]])
        emit_proj(s, P, wkr, 4, 32, P["lnt"], kdep, mk_out_cb(outs["kr_sh"], "kr"),
                  (P["wa"], P["wb"]))
        emit_proj(s, P, wdk, 4, 32, P["lnt"], kdep, mk_out_cb(outs["kc_sh"], "ckv"),
                  (P["wa"], P["wb"]))
        ckvdep = {k: [state["ckv"][k]] for k in range(4)}
        emit_proj(s, P, wupk, 4, 4, P["ckv_sb"], ckvdep,
                  mk_out_cb(outs["kc_sh"], "plain"), (P["wa"], P["wb"]),
                  rhs_stride=1024)
        emit_proj(s, P, wupv, 8, 4, P["ckv_sb"], ckvdep,
                  mk_out_cb(outs["v_sh"], "plain"), (P["wa"], P["wb"]),
                  rhs_stride=1024)
        s.finish([state["dma"][0], state["dma"][1]])
    return nc


def run_attn_l0b(x_shards, inp):
    ident = np.eye(128, dtype=np.float32)
    args = {
        "wq": tile_w(np.asarray(inp["L0_attn_b_Wq"], np.float32)),
        "wkr": tile_w(np.asarray(inp["L0_attn_b_Wkr"], np.float32)),
        "wdk": tile_w(np.asarray(inp["L0_attn_b_Wdk"], np.float32)),
        "wupk": tile_w(np.asarray(inp["L0_attn_b_Wupk"], np.float32)),
        "wupv": tile_w(np.asarray(inp["L0_attn_b_Wupv"], np.float32)),
        "ident": ident,
    }
    nc = build_attn_proj_l0b()
    in_maps = []
    for c in range(8):
        j = c % 4
        cosf, sinf = rope_tables_core(j)
        m = dict(args)
        m.update({"x": x_shards[c], "cosf": cosf, "sinf": sinf})
        in_maps.append(m)
    r = run_bass_kernel_spmd(nc, in_maps, core_ids=list(range(8)))
    q_sh = [as_bf16(r.results[c]["q_sh"]) for c in range(8)]
    kr_sh = [as_bf16(r.results[c]["kr_sh"]) for c in range(8)]
    kc_sh = [as_bf16(r.results[c]["kc_sh"]) for c in range(8)]
    v_sh = [as_bf16(r.results[c]["v_sh"]) for c in range(8)]
    krall = [reshard_feat_heads(kr_sh, b, n_heads=4) for b in range(2)]
    kcall = [reshard_feat_heads(kc_sh, b, n_heads=4) for b in range(2)]
    kall = []
    for b in range(2):
        kf = np.empty((8, 128, 4096), dtype=krall[0].dtype)
        krb = krall[b].reshape(4 * 128, 4096)
        kcb = kcall[b].reshape(4 * 128, 4096)
        for g in range(8):
            kf[g, 0:32] = krb[64 * g:64 * g + 32]
            kf[g, 32:64] = kcb[64 * g:64 * g + 32]
            kf[g, 64:96] = krb[64 * g + 32:64 * g + 64]
            kf[g, 96:128] = kcb[64 * g + 32:64 * g + 64]
        kall.append(np.ascontiguousarray(kf))
    vall = [reshard_v_tokmajor(v_sh, b, n_heads=8) for b in range(2)]
    wo = tile_w(np.asarray(inp["L0_attn_b_Wo"], np.float32))
    zb = np.zeros((128, 32), np.float32)
    nc2 = build_attn_flash_mha(n_heads=32, group=4, with_bias=False)
    in_maps2 = []
    for c in range(8):
        b, j = divmod(c, 4)
        in_maps2.append({"x": x_shards[c], "q_sh": q_sh[c], "kall": kall[b],
                         "vall": vall[b], "wo": wo, "bias_t": zb,
                         "tri": diag_mask_core(j), "ident": ident})
    r2 = run_bass_kernel_spmd(nc2, in_maps2, core_ids=list(range(8)))
    return [r2.results[c]["y"] for c in range(8)]


def permute_wqb(wq_b):
    out = np.empty_like(wq_b)
    for h in range(32):
        out[128 * h:128 * (h + 1)] = wq_b[192 * h:192 * h + 128]
        for j in range(32):
            out[4096 + 64 * h + j] = wq_b[192 * h + 128 + 2 * j]
            out[4096 + 64 * h + 32 + j] = wq_b[192 * h + 128 + 2 * j + 1]
    return out


def permute_wkva(wkv_a):
    out = np.zeros((640, 4096), wkv_a.dtype)
    out[0:512] = wkv_a[0:512]
    for j in range(32):
        out[512 + j] = wkv_a[512 + 2 * j]
        out[544 + j] = wkv_a[512 + 2 * j + 1]
    return out


def rope_tables_pe_core(j):
    pos = core_positions(j).astype(np.float64)
    f = np.arange(32, dtype=np.float64)
    inv = 1.0 / (10000.0 ** (2.0 * f / 64.0))
    ang = inv[:, None] * pos[None, :]
    return (np.ascontiguousarray(np.cos(ang).astype(np.float32)),
            np.ascontiguousarray(np.sin(ang).astype(np.float32)))


def build_attn_proj_l1a():
    nc = bass.Bass("TRN2", target_bir_lowering=False)
    x = nc.dram_tensor("x", [PER, 4096], F32, kind="ExternalInput").ap()
    wqa = nc.dram_tensor("wqa", [12, 128, 4096], F32, kind="ExternalInput").ap()
    wqb = nc.dram_tensor("wqb", [48, 128, 1536], F32, kind="ExternalInput").ap()
    wkva = nc.dram_tensor("wkva", [5, 128, 4096], F32, kind="ExternalInput").ap()
    gq = nc.dram_tensor("gq", [128, 12], F32, kind="ExternalInput").ap()
    gk = nc.dram_tensor("gk", [128, 4], F32, kind="ExternalInput").ap()
    idf = nc.dram_tensor("ident", [128, 128], F32, kind="ExternalInput").ap()
    cpe = nc.dram_tensor("cpe", [32, 1024], F32, kind="ExternalInput").ap()
    spe = nc.dram_tensor("spe", [32, 1024], F32, kind="ExternalInput").ap()
    q2_sh = nc.dram_tensor("q2_sh", [128, 48 * 1024], BF16, kind="ExternalOutput").ap()
    kvn_sh = nc.dram_tensor("kvn_sh", [128, 4 * 1024], BF16, kind="ExternalOutput").ap()
    kpe_sh = nc.dram_tensor("kpe_sh", [128, 1024], BF16, kind="ExternalOutput").ap()
    with ExitStack() as es:
        P = {}
        sb, ps = nc.sbuf_tensor, nc.psum_tensor
        alloc_ln_bufs(nc, es, P)
        P["id_bf"] = es.enter_context(sb("id_bf", [128, 128], BF16))
        P["id_f32"] = es.enter_context(sb("id_f32", [128, 128], F32))
        P["pLN"] = [es.enter_context(ps(f"pLN{i}", [128, 1024], BF16))
                    for i in range(2)]
        P["pD"] = [es.enter_context(ps(f"pD{i}", [128, 1024], F32))
                   for i in range(2)]
        P["pR"] = es.enter_context(ps("pR", [128, 1024], F32))
        P["wa"] = es.enter_context(sb("wa", [128, 4096], BF16))
        P["wb"] = es.enter_context(sb("wb", [128, 4096], BF16))
        P["hb0"] = es.enter_context(sb("hb0", [128, 1024], F32))
        P["hb1"] = es.enter_context(sb("hb1", [128, 1024], F32))
        P["hc0"] = es.enter_context(sb("hc0", [128, 1024], BF16))
        P["hc1"] = es.enter_context(sb("hc1", [128, 1024], BF16))
        P["rt4"] = es.enter_context(sb("rt4", [64, 4096], F32))
        P["cpe"] = es.enter_context(sb("cpe_sb", [32, 1024], F32))
        P["spe"] = es.enter_context(sb("spe_sb", [32, 1024], F32))
        P["cq_sb"] = es.enter_context(sb("cq_sb", [128, 12 * 1024], BF16))
        P["qn_sb"] = P["cq_sb"]
        P["ckv_sb"] = es.enter_context(sb("ckv_sb", [128, 4 * 1024], BF16))
        P["sqt"] = es.enter_context(sb("sqt", [128, 1024], BF16))
        P["gq_sb"] = es.enter_context(sb("gq_sb", [128, 12], F32))
        P["gk_sb"] = es.enter_context(sb("gk_sb", [128, 4], F32))
        P["onescol"] = es.enter_context(sb("onescol", [128, 1], BF16))
        P["ones_row"] = es.enter_context(sb("ones_row", [1, 128], BF16))
        P["rs_sb"] = es.enter_context(sb("rs_sb", [1, 2048], F32))
        P["rs_bf"] = es.enter_context(sb("rs_bf", [1, 1024], BF16))
        P["eps2"] = es.enter_context(sb("eps2", [1, 1], F32))
        block = es.enter_context(nc.Block())
        s = Sched(nc, es, block)
        t_id = load_ident(s, P, idf)
        t_cos = s.op("sync", lambda q: q.dma_start(out=P["cpe"][:], in_=cpe[:]),
                     dma=True)
        t_sin = s.op("sync", lambda q: q.dma_start(out=P["spe"][:], in_=spe[:]),
                     dma=True)
        t_gq = s.op("sync", lambda q: q.dma_start(out=P["gq_sb"][:], in_=gq[:]),
                    dma=True)
        t_gk = s.op("sync", lambda q: q.dma_start(out=P["gk_sb"][:], in_=gk[:]),
                    dma=True)
        t_oc = s.op("vector", lambda q: q.memset(P["onescol"][:], 1.0))
        t_or = s.op("vector", lambda q: q.memset(P["ones_row"][:], 1.0))
        t_ep = s.op("vector", lambda q: q.memset(P["eps2"][:], 1e-6))
        kdep = emit_ln(s, P, x)
        state = {"dma": [None, None], "rope": [None, None]}

        sq_toks = []

        def cq_cb(m, p, toks):
            t_sq = s.op("scalar", lambda q, p=p: q.activation(
                out=P["sqt"][:], in_=p[:], func=AF.Square),
                deps=toks + (sq_toks[-1:] if sq_toks else []))
            t_e = s.op("vector", lambda q, m=m, p=p: q.tensor_copy(
                P["cq_sb"][:, m * 1024:(m + 1) * 1024], p[:]),
                deps=toks + [state["dma"][m % 2], t_sq])
            for hf in range(2):
                t_mm = s.op("tensor", lambda q, m=m, hf=hf: q.matmul(
                    P["pR"][0:1, hf * 512:(hf + 1) * 512], P["onescol"][:],
                    P["sqt"][:, hf * 512:(hf + 1) * 512],
                    start=(m == 0), stop=(m == 11)),
                    deps=[t_sq, t_oc])
            sq_toks.append(t_mm)
            state["dma"][m % 2] = t_sq
            return t_e
        emit_proj(s, P, wqa, 12, 32, P["lnt"], kdep, cq_cb,
                  (P["wa"], P["wb"]), first_deps=[t_id[1]])
        t_l = s.op("scalar", lambda q: q.activation(
            out=P["rs_sb"][0:1, 0:1024], in_=P["pR"][0:1, 0:1024], func=AF.Ln,
            scale=1.0 / 1536.0, bias=P["eps2"][:]), deps=[sq_toks[-1], t_ep])
        t_r = s.op("scalar", lambda q: q.activation(
            out=P["rs_sb"][0:1, 1024:2048], in_=P["rs_sb"][0:1, 0:1024],
            func=AF.Exp, scale=-0.5), deps=[t_l])
        t_rb = s.op("vector", lambda q: q.tensor_copy(P["rs_bf"][:],
                                                      P["rs_sb"][0:1, 1024:2048]),
                    deps=[t_r])
        bct = []
        for hf in range(2):
            bct.append(s.op("tensor", lambda q, hf=hf: q.matmul(
                P["pD"][0][:, hf * 512:(hf + 1) * 512], P["ones_row"][:],
                P["rs_bf"][0:1, hf * 512:(hf + 1) * 512], start=True, stop=True),
                deps=[t_rb, t_or, P["proj_state"]["out"][0], P["proj_state"]["out"][1]]))
        qn_toks = {}
        for k in range(12):
            qn_toks[k] = [s.op("vector", lambda q, k=k: q.scalar_tensor_tensor(
                P["qn_sb"][:, k * 1024:(k + 1) * 1024],
                P["cq_sb"][:, k * 1024:(k + 1) * 1024],
                P["gq_sb"][:, k:k + 1], P["pD"][0][:, 0:1024],
                op0=ALU.mult, op1=ALU.mult), deps=bct + [t_gq])]
        allqn = [qn_toks[k][0] for k in range(12)]

        def q2_cb(m, p, toks):
            par = m % 2
            hb = (P["hb0"], P["hb1"])[par]
            hc = (P["hc0"], P["hc1"])[par]
            t_e = s.op("vector", lambda q, p=p, hb=hb: q.tensor_copy(hb[:], p[:]),
                       deps=toks + [state["dma"][par]])
            deps = [t_e]
            if m >= 32:
                deps = emit_rope_inplace(s, P, hb, P["cpe"][:], P["spe"][:],
                                         deps + [state["rope"][1 - par], t_cos,
                                                 t_sin],
                                         half_rows=32, base=0)
                deps = emit_rope_inplace(s, P, hb, P["cpe"][:], P["spe"][:],
                                         deps, half_rows=32, base=64)
                state["rope"][par] = deps[-1]
            t_c = s.op("vector", lambda q, hc=hc, hb=hb: q.tensor_copy(hc[:], hb[:]),
                       deps=deps)
            t_d = s.op("sync", lambda q, hc=hc, m=m: q.dma_start(
                out=q2_sh[:, m * 1024:(m + 1) * 1024], in_=hc[:]),
                deps=[t_c], dma=True)
            state["dma"][par] = t_d
            return t_e
        emit_proj(s, P, wqb, 48, 12, P["qn_sb"], qn_toks, q2_cb,
                  (P["wa"], P["wb"]), rhs_stride=1024, first_deps=allqn)

        sq2_toks = []
        ckv_toks = {}

        def kv_cb(m, p, toks):
            par = m % 2
            if m < 4:
                t_sq = s.op("scalar", lambda q, p=p: q.activation(
                    out=P["sqt"][:], in_=p[:], func=AF.Square),
                    deps=toks + (sq2_toks[-1:] if sq2_toks else [t_l]))
                t_e = s.op("vector", lambda q, m=m, p=p: q.tensor_copy(
                    P["ckv_sb"][:, m * 1024:(m + 1) * 1024], p[:]),
                    deps=toks + [state["dma"][par], t_sq])
                for hf in range(2):
                    t_mm = s.op("tensor", lambda q, m=m, hf=hf: q.matmul(
                        P["pR"][0:1, hf * 512:(hf + 1) * 512], P["onescol"][:],
                        P["sqt"][:, hf * 512:(hf + 1) * 512],
                        start=(m == 0), stop=(m == 3)),
                        deps=[t_sq, t_l])
                sq2_toks.append(t_mm)
                ckv_toks[m] = t_e
                state["dma"][par] = t_sq
                return t_e
            hb = (P["hb0"], P["hb1"])[par]
            hc = (P["hc0"], P["hc1"])[par]
            t_e = s.op("vector", lambda q, p=p, hb=hb: q.tensor_copy(hb[:], p[:]),
                       deps=toks + [state["dma"][par]])
            deps = emit_rope_inplace(s, P, hb, P["cpe"][:], P["spe"][:],
                                     [t_e, state["rope"][1 - par], t_cos, t_sin],
                                     half_rows=32, base=0)
            state["rope"][par] = deps[-1]
            t_c = s.op("vector", lambda q, hc=hc, hb=hb: q.tensor_copy(hc[:], hb[:]),
                       deps=deps)
            t_d = s.op("sync", lambda q, hc=hc: q.dma_start(
                out=kpe_sh[:], in_=hc[:]), deps=[t_c], dma=True)
            state["dma"][par] = t_d
            return t_e
        emit_proj(s, P, wkva, 5, 32, P["lnt"], kdep, kv_cb, (P["wa"], P["wb"]))
        t_l2 = s.op("scalar", lambda q: q.activation(
            out=P["rs_sb"][0:1, 0:1024], in_=P["pR"][0:1, 0:1024], func=AF.Ln,
            scale=1.0 / 512.0, bias=P["eps2"][:]), deps=[sq2_toks[-1], t_r])
        t_r2 = s.op("scalar", lambda q: q.activation(
            out=P["rs_sb"][0:1, 1024:2048], in_=P["rs_sb"][0:1, 0:1024],
            func=AF.Exp, scale=-0.5), deps=[t_l2])
        t_rb2 = s.op("vector", lambda q: q.tensor_copy(P["rs_bf"][:],
                                                       P["rs_sb"][0:1, 1024:2048]),
                     deps=[t_r2, allqn[-1]])
        bct2 = []
        for hf in range(2):
            bct2.append(s.op("tensor", lambda q, hf=hf: q.matmul(
                P["pD"][0][:, hf * 512:(hf + 1) * 512], P["ones_row"][:],
                P["rs_bf"][0:1, hf * 512:(hf + 1) * 512], start=True, stop=True),
                deps=[t_rb2, P["proj_state"]["out"][0], P["proj_state"]["out"][1]]
                     + allqn))
        t_kvns = []
        for k in range(4):
            t_kvn = s.op("vector", lambda q, k=k: q.scalar_tensor_tensor(
                P["qn_sb"][:, k * 1024:(k + 1) * 1024],
                P["ckv_sb"][:, k * 1024:(k + 1) * 1024],
                P["gk_sb"][:, k:k + 1], P["pD"][0][:, 0:1024],
                op0=ALU.mult, op1=ALU.mult), deps=bct2 + [t_gk, ckv_toks[k]])
            t_kvns.append(s.op("sync", lambda q, k=k: q.dma_start(
                out=kvn_sh[:, k * 1024:(k + 1) * 1024],
                in_=P["qn_sb"][:, k * 1024:(k + 1) * 1024]), deps=[t_kvn],
                dma=True))
        s.finish([state["dma"][0], state["dma"][1], t_kvns[-1]])
    return nc


def build_attn_flash_l1a():
    scale = 1.0 / float(np.sqrt(192.0))
    nc = bass.Bass("TRN2", target_bir_lowering=False)
    x = nc.dram_tensor("x", [PER, 4096], F32, kind="ExternalInput").ap()
    q2_sh = nc.dram_tensor("q2_sh", [128, 48 * 1024], BF16, kind="ExternalInput").ap()
    kvnall = nc.dram_tensor("kvnall", [4, 128, 4096], BF16, kind="ExternalInput").ap()
    kpeall = nc.dram_tensor("kpeall", [1, 128, 4096], BF16, kind="ExternalInput").ap()
    wkvb = nc.dram_tensor("wkvb", [64, 128, 512], F32, kind="ExternalInput").ap()
    wo = nc.dram_tensor("wo", [32, 128, 4096], F32, kind="ExternalInput").ap()
    tri = nc.dram_tensor("tri", [128, 32], F32, kind="ExternalInput").ap()
    idf = nc.dram_tensor("ident", [128, 128], F32, kind="ExternalInput").ap()
    y = nc.dram_tensor("y", [PER, 4096], F32, kind="ExternalOutput").ap()
    with ExitStack() as es:
        P = {}
        sb, ps = nc.sbuf_tensor, nc.psum_tensor
        P["id_f32"] = es.enter_context(sb("id_f32", [128, 128], F32))
        P["ctx_all"] = es.enter_context(sb("ctx_all", [128, 32 * 1024], BF16))
        P["kvn_sb"] = es.enter_context(sb("kvn_sb", [128, 4 * 4096], BF16))
        P["kpe_sb"] = es.enter_context(sb("kpe_sb", [128, 4096], BF16))
        P["kbuf1"] = es.enter_context(sb("kbuf1", [128, 4096], BF16))
        P["vbuf1"] = es.enter_context(sb("vbuf1", [128, 4096], BF16))
        P["vtmp"] = es.enter_context(sb("vtmp", [128, 512], F32))
        P["qbuf"] = [es.enter_context(sb(f"qbuf{i}", [128, 1024], BF16))
                     for i in range(2)]
        P["qpebuf"] = es.enter_context(sb("qpebuf", [128, 1024], BF16))
        P["wexp"] = [es.enter_context(sb(f"wexp{i}", [128, 1024], BF16))
                     for i in range(2)]
        alloc_flash_bufs(nc, es, P)
        P["wa"] = es.enter_context(sb("wa", [128, 4096], BF16))
        P["wb"] = es.enter_context(sb("wb", [128, 4096], BF16))
        alloc_resid_bufs(nc, es, P, gsz=4)
        block = es.enter_context(nc.Block())
        s = Sched(nc, es, block)
        t_id = s.op("sync", lambda q: q.dma_start(out=P["id_f32"][:], in_=idf[:]),
                    dma=True)
        P["t_idf"] = t_id
        t_kvn = [s.op("sync", lambda q, k=k: q.dma_start(
            out=P["kvn_sb"][:, k * 4096:(k + 1) * 4096], in_=kvnall[k, :, :]),
            dma=True) for k in range(4)]
        t_kpe = s.op("sync", lambda q: q.dma_start(out=P["kpe_sb"][:],
                                                   in_=kpeall[0, :, :]), dma=True)
        ctx_done = [[None] * 2 for _ in range(32)]
        with ExitStack() as es2:
            alloc_flash_psum(nc, es2, P)
            t_tri = s.op("sync", lambda q: q.dma_start(out=P["dmask"][:],
                                                       in_=tri[:]), dma=True)
            t_c1 = s.op("vector", lambda q: q.memset(P["ones_row"][:], 1.0))
            t_c2 = s.op("vector", lambda q: q.memset(P["negscale_row"][:], -scale))
            t_c3 = s.op("vector", lambda q: q.memset(P["onescol"][:], 1.0))
            P["t_ctxfree"] = [t_c1]
            P["t_psfree"] = {0: t_c2, 1: t_c2}
            P["t_pcfree"] = [t_c3]
            P["t_pdenfree"] = [t_c3]
            t_w_prev = [None, None]
            t_q_prev = [None, None]
            t_qpe_prev = [None]
            t_kv_use_prev = [None]
            for hh in range(32):
                hpar = hh % 2
                wsl = wkvb[2 * hh:2 * hh + 2].rearrange("m p j -> p m j")
                wex = P["wexp"][hpar]
                t_w = s.op("gpsimd", lambda q, wsl=wsl, wex=wex: q.dma_start(
                    out=wex[:].rearrange("p (m j) -> p m j", j=512), in_=wsl),
                    deps=[t_w_prev[hpar]], dma=True)
                guard = [P["t_psfree"][0], P["t_psfree"][1], t_kv_use_prev[0]]
                t_kb = []
                t_vb = []
                for c5 in range(8):
                    last = None
                    for ki in range(4):
                        last = s.op("tensor", lambda q, c5=c5, ki=ki, wex=wex:
                                    q.matmul(
                            P["pS"][0][:, 0:512],
                            wex[:, ki * 128:(ki + 1) * 128],
                            P["kvn_sb"][:, ki * 4096 + c5 * 512:
                                        ki * 4096 + (c5 + 1) * 512],
                            start=(ki == 0), stop=(ki == 3)),
                            deps=[t_w] + guard + t_kvn + t_kb[-1:])
                    t_ke = s.op("vector", lambda q, c5=c5: q.tensor_copy(
                        P["kbuf1"][:, c5 * 512:(c5 + 1) * 512], P["pS"][0][:, 0:512]),
                        deps=[last])
                    t_kb.append(t_ke)
                    last = None
                    for ki in range(4):
                        last = s.op("tensor", lambda q, c5=c5, ki=ki, wex=wex:
                                    q.matmul(
                            P["pS"][1][:, 0:512],
                            wex[:, 512 + ki * 128:512 + (ki + 1) * 128],
                            P["kvn_sb"][:, ki * 4096 + c5 * 512:
                                        ki * 4096 + (c5 + 1) * 512],
                            start=(ki == 0), stop=(ki == 3)),
                            deps=[t_w] + guard + t_kvn + t_vb[-1:])
                    t_ve = s.op("vector", lambda q: q.tensor_copy(
                        P["vtmp"][:], P["pS"][1][:, 0:512]), deps=[last])
                    tps = []
                    for i in range(4):
                        tps.append(s.op("tensor", lambda q, i=i: q.transpose(
                            P["pS"][1][:, 512 + i * 128:512 + (i + 1) * 128],
                            P["vtmp"][:, i * 128:(i + 1) * 128], P["id_f32"][:]),
                            deps=[t_ve, t_id]))
                    t_vv = s.op("vector", lambda q, c5=c5: q.tensor_copy(
                        P["vbuf1"][:, c5 * 512:(c5 + 1) * 512],
                        P["pS"][1][:, 512:1024]), deps=tps)
                    t_vb.append(t_vv)
                t_kdone = t_kb[-1]
                t_vdone = t_vb[-1]
                t_w_prev[hpar] = t_kb[-1]
                t_q = s.op("sync", lambda q, hh=hh, hpar=hpar: q.dma_start(
                    out=P["qbuf"][hpar][:],
                    in_=q2_sh[:, hh * 1024:(hh + 1) * 1024]),
                    deps=[t_q_prev[hpar]], dma=True)
                if hh % 2 == 0:
                    t_qpe = s.op("sync", lambda q, hh=hh: q.dma_start(
                        out=P["qpebuf"][:],
                        in_=q2_sh[:, (32 + hh // 2) * 1024:(33 + hh // 2) * 1024]),
                        deps=[t_qpe_prev[0]], dma=True)
                    P["t_qpe"] = t_qpe
                t_qpe = P["t_qpe"]
                qrow = (hh % 2) * 64
                qbuf = P["qbuf"][hpar]
                for st in range(2):

                    def smm(kc, out_ap, q0, deps, qbuf=qbuf, st=st, qrow=qrow,
                            t_q=t_q, t_qpe=t_qpe, t_kdone=t_kdone):
                        t1 = s.op("tensor", lambda q: q.matmul(
                            out_ap, P["kbuf1"][:, kc * 128:(kc + 1) * 128],
                            qbuf[:, st * 512 + q0:st * 512 + 512],
                            start=True, stop=False),
                            deps=deps + [t_q, t_kdone, t_tri])
                        return s.op("tensor", lambda q: q.matmul(
                            out_ap,
                            P["kpe_sb"][qrow:qrow + 64, kc * 128:(kc + 1) * 128],
                            P["qpebuf"][qrow:qrow + 64,
                                        st * 512 + q0:st * 512 + 512],
                            start=False, stop=True),
                            deps=[t_qpe, t_kpe])
                    ctx_dst = P["ctx_all"][:, hh * 1024 + st * 512:
                                           hh * 1024 + (st + 1) * 512]
                    t_fin = emit_flash_half(s, P, None, hh, st, smm,
                                            P["vbuf1"], t_vdone, ctx_dst, scale)
                    ctx_done[hh][st] = t_fin
                t_q_prev[hpar] = ctx_done[hh][1]
                t_qpe_prev[0] = ctx_done[hh][1]
                t_kv_use_prev[0] = ctx_done[hh][1]
        P["pD"] = [es.enter_context(ps(f"pD{i}", [128, 1024], F32))
                   for i in range(2)]
        P["pR"] = es.enter_context(ps("pR", [128, 1024], F32))
        src_deps = {k: [ctx_done[k][0], ctx_done[k][1]] if k > 0 else
                    [ctx_done[hh][hf] for hh in range(32) for hf in range(2)]
                    for k in range(32)}
        t_last = emit_out_proj_residual(s, P, wo, None, P["ctx_all"],
                                        src_deps, x, y, (P["wa"], P["wb"]),
                                        gsz=4)
        s.finish([t_last])
    return nc


def run_attn_l1a(x_shards, inp):
    ident = np.eye(128, dtype=np.float32)
    args = {
        "wqa": tile_w(np.asarray(inp["L1_attn_a_wq_a"], np.float32)),
        "wqb": tile_w(permute_wqb(np.asarray(inp["L1_attn_a_wq_b"], np.float32))),
        "wkva": tile_w(permute_wkva(np.asarray(inp["L1_attn_a_wkv_a"], np.float32))),
        "gq": bias_cols(np.asarray(inp["L1_attn_a_q_norm_w"], np.float32)),
        "gk": bias_cols(np.asarray(inp["L1_attn_a_kv_norm_w"], np.float32)),
        "ident": ident,
    }
    nc = build_attn_proj_l1a()
    in_maps = []
    for c in range(8):
        j = c % 4
        cpe, spe = rope_tables_pe_core(j)
        m = dict(args)
        m.update({"x": x_shards[c], "cpe": cpe, "spe": spe})
        in_maps.append(m)
    r = run_bass_kernel_spmd(nc, in_maps, core_ids=list(range(8)))
    q2_sh = [as_bf16(r.results[c]["q2_sh"]) for c in range(8)]
    kvn_sh = [as_bf16(r.results[c]["kvn_sh"]) for c in range(8)]
    kpe_sh = [as_bf16(r.results[c]["kpe_sh"]) for c in range(8)]
    kvnall = [reshard_feat_heads(kvn_sh, b, n_heads=4) for b in range(2)]
    kpeall = []
    for b in range(2):
        kp = reshard_feat_heads(kpe_sh, b, n_heads=1)
        kp[0, 64:128] = kp[0, 0:64]
        kpeall.append(kp)
    wkvb = tile_w(np.asarray(inp["L1_attn_a_wkv_b"], np.float32))
    wo = tile_w(np.asarray(inp["L1_attn_a_wo"], np.float32))
    nc2 = build_attn_flash_l1a()
    in_maps2 = []
    for c in range(8):
        b, j = divmod(c, 4)
        in_maps2.append({"x": x_shards[c], "q2_sh": q2_sh[c], "kvnall": kvnall[b],
                         "kpeall": kpeall[b], "wkvb": wkvb, "wo": wo,
                         "tri": diag_mask_core(j), "ident": ident})
    r2 = run_bass_kernel_spmd(nc2, in_maps2, core_ids=list(range(8)))
    return [r2.results[c]["y"] for c in range(8)]


def build_attn_proj_l0a():
    nc = bass.Bass("TRN2", target_bir_lowering=False)
    x = nc.dram_tensor("x", [PER, 4096], F32, kind="ExternalInput").ap()
    wqd = nc.dram_tensor("wqd", [128, 32 * 12], F32, kind="ExternalInput").ap()
    wkvd = nc.dram_tensor("wkvd", [128, 32 * 4], F32, kind="ExternalInput").ap()
    wqk = nc.dram_tensor("wqk", [128, 128], F32, kind="ExternalInput").ap()
    bqd = nc.dram_tensor("bqd", [128, 1], F32, kind="ExternalInput").ap()
    bkvd = nc.dram_tensor("bkvd", [128, 1], F32, kind="ExternalInput").ap()
    bqk = nc.dram_tensor("bqk", [128, 8], F32, kind="ExternalInput").ap()
    idf = nc.dram_tensor("ident", [128, 128], F32, kind="ExternalInput").ap()
    q_sh = nc.dram_tensor("q_sh", [128, 8 * 1024], BF16, kind="ExternalOutput").ap()
    ckv_sh = nc.dram_tensor("ckv_sh", [128, 1024], BF16, kind="ExternalOutput").ap()
    with ExitStack() as es:
        P = {}
        sb, ps = nc.sbuf_tensor, nc.psum_tensor
        alloc_ln_bufs(nc, es, P)
        P["id_bf"] = es.enter_context(sb("id_bf", [128, 128], BF16))
        P["id_f32"] = es.enter_context(sb("id_f32", [128, 128], F32))
        P["pLN"] = [es.enter_context(ps(f"pLN{i}", [128, 1024], BF16))
                    for i in range(2)]
        P["pD"] = [es.enter_context(ps(f"pD{i}", [128, 1024], F32))
                   for i in range(2)]
        P["pR"] = es.enter_context(ps("pR", [128, 1024], F32))
        P["wqd_sb"] = es.enter_context(sb("wqd_sb", [128, 32 * 12], BF16))
        P["wkvd_sb"] = es.enter_context(sb("wkvd_sb", [128, 32 * 4], BF16))
        P["wqk_sb"] = es.enter_context(sb("wqk_sb", [128, 128], BF16))
        P["bqd_sb"] = es.enter_context(sb("bqd_sb", [128, 1], F32))
        P["bkvd_sb"] = es.enter_context(sb("bkvd_sb", [128, 1], F32))
        P["bqk_sb"] = es.enter_context(sb("bqk_sb", [128, 8], F32))
        P["cq_sb"] = es.enter_context(sb("cq_sb", [128, 1024], BF16))
        P["ckv_b"] = es.enter_context(sb("ckv_b", [128, 1024], BF16))
        P["qp_sb"] = es.enter_context(sb("qp_sb", [128, 1024], BF16))
        block = es.enter_context(nc.Block())
        s = Sched(nc, es, block)
        t_id = load_ident(s, P, idf)
        lds = []
        for nm, src_t in (("wqd_sb", wqd), ("wkvd_sb", wkvd), ("wqk_sb", wqk)):
            lds.append(s.op("gpsimd", lambda q, nm=nm, src_t=src_t: q.dma_start(
                out=P[nm][:], in_=src_t[:]), dma=True))
        for nm, src_t in (("bqd_sb", bqd), ("bkvd_sb", bkvd), ("bqk_sb", bqk)):
            lds.append(s.op("sync", lambda q, nm=nm, src_t=src_t: q.dma_start(
                out=P[nm][:], in_=src_t[:]), dma=True))
        kdep = emit_ln(s, P, x)
        t_cq = None
        for hf in range(2):
            for k in range(32):
                t_cq = s.op("tensor", lambda q, hf=hf, k=k: q.matmul(
                    P["pR"][0:12, hf * 512:(hf + 1) * 512],
                    P["wqd_sb"][:, k * 12:(k + 1) * 12],
                    P["lnt"][:, k * PER + hf * 512:k * PER + (hf + 1) * 512],
                    start=(k == 0), stop=(k == 31)),
                    deps=lds[:1] + kdep[k])
        t_cqe = s.op("vector", lambda q: q.tensor_scalar(
            P["cq_sb"][0:12, :], P["pR"][0:12, 0:1024], P["bqd_sb"][0:12, :],
            None, op0=ALU.add), deps=[t_cq, lds[3]])
        t_ck = None
        for hf in range(2):
            for k in range(32):
                t_ck = s.op("tensor", lambda q, hf=hf, k=k: q.matmul(
                    P["pD"][0][0:4, hf * 512:(hf + 1) * 512],
                    P["wkvd_sb"][:, k * 4:(k + 1) * 4],
                    P["lnt"][:, k * PER + hf * 512:k * PER + (hf + 1) * 512],
                    start=(k == 0), stop=(k == 31)),
                    deps=lds[1:2] + kdep[k])
        t_cke = s.op("vector", lambda q: q.tensor_scalar(
            P["ckv_b"][0:4, :], P["pD"][0][0:4, 0:1024], P["bkvd_sb"][0:4, :],
            None, op0=ALU.add), deps=[t_ck, lds[4]])
        t_ckd = s.op("sync", lambda q: q.dma_start(out=ckv_sh[:], in_=P["ckv_b"][:]),
                     deps=[t_cke], dma=True)
        t_qd = None
        t_qe_prev = None
        for g in range(8):
            mms = []
            for sl in range(4):
                h = 4 * g + sl
                for hf in range(2):
                    mms.append(s.op("tensor", lambda q, sl=sl, h=h, hf=hf: q.matmul(
                        P["pD"][1][32 * sl:32 * sl + 4, hf * 512:(hf + 1) * 512],
                        P["wqk_sb"][0:12, 4 * h:4 * h + 4],
                        P["cq_sb"][0:12, hf * 512:(hf + 1) * 512],
                        start=True, stop=True, tile_position=(0, 32 * sl)),
                        deps=[t_cqe, lds[2], t_qe_prev]))
            t_qe = s.op("vector", lambda q, g=g: q.tensor_scalar(
                P["qp_sb"][:], P["pD"][1][:, 0:1024], P["bqk_sb"][:, g:g + 1],
                None, op0=ALU.add), deps=mms + [lds[5], t_qd])
            t_qe_prev = t_qe
            t_qd = s.op("sync", lambda q, g=g: q.dma_start(
                out=q_sh[:, g * 1024:(g + 1) * 1024], in_=P["qp_sb"][:]),
                deps=[t_qe], dma=True)
        s.finish([t_qd, t_ckd])
    return nc


def emit_flash_half_l0a(s, P, h, st, smm_cb, ckvt_sb, ckvt_dep, out_cb, scale):
    kcs = list(range(32))
    sc = P["fsc"]
    Crun, csb, cnew, aold, anew = (sc[:, i:i + 1] for i in range(5))
    mcol = sc[:, 5:9]
    dsc = P["fsd"]
    m_run = P["m_run"]
    t0 = s.op("vector", lambda q: q.memset(m_run[:], 0.0), deps=[P["t_mrfree"][0]])
    t0c = s.op("vector", lambda q: q.memset(Crun, -1e30), deps=[t0])
    t_exp_done = {0: P["t_psfree"][0], 1: P["t_psfree"][1]}
    t_comb_prev = [t0c]
    t_pv_prev = [P["t_pcfree"][0]]
    rounds = [kcs[i:i + 3] for i in range(0, len(kcs), 3)]
    for ri, rk in enumerate(rounds):
        par = ri % 2
        pS = P["pS"][par]
        Eb = P["Eb"][par]
        t_mx = []
        for slot, kc in enumerate(rk):
            out_ap = pS[:, slot * 512:slot * 512 + 512]
            t_s = smm_cb(kc, out_ap, 0, [t_exp_done[par]])
            t_mx.append(s.op("vector", lambda q, out_ap=out_ap, slot=slot:
                             q.reduce_max(mcol[:, slot:slot + 1], out_ap,
                                          axis=AX.X), deps=[t_s]))
        t_c = s.op("vector", lambda q, n=len(rk): q.reduce_max(
            mcol[:, 3:4], mcol[:, 0:n], axis=AX.X), deps=t_mx)
        t_tp = s.op("tensor", lambda q: q.transpose(
            P["pCt"][0:1, 0:128], mcol[:, 3:4], P["id_f32"][:]),
            deps=[t_c, t_pv_prev[0], t_comb_prev[0], P.get("t_idf")])
        t_cf = s.op("vector", lambda q: q.reduce_max(
            dsc[0:1, 2:3], P["pCt"][0:1, 0:128], axis=AX.X), deps=[t_tp])
        t_cb = s.op("vector", lambda q: q.tensor_copy(P["cbf"][:], dsc[0:1, 2:3]),
                    deps=[t_cf])
        t_nb = s.op("tensor", lambda q: q.matmul(
            P["pCb"][:, 0:1], P["negscale_row"][:], P["cbf"][:],
            start=True, stop=True), deps=[t_cb])
        t_nbs = s.op("vector", lambda q: q.tensor_copy(P["nb_sb"][:],
                                                       P["pCb"][:, 0:1]),
                     deps=[t_nb])
        t_bc = _bcast128(s, P, dsc[0:1, 2:3], sc[:, 9:10], [t_cf])
        t_cn = s.op("vector", lambda q: q.tensor_tensor(
            cnew, Crun, sc[:, 9:10], op=ALU.max), deps=[t_cf, t_bc])
        t_d1 = tt_op(s, sc[:, 10:11], Crun, cnew, ALU.subtract, deps=[t_cn])
        t_a1 = s.op("scalar", lambda q: q.activation(
            out=aold, in_=sc[:, 10:11], func=AF.Exp, scale=scale), deps=[t_d1])
        t_d2 = tt_op(s, sc[:, 11:12], sc[:, 9:10], cnew, ALU.subtract, deps=[t_cn])
        t_a2 = s.op("scalar", lambda q: q.activation(
            out=anew, in_=sc[:, 11:12], func=AF.Exp, scale=scale), deps=[t_d2])
        t_cr = s.op("vector", lambda q: q.tensor_copy(Crun, cnew), deps=[t_a1, t_a2])
        t_exps = []
        for slot, kc in enumerate(rk):
            t_exps.append(s.op("scalar", lambda q, slot=slot: q.activation(
                out=Eb[:, slot * 512:slot * 512 + 512],
                in_=pS[:, slot * 512:slot * 512 + 512],
                func=AF.Exp, scale=scale, bias=P["nb_sb"][:]),
                deps=[t_nbs] + t_mx))
        t_exp_done[par] = t_exps[-1]
        t_pv = None
        for slot, kc in enumerate(rk):
            t_pv = s.op("tensor", lambda q, slot=slot, kc=kc,
                        first=(slot == 0): q.matmul(
                P["pM"][0:36, 0:512], ckvt_sb[:, kc * 36:(kc + 1) * 36],
                Eb[:, slot * 512:slot * 512 + 512],
                start=first, stop=(slot == len(rk) - 1)),
                deps=[t_exps[slot], ckvt_dep, t_pv_prev[0], t_nbs, t_bc])
        t_m1 = s.op("vector", lambda q: q.tensor_scalar(
            m_run[:], m_run[:], aold[0:36, :], None, op0=ALU.mult),
            deps=[t_cr, t_a1])
        t_m2 = s.op("vector", lambda q: q.scalar_tensor_tensor(
            m_run[:], P["pM"][0:36, 0:512], anew[0:36, :], m_run[:],
            op0=ALU.mult, op1=ALU.add), deps=[t_m1, t_pv, t_a2])
        t_comb_prev = [t_m2]
        t_pv_prev = [t_m2]
    t_l = s.op("scalar", lambda q: q.activation(
        out=P["dlog"][0:1, 0:512], in_=m_run[0:1, :], func=AF.Ln),
        deps=[t_comb_prev[0]])
    t_r = s.op("scalar", lambda q: q.activation(
        out=P["dlog"][0:1, 512:1024], in_=P["dlog"][0:1, 0:512],
        func=AF.Exp, scale=-1.0), deps=[t_l])
    t_rb = s.op("vector", lambda q: q.tensor_copy(P["recip_bf"][:],
                                                  P["dlog"][0:1, 512:1024]),
                deps=[t_r])
    t_ex = s.op("tensor", lambda q: q.matmul(
        P["pS"][0][:, 0:512], P["ones_row"][:], P["recip_bf"][:],
        start=True, stop=True), deps=[t_rb, t_exp_done[0]])
    t_fin = out_cb(m_run, P["pS"][0], [t_ex, t_comb_prev[0]])
    P["t_psfree"][0] = t_fin
    P["t_psfree"][1] = t_exp_done[1]
    P["t_pcfree"][0] = t_comb_prev[0]
    P["t_mrfree"][0] = t_fin
    return t_fin


def build_attn_flash_l0a():
    scale = 0.5
    nc = bass.Bass("TRN2", target_bir_lowering=False)
    x = nc.dram_tensor("x", [PER, 4096], F32, kind="ExternalInput").ap()
    q_sh = nc.dram_tensor("q_sh", [128, 8 * 1024], BF16, kind="ExternalInput").ap()
    ckvf = nc.dram_tensor("ckvf", [128, 4096], BF16, kind="ExternalInput").ap()
    ckvt = nc.dram_tensor("ckvt", [128, 32 * 36], BF16, kind="ExternalInput").ap()
    wvu = nc.dram_tensor("wvu", [128, 4096], F32, kind="ExternalInput").ap()
    bvu = nc.dram_tensor("bvu", [128, 32], F32, kind="ExternalInput").ap()
    wo = nc.dram_tensor("wo", [32, 128, 4096], F32, kind="ExternalInput").ap()
    bo = nc.dram_tensor("bo", [128, 32], F32, kind="ExternalInput").ap()
    idf = nc.dram_tensor("ident", [128, 128], F32, kind="ExternalInput").ap()
    y = nc.dram_tensor("y", [PER, 4096], F32, kind="ExternalOutput").ap()
    with ExitStack() as es:
        P = {}
        sb, ps = nc.sbuf_tensor, nc.psum_tensor
        P["id_f32"] = es.enter_context(sb("id_f32", [128, 128], F32))
        P["ctx_all"] = es.enter_context(sb("ctx_all", [128, 32 * 1024], BF16))
        P["qp_sb"] = es.enter_context(sb("qp_sb", [128, 8 * 1024], BF16))
        P["ckvf_sb"] = es.enter_context(sb("ckvf_sb", [128, 4096], BF16))
        P["ckvt_sb"] = es.enter_context(sb("ckvt_sb", [128, 32 * 36], BF16))
        P["wvu_sb"] = es.enter_context(sb("wvu_sb", [128, 4096], BF16))
        P["bvu_sb"] = es.enter_context(sb("bvu_sb", [128, 32], F32))
        P["M_all"] = es.enter_context(sb("M_all", [128, 8 * 1024], BF16))
        P["m_run"] = es.enter_context(sb("m_run", [36, 512], F32))
        alloc_flash_bufs(nc, es, P)
        P["wa"] = es.enter_context(sb("wa", [128, 4096], BF16))
        P["wb"] = es.enter_context(sb("wb", [128, 4096], BF16))
        alloc_resid_bufs(nc, es, P)
        P["bias_sb"] = es.enter_context(sb("bias_sb", [128, 32], F32))
        block = es.enter_context(nc.Block())
        s = Sched(nc, es, block)
        t_id = s.op("sync", lambda q: q.dma_start(out=P["id_f32"][:], in_=idf[:]),
                    dma=True)
        P["t_idf"] = t_id
        t_q = s.op("sync", lambda q: q.dma_start(out=P["qp_sb"][:], in_=q_sh[:]),
                   dma=True)
        t_cf = s.op("sync", lambda q: q.dma_start(out=P["ckvf_sb"][:], in_=ckvf[:]),
                    dma=True)
        t_ct = s.op("sync", lambda q: q.dma_start(out=P["ckvt_sb"][:], in_=ckvt[:]),
                    dma=True)
        t_wv = s.op("gpsimd", lambda q: q.dma_start(out=P["wvu_sb"][:], in_=wvu[:]),
                    dma=True)
        t_bv = s.op("sync", lambda q: q.dma_start(out=P["bvu_sb"][:], in_=bvu[:]),
                    dma=True)
        t_bo = s.op("sync", lambda q: q.dma_start(out=P["bias_sb"][:], in_=bo[:]),
                    dma=True)
        M_done = {}
        with ExitStack() as es2:
            P["pS"] = [es2.enter_context(ps(f"pS{i}", [128, 1536], F32))
                       for i in range(2)]
            P["pM"] = es2.enter_context(ps("pM", [128, 512], F32))
            P["pCt"] = P["pM"][0:1, 0:128]
            P["pCb"] = P["pM"][:, 256:264]
            t_c1 = s.op("vector", lambda q: q.memset(P["ones_row"][:], 1.0))
            t_c2 = s.op("vector", lambda q: q.memset(P["negscale_row"][:], -scale))
            P["t_psfree"] = {0: t_c2, 1: t_c2}
            P["t_pcfree"] = [t_c1]
            P["t_mrfree"] = [t_c1]
            for g in range(8):
                for sl in range(4):
                    h = 4 * g + sl
                    for st in range(2):

                        def smm(kc, out_ap, q0, deps, g=g, sl=sl, st=st):
                            return s.op("tensor", lambda q: q.matmul(
                                out_ap,
                                P["ckvf_sb"][32 * sl:32 * sl + 4,
                                             kc * 128:(kc + 1) * 128],
                                P["qp_sb"][32 * sl:32 * sl + 4,
                                           g * 1024 + st * 512:
                                           g * 1024 + st * 512 + 512],
                                start=True, stop=True,
                                tile_position=(32 * sl, 0)),
                                deps=deps + [t_q, t_cf])

                        def out_cb(m_run, pS0, deps, g=g, sl=sl, st=st):
                            dst = P["M_all"][32 * sl:32 * sl + 4,
                                             g * 1024 + st * 512:
                                             g * 1024 + st * 512 + 512]
                            return s.op("vector", lambda q: q.tensor_tensor(
                                dst, m_run[32:36, :], pS0[32:36, 0:512],
                                op=ALU.mult), deps=deps)
                        M_done[(h, st)] = emit_flash_half_l0a(
                            s, P, h, st, smm, P["ckvt_sb"], t_ct, out_cb, scale)
        P["pD"] = [es.enter_context(ps(f"pD{i}", [128, 1024], F32))
                   for i in range(2)]
        P["pR"] = es.enter_context(ps("pR", [128, 1024], F32))
        all_M = list(M_done.values())
        t_ue_prev = [all_M[-1], all_M[-1]]
        ctx_done = {}
        for h in range(32):
            g, sl = h // 4, h % 4
            p = P["pD"][h % 2]
            mms = []
            for hf in range(2):
                mms.append(s.op("tensor", lambda q, h=h, g=g, sl=sl, hf=hf:
                                q.matmul(
                    p[:, hf * 512:(hf + 1) * 512],
                    P["wvu_sb"][32 * sl:32 * sl + 4, h * 128:(h + 1) * 128],
                    P["M_all"][32 * sl:32 * sl + 4,
                               g * 1024 + hf * 512:g * 1024 + (hf + 1) * 512],
                    start=True, stop=True, tile_position=(32 * sl, 0)),
                    deps=[t_wv, M_done[(h, 0)], M_done[(h, 1)],
                          t_ue_prev[h % 2]] + all_M))
            t_ue = s.op("vector", lambda q, h=h, p=p: q.tensor_scalar(
                P["ctx_all"][:, h * 1024:(h + 1) * 1024], p[:, 0:1024],
                P["bvu_sb"][:, h:h + 1], None, op0=ALU.add),
                deps=mms + [t_bv])
            t_ue_prev[h % 2] = t_ue
            ctx_done[h] = t_ue
        src_deps = {k: [ctx_done[k]] if k > 0 else list(ctx_done.values())
                    for k in range(32)}
        t_last = emit_out_proj_residual(s, P, wo, P["bias_sb"], P["ctx_all"],
                                        src_deps, x, y, (P["wa"], P["wb"]))
        s.finish([t_last])
    return nc


def run_attn_l0a(x_shards, inp):
    ident = np.eye(128, dtype=np.float32)
    Wq_d = np.asarray(inp["L0_attn_a_Wq_d"], np.float32)
    Wkv_d = np.asarray(inp["L0_attn_a_Wkv_d"], np.float32)
    W_qk = np.asarray(inp["L0_attn_a_W_qk"], np.float32)
    wqd = np.ascontiguousarray(
        Wq_d.reshape(32, 128, 12).transpose(1, 0, 2).reshape(128, 32 * 12))
    wkvd = np.ascontiguousarray(
        Wkv_d.reshape(32, 128, 4).transpose(1, 0, 2).reshape(128, 32 * 4))
    wqk = np.zeros((128, 128), np.float32)
    wqk[0:12] = W_qk
    def padcol(v):
        o = np.zeros((128, 1), np.float32)
        o[:len(v), 0] = v
        return o
    bqk = np.zeros((128, 8), np.float32)
    bqkv = np.asarray(inp["L0_attn_a_b_qk"], np.float32)
    for g in range(8):
        for sq in range(4):
            bqk[32 * sq:32 * sq + 4, g] = bqkv[4 * (4 * g + sq):4 * (4 * g + sq) + 4]
    args = {"wqd": wqd, "wkvd": wkvd, "wqk": wqk,
            "bqd": padcol(np.asarray(inp["L0_attn_a_bq_d"], np.float32)),
            "bkvd": padcol(np.asarray(inp["L0_attn_a_bkv_d"], np.float32)),
            "bqk": bqk, "ident": ident}
    nc = build_attn_proj_l0a()
    in_maps = [dict(args, x=x_shards[c]) for c in range(8)]
    r = run_bass_kernel_spmd(nc, in_maps, core_ids=list(range(8)))
    q_sh = [as_bf16(r.results[c]["q_sh"]) for c in range(8)]
    ckv_sh = [as_bf16(r.results[c]["ckv_sh"]) for c in range(8)]
    ckvg = []
    for b in range(2):
        g = np.empty((4, 4096), dtype=ckv_sh[0].dtype)
        for j in range(4):
            g[:, _POS_IDX[j]] = np.asarray(ckv_sh[b * 4 + j])[0:4, :]
        ckvg.append(g)
    ckvf, ckvt = [], []
    for b in range(2):
        cf = np.zeros((128, 4096), np.float32)
        for sq in range(4):
            cf[32 * sq:32 * sq + 4] = ckvg[b].astype(np.float32)
        ckvf.append(np.ascontiguousarray(cf.astype(ml_dtypes.bfloat16)))
        ct = np.zeros((128, 32 * 36), np.float32)
        cg = ckvg[b].astype(np.float32)
        for kc in range(32):
            ct[:, kc * 36] = 1.0
            ct[:, kc * 36 + 32:kc * 36 + 36] = cg[:, kc * 128:(kc + 1) * 128].T
        ckvt.append(np.ascontiguousarray(ct.astype(ml_dtypes.bfloat16)))
    Wv_u = np.asarray(inp["L0_attn_a_Wv_u"], np.float32)
    wvu = np.zeros((128, 4096), np.float32)
    for sq in range(4):
        wvu[32 * sq:32 * sq + 4] = Wv_u
    bvu = K_bias32(np.asarray(inp["L0_attn_a_bv_u"], np.float32))
    bo = K_bias32(np.asarray(inp["L0_attn_a_bo"], np.float32))
    wo = tile_w(np.ascontiguousarray(np.asarray(inp["L0_attn_a_Wo"],
                                                np.float32).T))
    nc2 = build_attn_flash_l0a()
    in_maps2 = []
    for c in range(8):
        b = c // 4
        in_maps2.append({"x": x_shards[c], "q_sh": q_sh[c], "ckvf": ckvf[b],
                         "ckvt": ckvt[b], "wvu": wvu, "bvu": bvu, "wo": wo,
                         "bo": bo, "ident": ident})
    r2 = run_bass_kernel_spmd(nc2, in_maps2, core_ids=list(range(8)))
    return [r2.results[c]["y"] for c in range(8)]


def K_bias32(v):
    return bias_cols(v)


# version 02 (final)
# speedup vs baseline: 1.3922x; 1.3922x over previous
import sys
import numpy as np
from contextlib import ExitStack

sys.path.insert(0, "/opt/trn_rl_repo")
import concourse.bass as bass
import concourse.mybir as mybir
from concourse.bass_utils import run_bass_kernel_spmd

import ml_dtypes

F32 = mybir.dt.float32
BF16 = mybir.dt.bfloat16
I32 = mybir.dt.int32
AF = mybir.ActivationFunctionType
ALU = mybir.AluOpType
AX = mybir.AxisListType

ENGINES = ("tensor", "vector", "scalar", "gpsimd", "sync")

B, S, D = 2, 4096, 4096
PER = 1024
NQT = PER // 128
NK = D // 128


class Sched:

    NDMA = 3

    def __init__(self, nc, es, block):
        self.nc, self.block = nc, block
        self.sems = {e: es.enter_context(nc.semaphore(f"s_{e}")) for e in ENGINES}
        self.cnt = {e: 0 for e in ENGINES}
        self.hw = {e: {} for e in ENGINES}
        self.nops = 0
        self.dsems = {e: [es.enter_context(nc.semaphore(f"d_{e}{i}"))
                          for i in range(self.NDMA)] for e in ENGINES}
        self.dcnt = {e: [0] * self.NDMA for e in ENGINES}
        self.dix = {e: 0 for e in ENGINES}
        self.dring = {e: [None] * self.NDMA for e in ENGINES}

    SERIAL = False
    SERIAL_ENGS = ()

    def op(self, eng, fn, deps=(), dma=False):
        if Sched.SERIAL or eng in Sched.SERIAL_ENGS:
            deps = list(deps) + [(e, self.cnt[e]) for e in ENGINES
                                 if self.cnt[e] > 0]
        if dma:
            slot = self.dix[eng]
            deps = list(deps) + [self.dring[eng][slot]]
        hw = self.hw[eng]
        waits = []
        for d in deps:
            if d is None:
                continue
            seng, val = d
            if seng == eng:
                if eng == "tensor":
                    continue
                if isinstance(val, int) and val > self.cnt[eng]:
                    raise RuntimeError("self-dep on future op")
            if hw.get(seng, 0) < val:
                waits.append((seng, val))
                hw[seng] = val
        sems = dict(self.sems)
        for e in ENGINES:
            for i in range(self.NDMA):
                sems[(e, i)] = self.dsems[e][i]
        self.nops += 1
        if dma:
            self.dcnt[eng][slot] += 16
            tok = ((eng, slot), self.dcnt[eng][slot])
            self.dring[eng][slot] = tok
            self.dix[eng] = (slot + 1) % self.NDMA
            dsem = self.dsems[eng][slot]

            def body(q):
                for seng, val in waits:
                    q.wait_ge(sems[seng], val)
                fn(q).then_inc(dsem, 16)
        else:
            self.cnt[eng] += 1
            tok = (eng, self.cnt[eng])
            mysem = self.sems[eng]

            def body(q):
                for seng, val in waits:
                    q.wait_ge(sems[seng], val)
                fn(q).then_inc(mysem, 1)
        getattr(self.block, eng)(body)
        return tok

    def finish(self, deps):
        self.op("sync", lambda q: q.nop(), deps=deps)


_POS_IDX = [np.asarray([p for p in range(S) if (p // 32) % 4 == j])
            for j in range(4)]


def core_positions(j):
    return _POS_IDX[j]


def shard_rows(arr):
    out = []
    for c in range(8):
        b, j = divmod(c, 4)
        out.append(np.ascontiguousarray(arr[b, _POS_IDX[j]]))
    return out


def unshard_rows(shards):
    sh = shards[0].shape[1:]
    out = np.empty((B, S) + sh, dtype=shards[0].dtype)
    for c in range(8):
        b, j = divmod(c, 4)
        out[b, _POS_IDX[j]] = shards[c]
    return out


def tile_w(w_do_di):
    do, di = w_do_di.shape
    nm, nk = do // 128, di // 128
    t = w_do_di.reshape(nm, 128, nk, 128).transpose(0, 3, 2, 1)
    return np.ascontiguousarray(t.reshape(nm, 128, nk * 128))


def bias_cols(b_vec):
    nm = b_vec.shape[0] // 128
    return np.ascontiguousarray(b_vec.reshape(nm, 128).T)


def emit_ln(s, P, x_dram, n_tok=PER, eps=1e-5):
    nqt = n_tok // 128
    lnt = P["lnt"]
    eps_t = P["eps"]
    t_eps = s.op("vector", lambda q: q.memset(eps_t[:], eps))
    evac = [[None] * 4 for _ in range(nqt)]
    t_norm_prev = [None, None]
    t_evac_prev = [None, None]
    t_tp_last = [None]
    for qt in range(nqt):
        xb_t = (P["xa"], P["xb"])[qt % 2]
        t_x = s.op("sync", lambda q, qt=qt, xb_t=xb_t: q.dma_start(
            out=xb_t[:], in_=x_dram[qt * 128:(qt + 1) * 128, :]),
            deps=[t_norm_prev[qt % 2]], dma=True)
        par = qt % 2
        t_sts = []
        for g in range(8):
            st = P["stats"][:, par * 48 + g * 6:par * 48 + (g + 1) * 6]
            t_sts.append(s.op("vector", lambda q, g=g, st=st, xb_t=xb_t: q.bn_stats(
                st, xb_t[:, g * 512:(g + 1) * 512]), deps=[t_x]))
        mvv = P["mv"][:, par * 4:par * 4 + 2]
        lnvv = P["mv"][:, par * 4 + 2:par * 4 + 3]
        nmv = P["mv"][:, par * 4 + 3:par * 4 + 4]
        rstdv = P["rstd"][:, par:par + 1]
        t_mv = s.op("vector", lambda q, par=par, mvv=mvv: q.bn_aggr(
            mvv, P["stats"][:, par * 48:par * 48 + 48]), deps=t_sts)
        t_lnv = s.op("scalar", lambda q, mvv=mvv, lnvv=lnvv: q.activation(
            out=lnvv, in_=mvv[:, 1:2], func=AF.Ln, bias=eps_t[:]),
            deps=[t_mv, t_eps])
        t_rstd = s.op("scalar", lambda q, lnvv=lnvv, rstdv=rstdv: q.activation(
            out=rstdv, in_=lnvv, func=AF.Exp, scale=-0.5), deps=[t_lnv])
        t_nm = s.op("vector", lambda q, mvv=mvv, nmv=nmv: q.tensor_scalar(
            nmv, mvv[:, 0:1], -1.0, None, op0=ALU.mult), deps=[t_mv])
        t_norm = s.op("vector", lambda q, xb_t=xb_t, nmv=nmv, rstdv=rstdv:
                      q.tensor_scalar(P["lntok"][:], xb_t[:], nmv, rstdv,
                                      op0=ALU.add, op1=ALU.mult),
                      deps=[t_x, t_nm, t_rstd, t_tp_last[0]])
        t_norm_prev[qt % 2] = t_norm
        for g in range(4):
            p = P["pLN"][g % 2]
            tps = []
            for kk in range(8):
                k = g * 8 + kk
                tps.append(s.op("tensor", lambda q, k=k, kk=kk, p=p: q.transpose(
                    p[:, kk * 128:(kk + 1) * 128],
                    P["lntok"][:, k * 128:(k + 1) * 128], P["id_bf"][:]),
                    deps=[t_norm, t_evac_prev[g % 2], P.get("t_idbf")]))
            t_tp_last[0] = tps[-1]
            dst = lnt[:].rearrange("p (k t) -> p k t", t=n_tok)[
                :, g * 8:(g + 1) * 8, qt * 128:(qt + 1) * 128]
            src = p[:].rearrange("p (k t) -> p k t", t=128)
            t_e = s.op("vector", lambda q, src=src, dst=dst: q.tensor_copy(dst, src),
                       deps=tps)
            t_evac_prev[g % 2] = t_e
            evac[qt][g] = t_e
    kdep = {k: [evac[qt][k // 8] for qt in range(nqt)] for k in range(32)}
    return kdep


def emit_residual_group(s, P, x_dram, y_dram, m_group, dfeat_deps, prev_guard,
                        n_tok=PER):
    nqt = n_tok // 128
    g0 = m_group[0]
    ng = len(m_group)
    pR = P["pR"]
    stores = []
    rs = P.setdefault("resid_state", {"add": [None, None], "store": [None, None]})
    t_add_prev = rs["add"]
    t_store_prev = rs["store"]
    t_tp_prev = prev_guard
    t_add_last = None
    for qt in range(nqt):
        xs = (P["xs0"], P["xs1"])[qt % 2]
        ys = (P["ys0"], P["ys1"])[qt % 2]
        t_x = s.op("sync", lambda q, qt=qt, xs=xs: q.dma_start(
            out=xs[:, 0:ng * 128],
            in_=x_dram[qt * 128:(qt + 1) * 128, g0 * 128:g0 * 128 + ng * 128]),
            deps=[t_add_prev[qt % 2]], dma=True)
        tps = []
        for i in range(ng):
            tps.append(s.op("tensor", lambda q, i=i, qt=qt: q.transpose(
                pR[:, i * 128:(i + 1) * 128],
                P["dfeat"][:, i * 1024 + qt * 128:i * 1024 + (qt + 1) * 128],
                P["id_f32"][:]),
                deps=dfeat_deps + [t_tp_prev, t_add_last, P.get("t_idf")]))
        t_tp_prev = tps[-1]
        t_add = s.op("vector", lambda q, xs=xs, ys=ys: q.tensor_tensor(
            ys[:, 0:ng * 128], pR[:, 0:ng * 128], xs[:, 0:ng * 128], op=ALU.add),
            deps=tps + [t_x, t_store_prev[qt % 2]])
        t_add_prev[qt % 2] = t_add
        t_add_last = t_add
        tst = s.op("sync", lambda q, qt=qt, ys=ys: q.dma_start(
            out=y_dram[qt * 128:(qt + 1) * 128, g0 * 128:g0 * 128 + ng * 128],
            in_=ys[:, 0:ng * 128]), deps=[t_add], dma=True)
        t_store_prev[qt % 2] = tst
        stores.append(tst)
    return stores


def ts_op(s, out, in0, s1, s2, op0, op1=None, deps=(), eng="vector"):
    if op1 is None:
        return s.op(eng, lambda q: q.tensor_scalar(out, in0, s1, None, op0=op0),
                    deps=deps)
    return s.op(eng, lambda q: q.tensor_scalar(out, in0, s1, s2, op0=op0, op1=op1),
                deps=deps)


def tt_op(s, out, a, b, op, deps=(), eng="vector"):
    return s.op(eng, lambda q: q.tensor_tensor(out, a, b, op=op), deps=deps)


def stt_op(s, out, in0, sc, in1, op0, op1, deps=(), eng="vector"):
    return s.op(eng, lambda q: q.scalar_tensor_tensor(out, in0, sc, in1,
                                                      op0=op0, op1=op1), deps=deps)


def emit_searchsorted_wm(s, P, lsi_dram, bvec_sb, t_pr_free):
    sc = P["sc"]
    t0 = s.op("sync", lambda q: q.dma_start(out=P["lsi_i"][:], in_=lsi_dram[:]),
              dma=True)
    lsi_f = sc[:, 0:4]
    t1 = s.op("vector", lambda q: q.tensor_copy(lsi_f, P["lsi_i"][:]), deps=[t0])
    iota3 = P["cvec_sb"][:, 0:3]
    c = sc[:, 4:8]
    nb = sc[:, 28:29]
    d0 = ts_op(s, nb, bvec_sb, -1.0, None, ALU.mult, deps=[P["t_bvec"]])
    d = ts_op(s, c, lsi_f, nb, None, ALU.add, deps=[t1, d0])
    d = ts_op(s, c, c, 0.0, 1.0, ALU.max, ALU.min, deps=[d])
    c0, c1, c2, c3 = (c[:, j:j + 1] for j in range(4))
    u, t1v, t2v, v, w = (sc[:, 8 + j:9 + j] for j in range(5))
    r1, nc2, r2, res = (sc[:, 13 + j:14 + j] for j in range(4))
    d = ts_op(s, u, c0, -1.0, 1.0, ALU.mult, ALU.add, deps=[d])
    d = tt_op(s, t1v, c1, u, ALU.mult, deps=[d])
    d = ts_op(s, t2v, c1, -2.0, 2.0, ALU.mult, ALU.add, deps=[d])
    d = tt_op(s, v, t1v, t2v, ALU.add, deps=[d])
    d = ts_op(s, w, c3, -1.0, 4.0, ALU.mult, ALU.add, deps=[d])
    d = tt_op(s, r1, c2, v, ALU.mult, deps=[d])
    d = ts_op(s, nc2, c2, -1.0, 1.0, ALU.mult, ALU.add, deps=[d])
    d = tt_op(s, r2, nc2, w, ALU.mult, deps=[d])
    d = tt_op(s, res, r1, r2, ALU.add, deps=[d])
    seg, aidx, use, eq3 = sc[:, 17:18], sc[:, 18:19], sc[:, 19:20], sc[:, 20:23]
    d = ts_op(s, seg, res, -1.0, None, ALU.add, deps=[d])
    d2 = ts_op(s, aidx, seg, -1.0, 0.0, ALU.add, ALU.max, deps=[d])
    d2 = ts_op(s, aidx, aidx, 2.0, None, ALU.min, deps=[d2])
    d3 = ts_op(s, use, seg, 0.0, 1.0, ALU.max, ALU.min, deps=[d])
    d4 = ts_op(s, eq3, iota3, aidx, None, ALU.is_equal, deps=[d2, P["t_cvec"]])
    d4 = ts_op(s, P["wm3"][:], eq3, use, None, ALU.mult, deps=[d4, d3])
    t_o = s.op("vector", lambda q: q.memset(P["ones11"][:], 1.0))
    ps = P["pR"]
    t_m1 = s.op("tensor", lambda q: q.matmul(ps[0:3, 0:1], P["wm3"][0:1, :],
                                             P["ones11"][:], start=True, stop=True),
                deps=[d4, t_o, t_pr_free])
    t_e1 = s.op("vector", lambda q: q.tensor_copy(P["wm3t"][:], ps[0:3, 0:1]),
                deps=[t_m1])
    t_m2 = s.op("tensor", lambda q: q.matmul(ps[0:48, 1:2], P["sel_sb"][:],
                                             P["wm3t"][:], start=True, stop=True),
                deps=[t_e1])
    t_e2 = s.op("vector", lambda q: q.tensor_copy(P["wm48"][:], ps[0:48, 1:2]),
                deps=[t_m2])
    return t_e2, P["wm48"]


def alloc_common(nc, es, P, n_tok=PER):
    sb, ps = nc.sbuf_tensor, nc.psum_tensor
    P["xa"] = es.enter_context(sb("xa", [128, 4096], F32))
    P["xb"] = es.enter_context(sb("xb", [128, 4096], F32))
    P["lntok"] = es.enter_context(sb("lntok", [128, 4096], BF16))
    P["stats"] = es.enter_context(sb("stats", [128, 96], F32))
    P["mv"] = es.enter_context(sb("mv", [128, 8], F32))
    P["rstd"] = es.enter_context(sb("rstd", [128, 2], F32))
    P["eps"] = es.enter_context(sb("eps", [128, 1], F32))
    P["lnt"] = es.enter_context(sb("lnt", [128, 32 * n_tok], BF16))
    P["id_bf"] = es.enter_context(sb("id_bf", [128, 128], BF16))
    P["id_f32"] = es.enter_context(sb("id_f32", [128, 128], F32))
    P["pLN"] = [es.enter_context(ps(f"pLN{i}", [128, 1024], BF16)) for i in range(2)]
    P["pD"] = [es.enter_context(ps(f"pD{i}", [128, 1024], F32)) for i in range(2)]
    P["pR"] = es.enter_context(ps("pR", [128, 1024], F32))
    P["xs0"] = es.enter_context(sb("xs0", [128, 1024], F32))
    P["xs1"] = es.enter_context(sb("xs1", [128, 1024], F32))
    P["ys0"] = es.enter_context(sb("ys0", [128, 1024], F32))
    P["ys1"] = es.enter_context(sb("ys1", [128, 1024], F32))


def load_ident(s, P, idf_dram):
    t1 = s.op("sync", lambda q: q.dma_start(out=P["id_f32"][:], in_=idf_dram[:]),
              dma=True)
    t2 = s.op("gpsimd", lambda q: q.dma_start(out=P["id_bf"][:], in_=idf_dram[:]),
              dma=True)
    P["t_idf"] = t1
    P["t_idbf"] = t2
    return [t1, t2]


def emit_router(s, P, kdep, t_wr, t_rep, t_he):
    lnt = P["lnt"]
    pL = P["pD"][0]
    pA = P["pR"]
    comb_done = []
    t_prev_tp = t_he
    for qt in range(8):
        t_mm = None
        for k in range(32):
            t_mm = s.op("tensor", lambda q, k=k, qt=qt: q.matmul(
                pL[:, 0:8],
                lnt[:, k * PER + qt * 128:k * PER + (qt + 1) * 128],
                P["wr_sb"][:, k * 8:(k + 1) * 8],
                start=(k == 0), stop=(k == 31)),
                deps=[t_wr, comb_done[-1] if comb_done else None] + kdep[k])
        c = P["rt"]
        m1n = c[:, 0:1]
        d = s.op("vector", lambda q: q.reduce_max(m1n, pL[:, 0:8], axis=AX.X,
                                                  negate=True), deps=[t_mm])
        ee = P["comb"][:, 0:8]
        d = s.op("scalar", lambda q, ee=ee: q.activation(
            out=ee, in_=pL[:, 0:8], func=AF.Exp, bias=m1n), deps=[d])
        m1e, mask1, keep = c[:, 1:2], c[:, 2:10], c[:, 10:18]
        p2m, m2, den, rec = c[:, 18:26], c[:, 26:27], c[:, 27:28], c[:, 28:29]
        d = s.op("vector", lambda q, ee=ee: q.reduce_max(m1e, ee, axis=AX.X),
                 deps=[d])
        d = ts_op(s, mask1, ee, m1e, None, ALU.is_ge, deps=[d])
        d = ts_op(s, keep, mask1, -1.0, 1.0, ALU.mult, ALU.add, deps=[d])
        d = tt_op(s, p2m, ee, keep, ALU.mult, deps=[d])
        d = s.op("vector", lambda q: q.reduce_max(m2, p2m, axis=AX.X), deps=[d])
        d = ts_op(s, mask1, ee, m2, None, ALU.is_ge, deps=[d])
        d = tt_op(s, p2m, ee, mask1, ALU.mult, deps=[d])
        d2 = tt_op(s, den, m1e, m2, ALU.add, deps=[d])
        d2 = s.op("vector", lambda q: q.reciprocal(rec, den), deps=[d2])
        d2 = ts_op(s, rec, rec, 2.0, None, ALU.mult, deps=[d2])
        d2 = ts_op(s, p2m, p2m, rec, None, ALU.mult, deps=[d2])
        t_tp = s.op("tensor", lambda q, qt=qt: q.transpose(
            pA[0:8, qt * 128:(qt + 1) * 128], p2m, P["id_f32"][:]),
            deps=[d2, t_prev_tp, P.get("t_idf")])
        t_prev_tp = t_tp
        comb_done.append(t_tp)
    t_ct = s.op("vector", lambda q: q.tensor_copy(P["combT"][:], pA[0:8, 0:1024]),
                deps=comb_done)
    t_ex = []
    for h in range(2):
        t_ex.append(s.op("tensor", lambda q, h=h: q.matmul(
            pA[:, h * 512:(h + 1) * 512], P["rep_sb"][:],
            P["combT"][:, h * 512:(h + 1) * 512], start=True, stop=True),
            deps=[t_ct, t_rep]))
    t_c128 = s.op("vector", lambda q: q.tensor_copy(P["comb128"][:], pA[:, 0:1024]),
                  deps=t_ex)
    t_hw = s.op("vector", lambda q: q.tensor_tensor(
        P["hw"][:], P["hfeat"][:], P["comb128"][:], op=ALU.mult),
        deps=[t_c128, t_he])
    return t_hw, t_c128


def build_ffn(kind, core, dbg=False):
    nc = bass.Bass("TRN2", target_bir_lowering=False)
    x = nc.dram_tensor("x", [PER, 4096], F32, kind="ExternalInput").ap()
    wt = nc.dram_tensor("wt", [32, 128, 4096], F32, kind="ExternalInput").ap()
    bias_t = nc.dram_tensor("bias_t", [128, 32], F32, kind="ExternalInput").ap()
    idf = nc.dram_tensor("ident", [128, 128], F32, kind="ExternalInput").ap()
    cvec = nc.dram_tensor("cvec", [128, 8], F32, kind="ExternalInput").ap()
    R = 48 if kind == "l0" else 128
    bvec = nc.dram_tensor("bvec", [128, 1], F32, kind="ExternalInput").ap()
    acat = nc.dram_tensor("acat", [4096, R], F32, kind="ExternalInput").ap()
    bcat = nc.dram_tensor("bcat", [R, 4096], F32, kind="ExternalInput").ap()
    if kind == "l0":
        lsi = nc.dram_tensor("lsi", [128, 4], I32, kind="ExternalInput").ap()
        sel = nc.dram_tensor("sel", [3, 48], F32, kind="ExternalInput").ap()
    else:
        wr = nc.dram_tensor("wr", [4096, 8], F32, kind="ExternalInput").ap()
        rep = nc.dram_tensor("rep", [8, 128], F32, kind="ExternalInput").ap()
    y = nc.dram_tensor("y", [PER, 4096], F32, kind="ExternalOutput").ap()
    if dbg:
        lnt_d = nc.dram_tensor("lnt_d", [128, 32 * PER], BF16,
                               kind="ExternalOutput").ap()
        sc_d = nc.dram_tensor("sc_d", [128, 32], F32, kind="ExternalOutput").ap()
        wm_d = nc.dram_tensor("wm_d", [48, 1], F32, kind="ExternalOutput").ap()

    with ExitStack() as es:
        P = {}
        sb = nc.sbuf_tensor
        alloc_common(nc, es, P)
        P["wa"] = es.enter_context(sb("wa", [128, 4096], BF16))
        P["wb"] = es.enter_context(sb("wb", [128, 4096], BF16))
        P["bias_sb"] = es.enter_context(sb("bias_sb", [128, 32], F32))
        P["acat_sb"] = es.enter_context(sb("acat_sb", [128, 32 * R], BF16))
        P["bcat_sb"] = es.enter_context(sb("bcat_sb", [R, 4096], BF16))
        P["hfeat"] = es.enter_context(sb("hfeat", [R, 1024], BF16))
        P["dfeat"] = es.enter_context(sb("dfeat", [128, 8 * 1024], F32))
        P["cvec_sb"] = es.enter_context(sb("cvec_sb", [128, 8], F32))
        if kind == "l0":
            P["bvec_sb"] = es.enter_context(sb("bvec_sb", [128, 1], F32))
            P["sc"] = es.enter_context(sb("sc", [128, 32], F32))
            P["lsi_i"] = es.enter_context(sb("lsi_i", [128, 4], I32))
            P["wm3"] = es.enter_context(sb("wm3", [128, 3], BF16))
            P["wm3t"] = es.enter_context(sb("wm3t", [3, 1], BF16))
            P["wm48"] = es.enter_context(sb("wm48", [48, 1], F32))
            P["ones11"] = es.enter_context(sb("ones11", [1, 1], BF16))
            P["sel_sb"] = es.enter_context(sb("sel_sb", [3, 48], BF16))
            P["bcat_sc"] = es.enter_context(sb("bcat_sc", [R, 4096], BF16))
        else:
            P["wr_sb"] = es.enter_context(sb("wr_sb", [128, 32 * 8], BF16))
            P["rep_sb"] = es.enter_context(sb("rep_sb", [8, 128], BF16))
            P["rt"] = es.enter_context(sb("rt", [128, 32], F32))
            P["comb"] = es.enter_context(sb("comb", [128, 8], F32))
            P["combT"] = es.enter_context(sb("combT", [8, 1024], BF16))
            P["comb128"] = es.enter_context(sb("comb128", [128, 1024], BF16))
            P["hw"] = es.enter_context(sb("hw", [128, 1024], BF16))
        block = es.enter_context(nc.Block())
        s = Sched(nc, es, block)

        t_id = load_ident(s, P, idf)
        t_cvec = s.op("sync", lambda q: q.dma_start(out=P["cvec_sb"][:], in_=cvec[:]),
                      dma=True)
        P["t_cvec"] = t_cvec
        t_b = s.op("sync", lambda q: q.dma_start(out=P["bias_sb"][:], in_=bias_t[:]),
                   dma=True)
        acat3 = acat.rearrange("(k p) r -> p k r", p=128)
        t_a = s.op("gpsimd", lambda q: q.dma_start(
            out=P["acat_sb"][:].rearrange("p (k r) -> p k r", r=R), in_=acat3),
            dma=True)
        t_bc = s.op("gpsimd", lambda q: q.dma_start(out=P["bcat_sb"][:], in_=bcat[:]),
                    dma=True)
        if kind == "l0":
            t_sel = s.op("gpsimd", lambda q: q.dma_start(out=P["sel_sb"][:],
                                                         in_=sel[:]), dma=True)
            P["t_bvec"] = s.op("sync", lambda q: q.dma_start(
                out=P["bvec_sb"][:], in_=bvec[:]), dma=True)
        else:
            wr3 = wr.rearrange("(k p) r -> p k r", p=128)
            t_wr = s.op("gpsimd", lambda q: q.dma_start(
                out=P["wr_sb"][:].rearrange("p (k r) -> p k r", r=8), in_=wr3),
                dma=True)
            t_rep = s.op("gpsimd", lambda q: q.dma_start(out=P["rep_sb"][:],
                                                         in_=rep[:]), dma=True)

        kdep = emit_ln(s, P, x)
        t_lntd = None
        if dbg:
            alldep = [kdep[k][qt] for k in range(32) for qt in range(8)]
            t_lntd = s.op("sync", lambda q: q.dma_start(out=lnt_d[:],
                          in_=P["lnt"][:]), deps=alldep, dma=True)

        ph = P["pR"][0:R, :]
        t_h_toks = []
        for h in range(2):
            last = None
            for k in range(32):
                last = s.op("tensor", lambda q, k=k, h=h: q.matmul(
                    ph[:, h * 512:(h + 1) * 512],
                    P["acat_sb"][:, k * R:(k + 1) * R],
                    P["lnt"][:, k * PER + h * 512:k * PER + (h + 1) * 512],
                    start=(k == 0), stop=(k == 31)),
                    deps=[t_a, t_id[1]] + kdep[k])
            t_h_toks.append(last)
        t_he = s.op("vector", lambda q: q.tensor_copy(P["hfeat"][:], ph),
                    deps=t_h_toks)

        if kind == "l0":
            t_wm, wm48 = emit_searchsorted_wm(s, P, lsi, P["bvec_sb"][:], t_he)
            if dbg:
                t_d1 = s.op("sync", lambda q: q.dma_start(out=sc_d[:],
                            in_=P["sc"][:]), deps=[t_wm], dma=True)
                t_d2 = s.op("sync", lambda q: q.dma_start(out=wm_d[:],
                            in_=P["wm48"][:]), deps=[t_wm], dma=True)
            t_lhs = s.op("vector", lambda q: q.tensor_scalar(
                P["bcat_sc"][:], P["bcat_sb"][:], wm48[:], None, op0=ALU.mult),
                deps=[t_bc, t_wm])
            lora_lhs, lora_rhs = P["bcat_sc"], P["hfeat"]
            t_router_done = t_wm
        else:
            t_hw, t_router_done = emit_router(s, P, kdep, t_wr, t_rep, t_he)
            t_lhs = t_hw
            lora_lhs, lora_rhs = P["bcat_sb"], P["hw"]
            t_bc2 = t_bc

        store_toks = []
        t_wprev = [None, None]
        t_evac_prev = [None, None]
        dfeat_deps = [None] * 8
        prev_guard = [t_router_done]
        for m in range(32):
            wb_t = (P["wa"], P["wb"])[m % 2]
            t_w = s.op("gpsimd", lambda q, m=m, wb_t=wb_t: q.dma_start(
                out=wb_t[:], in_=wt[m, :, :]), deps=[t_wprev[m % 2]], dma=True)
            p = P["pD"][m % 2]
            half_toks = []
            for h in range(2):
                last = None
                for k in range(32):
                    last = s.op("tensor", lambda q, k=k, h=h, m=m, p=p, wb_t=wb_t:
                                q.matmul(
                        p[:, h * 512:(h + 1) * 512],
                        wb_t[:, k * 128:(k + 1) * 128],
                        P["lnt"][:, k * PER + h * 512:k * PER + (h + 1) * 512],
                        start=(k == 0), stop=False),
                        deps=[t_w, t_evac_prev[m % 2], t_router_done] + kdep[k])
                last = s.op("tensor", lambda q, h=h, m=m, p=p: q.matmul(
                    p[:, h * 512:(h + 1) * 512],
                    lora_lhs[:, m * 128:(m + 1) * 128],
                    lora_rhs[:, h * 512:(h + 1) * 512],
                    start=False, stop=True),
                    deps=[t_lhs, t_he, t_bc])
                half_toks.append(last)
            t_wprev[m % 2] = half_toks[-1]
            dst = P["dfeat"][:, (m % 8) * 1024:(m % 8 + 1) * 1024]
            t_e = s.op("vector", lambda q, p=p, dst=dst, m=m: q.tensor_scalar(
                dst, p[:], P["bias_sb"][:, m:m + 1], None, op0=ALU.add),
                deps=half_toks + [t_b] +
                     ([store_toks[-1]] if store_toks else []))
            t_evac_prev[m % 2] = t_e
            dfeat_deps[m % 8] = t_e
            if m % 8 == 7:
                sts = emit_residual_group(s, P, x, y, list(range(m - 7, m + 1)),
                                          list(dfeat_deps), prev_guard[0])
                store_toks.extend(sts)
                prev_guard[0] = sts[-1]
        s.finish([store_toks[-1]] + ([t_lntd] if dbg else []))
    return nc


_FFN_CACHE = {}


def run_ffn(kind, x_shards, wargs):
    in_maps = []
    for c in range(8):
        m = dict(wargs)
        m["x"] = x_shards[c]
        in_maps.append(m)
    for c in range(8):
        in_maps[c]["bvec"] = np.full((128, 1), float(c // 4), np.float32)
    nc = build_ffn(kind, 0)
    r = run_bass_kernel_spmd(nc, in_maps, core_ids=list(range(8)))
    return [r.results[c]["y"] for c in range(8)]


def prep_ffn_l0(inp):
    return {
        "wt": tile_w(np.asarray(inp["L0_ffn_weight"], np.float32)),
        "bias_t": bias_cols(np.asarray(inp["L0_ffn_bias"], np.float32)),
        "ident": np.eye(128, dtype=np.float32),
        "cvec": np.tile(np.arange(8, dtype=np.float32), (128, 1)),
        "acat": np.ascontiguousarray(
            np.asarray(inp["L0_ffn_lora_A"], np.float32).reshape(48, 4096).T),
        "bcat": np.ascontiguousarray(
            np.asarray(inp["L0_ffn_lora_B"], np.float32).transpose(0, 2, 1)
            .reshape(48, 4096)),
        "lsi": np.tile(np.asarray(inp["L0_ffn_lora_start_indices"], np.int32).reshape(1, 4), (128, 1)),
        "sel": np.ascontiguousarray(
            np.kron(np.eye(3), np.ones((1, 16))).astype(np.float32)),
    }


def prep_ffn_l1(inp):
    return {
        "wt": tile_w(np.asarray(inp["L1_ffn_W_base"], np.float32)),
        "bias_t": bias_cols(np.asarray(inp["L1_ffn_b_base"], np.float32)),
        "ident": np.eye(128, dtype=np.float32),
        "cvec": np.tile(np.arange(8, dtype=np.float32), (128, 1)),
        "acat": np.ascontiguousarray(
            np.asarray(inp["L1_ffn_A"], np.float32).reshape(128, 4096).T),
        "bcat": np.ascontiguousarray(
            np.asarray(inp["L1_ffn_B"], np.float32).transpose(0, 2, 1)
            .reshape(128, 4096)),
        "wr": np.ascontiguousarray(
            np.asarray(inp["L1_ffn_W_router"], np.float32).T),
        "rep": np.ascontiguousarray(
            np.kron(np.eye(8), np.ones((1, 16))).astype(np.float32)),
    }


def kernel(**inputs):
    inp = {k: np.asarray(v) for k, v in inputs.items()}
    x = np.asarray(inp["L0_attn_a_x"], np.float32)
    xs = shard_rows(x)
    ffn0 = prep_ffn_l0(inp)
    ffn1 = prep_ffn_l1(inp)
    xs = run_attn_l0a(xs, inp)
    xs = run_ffn("l0", xs, ffn0)
    xs = run_attn_l0b(xs, inp)
    xs = run_ffn("l0", xs, ffn0)
    xs = run_attn_l1a(xs, inp)
    xs = run_ffn("l1", xs, ffn1)
    xs = run_attn_l1b(xs, inp)
    xs = run_ffn("l1", xs, ffn1)
    return unshard_rows([np.asarray(s, np.float32) for s in xs])


def alloc_ln_bufs(nc, es, P, n_tok=PER):
    sb, ps = nc.sbuf_tensor, nc.psum_tensor
    P["xa"] = es.enter_context(sb("xa", [128, 4096], F32))
    P["xb"] = es.enter_context(sb("xb", [128, 4096], F32))
    P["lntok"] = es.enter_context(sb("lntok", [128, 4096], BF16))
    P["stats"] = es.enter_context(sb("stats", [128, 96], F32))
    P["mv"] = es.enter_context(sb("mv", [128, 8], F32))
    P["rstd"] = es.enter_context(sb("rstd", [128, 2], F32))
    P["eps"] = es.enter_context(sb("eps", [128, 1], F32))
    P["lnt"] = es.enter_context(sb("lnt", [128, 32 * n_tok], BF16))


def alloc_resid_bufs(nc, es, P, gsz=8):
    sb = nc.sbuf_tensor
    P["dfeat"] = es.enter_context(sb("dfeat", [128, gsz * 1024], F32))
    P["xs0"] = es.enter_context(sb("xs0", [128, 1024], F32))
    P["xs1"] = es.enter_context(sb("xs1", [128, 1024], F32))
    P["ys0"] = es.enter_context(sb("ys0", [128, 1024], F32))
    P["ys1"] = es.enter_context(sb("ys1", [128, 1024], F32))


def emit_proj(s, P, wt_dram, nm, nk, rhs_sb, rhs_deps, out_cb, wbufs,
              extra=None, n_tok=PER, first_deps=(), rhs_stride=None):
    if rhs_stride is None:
        rhs_stride = n_tok
    st8 = P.setdefault("proj_state", {"w": [None, None], "out": [None, None]})
    t_wprev = st8["w"]
    t_outprev = st8["out"]
    nh = n_tok // 512
    for m in range(nm):
        wb = wbufs[m % 2]
        t_w = s.op("gpsimd", lambda q, m=m, wb=wb: q.dma_start(
            out=wb[:, 0:nk * 128], in_=wt_dram[m, :, :]),
            deps=[t_wprev[m % 2]], dma=True)
        p = P["pD"][m % 2]
        toks = []
        for h in range(nh):
            last = None
            for ki in range(nk):
                last = s.op("tensor", lambda q, ki=ki, h=h, p=p, wb=wb: q.matmul(
                    p[:, h * 512:(h + 1) * 512],
                    wb[:, ki * 128:(ki + 1) * 128],
                    rhs_sb[:, ki * rhs_stride + h * 512:ki * rhs_stride + (h + 1) * 512],
                    start=(ki == 0), stop=(ki == nk - 1 and extra is None)),
                    deps=[t_w, t_outprev[m % 2]] + list(first_deps) + rhs_deps[ki])
            if extra is not None:
                last = extra(m, h, p, [last])
            toks.append(last)
        t_out = out_cb(m, p, toks)
        t_outprev[m % 2] = t_out
        t_wprev[m % 2] = toks[-1]


def emit_rope_inplace(s, P, hb, cos_t, sin_t, deps, half_rows=64, base=0,
                      width=1024):
    rt = P["rt4"]
    h = half_rows
    x1 = hb[base:base + h, :]
    x2 = hb[base + h:base + 2 * h, :]
    c1 = cos_t[0:h, 0:width]
    s1 = sin_t[0:h, 0:width]
    A = rt[0:h, 0:1024][:, 0:width]
    Bv = rt[0:h, 1024:2048][:, 0:width]
    Cv = rt[0:h, 2048:3072][:, 0:width]
    Dv = rt[0:h, 3072:4096][:, 0:width]
    if base == 0:
        z1 = x1
        tz1 = None
    else:
        z1 = rt[0:h, 2048:3072][:, 0:width]
        z1 = A
        tz1 = s.op("vector", lambda q: q.tensor_copy(z1, x1), deps=deps)
    z2 = Bv
    tz2 = s.op("vector", lambda q: q.tensor_copy(z2, x2), deps=deps)
    pA = Cv
    pD = Dv
    d1 = tt_op(s, pA, z1, c1, ALU.mult, deps=[tz1, tz2] + list(deps))
    d4 = tt_op(s, pD, z1, s1, ALU.mult, deps=[d1])
    d2 = tt_op(s, z1 if base != 0 else A, z2, s1, ALU.mult, deps=[d4])
    x2sin = z1 if base != 0 else A
    d3 = tt_op(s, z2, z2, c1, ALU.mult, deps=[d2])
    e1 = tt_op(s, pA, pA, x2sin, ALU.subtract, deps=[d3])
    e2 = tt_op(s, z2, z2, pD, ALU.add, deps=[e1])
    f1 = s.op("vector", lambda q: q.tensor_copy(x1, pA), deps=[e2])
    f2 = s.op("vector", lambda q: q.tensor_copy(x2, z2), deps=[f1])
    return [f1, f2]


def emit_out_proj_residual(s, P, wo_dram, bias_sb, src_sb, src_deps,
                           x_dram, y_dram, wbufs, extra_out=None, gsz=8):
    store_toks = []
    t_wprev = [None, None]
    t_evac_prev = [None, None]
    dfeat_deps = [None] * gsz
    prev_guard = [None]
    for m in range(32):
        wb = wbufs[m % 2]
        t_w = s.op("gpsimd", lambda q, m=m, wb=wb: q.dma_start(
            out=wb[:], in_=wo_dram[m, :, :]), deps=[t_wprev[m % 2]], dma=True)
        p = P["pD"][m % 2]
        half_toks = []
        for h in range(2):
            last = None
            for k in range(32):
                last = s.op("tensor", lambda q, k=k, h=h, p=p, wb=wb: q.matmul(
                    p[:, h * 512:(h + 1) * 512],
                    wb[:, k * 128:(k + 1) * 128],
                    src_sb[:, k * PER + h * 512:k * PER + (h + 1) * 512],
                    start=(k == 0), stop=True if k == 31 else False),
                    deps=[t_w, t_evac_prev[m % 2]] + src_deps[k])
            half_toks.append(last)
        t_wprev[m % 2] = half_toks[-1]
        dst = P["dfeat"][:, (m % gsz) * 1024:(m % gsz + 1) * 1024]
        if bias_sb is not None:
            t_e = s.op("vector", lambda q, p=p, dst=dst, m=m: q.tensor_scalar(
                dst, p[:], bias_sb[:, m:m + 1], None, op0=ALU.add),
                deps=half_toks + ([store_toks[-1]] if store_toks else []))
        else:
            t_e = s.op("vector", lambda q, p=p, dst=dst: q.tensor_copy(dst, p[:]),
                       deps=half_toks + ([store_toks[-1]] if store_toks else []))
        t_evac_prev[m % 2] = t_e
        dfeat_deps[m % gsz] = t_e
        if m % gsz == gsz - 1:
            sts = emit_residual_group(s, P, x_dram, y_dram,
                                      list(range(m - gsz + 1, m + 1)),
                                      list(dfeat_deps), prev_guard[0])
            store_toks.extend(sts)
            prev_guard[0] = sts[-1]
    return store_toks[-1]


def emit_flash_half(s, P, F, h, st, smm_cb, vbuf, v_dep, ctx_dst,
                    scale, n_q=512):
    nfull = 16 * st
    kcs = list(range(nfull + 16))
    sc = P["fsc"]
    Crun = sc[:, 0:1]
    csb = sc[:, 1:2]
    cnew = sc[:, 2:3]
    aold = sc[:, 3:4]
    anew = sc[:, 4:5]
    mcol = sc[:, 5:9]
    dsc = P["fsd"]
    aoldd, anewd = dsc[0:1, 0:1], dsc[0:1, 1:2]
    ctx_sb = P["ctx_sb"]
    den_sb = P["den_sb"]
    t0 = s.op("vector", lambda q: q.memset(ctx_sb[:], 0.0), deps=[P["t_ctxfree"][0]])
    t0b = s.op("vector", lambda q: q.memset(den_sb[:], 0.0), deps=[t0])
    t0c = s.op("vector", lambda q: q.memset(Crun, -1e30), deps=[t0b])
    t_exp_done = {0: P["t_psfree"][0], 1: P["t_psfree"][1]}
    t_comb_prev = [t0c]
    t_pv_prev = [P["t_pcfree"][0]]
    t_den_prev = [P["t_pdenfree"][0]]
    rounds = [kcs[i:i + 3] for i in range(0, len(kcs), 3)]
    for ri, rk in enumerate(rounds):
        par = ri % 2
        pS = P["pS"][par]
        Eb = P["Eb"][par]
        q0r = max(0, 32 * (rk[0] - nfull))
        t_mx = []
        t_ss = []
        for slot, kc in enumerate(rk):
            q0 = max(0, 32 * (kc - nfull))
            out_ap = pS[:, slot * 512 + q0:slot * 512 + 512]
            t_s = smm_cb(kc, out_ap, q0, [t_exp_done[par]])
            if kc >= nfull:
                t_s = s.op("vector", lambda q, q0=q0, slot=slot:
                           q.tensor_tensor(
                    pS[:, slot * 512 + q0:slot * 512 + q0 + 32],
                    pS[:, slot * 512 + q0:slot * 512 + q0 + 32],
                    P["dmask"][:], op=ALU.add), deps=[t_s])
            t_ss.append(t_s)
            t_mx.append(s.op("vector", lambda q, out_ap=out_ap, slot=slot:
                             q.reduce_max(mcol[:, slot:slot + 1], out_ap,
                                          axis=AX.X), deps=[t_s]))
        t_c = s.op("vector", lambda q, n=len(rk): q.reduce_max(
            mcol[:, 3:4], mcol[:, 0:n], axis=AX.X), deps=t_mx)
        t_tp = s.op("tensor", lambda q: q.transpose(
            P["pCt"][0:1, 0:128], mcol[:, 3:4], P["id_f32"][:]),
            deps=[t_c, t_pv_prev[0], t_comb_prev[0], P.get("t_idf")])
        t_cf = s.op("vector", lambda q: q.reduce_max(
            dsc[0:1, 2:3], P["pCt"][0:1, 0:128], axis=AX.X), deps=[t_tp])
        t_cb = s.op("vector", lambda q: q.tensor_copy(P["cbf"][:], dsc[0:1, 2:3]),
                    deps=[t_cf])
        t_nb = s.op("tensor", lambda q: q.matmul(
            P["pCb"][:, 0:1], P["negscale_row"][:], P["cbf"][:],
            start=True, stop=True), deps=[t_cb])
        t_nbs = s.op("vector", lambda q: q.tensor_copy(P["nb_sb"][:],
                                                       P["pCb"][:, 0:1]),
                     deps=[t_nb])
        t_bc = _bcast128(s, P, dsc[0:1, 2:3], sc[:, 9:10], [t_cf])
        t_cn = s.op("vector", lambda q: q.tensor_tensor(
            cnew, Crun, sc[:, 9:10], op=ALU.max), deps=[t_cf, t_bc])
        t_d1 = tt_op(s, sc[:, 10:11], Crun, cnew, ALU.subtract, deps=[t_cn])
        t_a1 = s.op("scalar", lambda q: q.activation(
            out=aold, in_=sc[:, 10:11], func=AF.Exp, scale=scale), deps=[t_d1])
        t_d2 = tt_op(s, sc[:, 11:12], sc[:, 9:10], cnew, ALU.subtract, deps=[t_cn])
        t_a2 = s.op("scalar", lambda q: q.activation(
            out=anew, in_=sc[:, 11:12], func=AF.Exp, scale=scale), deps=[t_d2])
        t_cr = s.op("vector", lambda q: q.tensor_copy(Crun, cnew), deps=[t_a1, t_a2])
        t_exps = []
        for slot, kc in enumerate(rk):
            q0 = max(0, 32 * (kc - nfull))
            t_exps.append(s.op("scalar", lambda q, slot=slot, q0=q0: q.activation(
                out=Eb[:, slot * 512 + q0:slot * 512 + 512],
                in_=pS[:, slot * 512 + q0:slot * 512 + 512],
                func=AF.Exp, scale=scale, bias=P["nb_sb"][:]),
                deps=[t_nbs] + t_mx))
        t_exp_done[par] = t_exps[-1]
        t_den = None
        t_pv = None
        for slot, kc in enumerate(rk):
            q0 = max(0, 32 * (kc - nfull))
            t_den = s.op("tensor", lambda q, slot=slot, q0=q0, first=(slot == 0):
                         q.matmul(
                P["pDen"][0:1, q0:512], P["onescol"][:],
                Eb[:, slot * 512 + q0:slot * 512 + 512],
                start=first, stop=(slot == len(rk) - 1)),
                deps=[t_exps[slot], t_den_prev[0]])
            t_pv = s.op("tensor", lambda q, slot=slot, kc=kc, q0=q0,
                        first=(slot == 0): q.matmul(
                P["pC"][:, q0:512], vbuf[:, kc * 128:(kc + 1) * 128],
                Eb[:, slot * 512 + q0:slot * 512 + 512],
                start=first, stop=(slot == len(rk) - 1)),
                deps=[t_exps[slot], v_dep, t_pv_prev[0], t_nbs, t_bc])
        t_m1 = s.op("vector", lambda q: q.tensor_scalar(
            ctx_sb[:], ctx_sb[:], aold, None, op0=ALU.mult),
            deps=[t_cr, t_a1])
        t_m2 = s.op("vector", lambda q, q0r=q0r: q.scalar_tensor_tensor(
            ctx_sb[:, q0r:512], P["pC"][:, q0r:512], anew, ctx_sb[:, q0r:512],
            op0=ALU.mult, op1=ALU.add), deps=[t_m1, t_pv, t_a2])
        t_ad = _sc_to_p0(s, P, aold, aoldd, [t_a1])
        t_an = _sc_to_p0(s, P, anew, anewd, [t_a2])
        t_m3 = s.op("vector", lambda q: q.tensor_scalar(
            den_sb[:], den_sb[:], aoldd, None, op0=ALU.mult), deps=[t_ad, t_m2])
        t_m4 = s.op("vector", lambda q, q0r=q0r: q.scalar_tensor_tensor(
            den_sb[0:1, q0r:512], P["pDen"][0:1, q0r:512], anewd,
            den_sb[0:1, q0r:512], op0=ALU.mult, op1=ALU.add),
            deps=[t_m3, t_den, t_an])
        t_comb_prev = [t_m2]
        t_pv_prev = [t_m2]
        t_den_prev = [t_m4]
    t_l = s.op("scalar", lambda q: q.activation(
        out=dsc[0:1, 3:4].broadcast_to((1, 1)) if False else P["dlog"][0:1, 0:512],
        in_=den_sb[:], func=AF.Ln), deps=[t_den_prev[0], t_comb_prev[0]])
    t_r = s.op("scalar", lambda q: q.activation(
        out=P["dlog"][0:1, 512:1024], in_=P["dlog"][0:1, 0:512],
        func=AF.Exp, scale=-1.0), deps=[t_l])
    t_rb = s.op("vector", lambda q: q.tensor_copy(P["recip_bf"][:],
                                                  P["dlog"][0:1, 512:1024]),
                deps=[t_r])
    t_ex = s.op("tensor", lambda q: q.matmul(
        P["pS"][0][:, 0:512], P["ones_row"][:], P["recip_bf"][:],
        start=True, stop=True), deps=[t_rb, t_exp_done[0]])
    t_fin = s.op("vector", lambda q: q.tensor_tensor(
        ctx_dst, ctx_sb[:], P["pS"][0][:, 0:512], op=ALU.mult),
        deps=[t_ex, t_comb_prev[0]])
    P["t_psfree"][0] = t_fin
    P["t_psfree"][1] = t_exp_done[1]
    P["t_pcfree"][0] = t_comb_prev[0]
    P["t_pdenfree"][0] = t_den_prev[0]
    P["t_ctxfree"][0] = t_fin
    return t_fin


def _bcast128(s, P, src11, dst1281, deps):
    t1 = s.op("vector", lambda q: q.tensor_copy(P["cbf2"][:], src11), deps=deps)
    t2 = s.op("tensor", lambda q: q.matmul(P["pCb"][:, 1:2], P["ones_row"][:],
                                           P["cbf2"][:], start=True, stop=True),
              deps=[t1])
    t3 = s.op("vector", lambda q: q.tensor_copy(dst1281, P["pCb"][:, 1:2]),
              deps=[t2])
    return t3


def _sc_to_p0(s, P, src1281, dst11, deps):
    return s.op("vector", lambda q: q.tensor_copy(dst11, src1281[0:1, :]),
                deps=deps)


def build_attn_proj_l1b():
    nc = bass.Bass("TRN2", target_bir_lowering=False)
    x = nc.dram_tensor("x", [PER, 4096], F32, kind="ExternalInput").ap()
    wt = nc.dram_tensor("wt", [96, 128, 4096], F32, kind="ExternalInput").ap()
    bias_t = nc.dram_tensor("bias_t", [128, 96], F32, kind="ExternalInput").ap()
    idf = nc.dram_tensor("ident", [128, 128], F32, kind="ExternalInput").ap()
    cosf = nc.dram_tensor("cosf", [64, 1024], F32, kind="ExternalInput").ap()
    sinf = nc.dram_tensor("sinf", [64, 1024], F32, kind="ExternalInput").ap()
    outs = {}
    for nm in ("q_sh", "k_sh", "v_sh"):
        outs[nm] = nc.dram_tensor(nm, [128, 32 * 1024], BF16,
                                  kind="ExternalOutput").ap()
    with ExitStack() as es:
        P = {}
        sb, ps = nc.sbuf_tensor, nc.psum_tensor
        alloc_ln_bufs(nc, es, P)
        P["id_bf"] = es.enter_context(sb("id_bf", [128, 128], BF16))
        P["id_f32"] = es.enter_context(sb("id_f32", [128, 128], F32))
        P["pLN"] = [es.enter_context(ps(f"pLN{i}", [128, 1024], BF16))
                    for i in range(2)]
        P["pD"] = [es.enter_context(ps(f"pD{i}", [128, 1024], F32))
                   for i in range(2)]
        P["wa"] = es.enter_context(sb("wa", [128, 4096], BF16))
        P["wb"] = es.enter_context(sb("wb", [128, 4096], BF16))
        P["bias_sb"] = es.enter_context(sb("bias_sb", [128, 96], F32))
        P["hb0"] = es.enter_context(sb("hb0", [128, 1024], F32))
        P["hb1"] = es.enter_context(sb("hb1", [128, 1024], F32))
        P["hc0"] = es.enter_context(sb("hc0", [128, 1024], BF16))
        P["hc1"] = es.enter_context(sb("hc1", [128, 1024], BF16))
        P["rt4"] = es.enter_context(sb("rt4", [64, 4096], F32))
        P["cosf"] = es.enter_context(sb("cosf_sb", [64, 1024], F32))
        P["sinf"] = es.enter_context(sb("sinf_sb", [64, 1024], F32))
        block = es.enter_context(nc.Block())
        s = Sched(nc, es, block)
        t_id = load_ident(s, P, idf)
        t_b = s.op("sync", lambda q: q.dma_start(out=P["bias_sb"][:], in_=bias_t[:]),
                   dma=True)
        t_cos = s.op("sync", lambda q: q.dma_start(out=P["cosf"][:], in_=cosf[:]),
                     dma=True)
        t_sin = s.op("sync", lambda q: q.dma_start(out=P["sinf"][:], in_=sinf[:]),
                     dma=True)
        kdep = emit_ln(s, P, x)
        state = {"dma": [None, None], "rope": [None, None]}

        def out_cb(m, p, toks):
            par = m % 2
            hb = (P["hb0"], P["hb1"])[par]
            hc = (P["hc0"], P["hc1"])[par]
            t_e = s.op("vector", lambda q, p=p, hb=hb, m=m: q.tensor_scalar(
                hb[:], p[:], P["bias_sb"][:, m:m + 1], None, op0=ALU.add),
                deps=toks + [t_b, state["dma"][par]])
            deps = [t_e, t_cos, t_sin]
            if m < 64:
                deps = emit_rope_inplace(s, P, hb, P["cosf"][:], P["sinf"][:],
                                         deps + [state["rope"][1 - par]])
                state["rope"][par] = deps[-1]
            t_c = s.op("vector", lambda q, hc=hc, hb=hb: q.tensor_copy(hc[:], hb[:]),
                       deps=deps)
            dst = (outs["q_sh"] if m < 32 else outs["k_sh"] if m < 64
                   else outs["v_sh"])
            t_d = s.op("sync", lambda q, dst=dst, hc=hc, m=m: q.dma_start(
                out=dst[:, (m % 32) * 1024:(m % 32 + 1) * 1024], in_=hc[:]),
                deps=[t_c], dma=True)
            state["dma"][par] = t_d
            return t_e
        emit_proj(s, P, wt, 96, 32, P["lnt"], kdep, out_cb,
                  (P["wa"], P["wb"]), first_deps=[t_id[1]])
        s.finish([state["dma"][0], state["dma"][1]])
    return nc


def alloc_flash_psum(nc, es2, P):
    ps = nc.psum_tensor
    P["pM1"] = es2.enter_context(ps("pM1", [128, 1024], F32))
    P["pS2"] = [es2.enter_context(ps(f"pS2{i}", [128, 1024], F32))
                for i in range(2)]
    P["pCtx"] = es2.enter_context(ps("pCtx", [128, 512], F32))
    P["pDen"] = es2.enter_context(ps("pDen", [128, 512], F32))
    P["pCt"] = P["pM1"][0:1, 0:128]
    P["pCb"] = P["pM1"][:, 256:264]


def alloc_flash_bufs(nc, es2, P):
    sb = nc.sbuf_tensor
    P["Eb"] = [es2.enter_context(sb(f"Eb{i}", [128, 1024], BF16)) for i in range(2)]
    P["fsc"] = es2.enter_context(sb("fsc", [128, 96], F32))
    P["fsd"] = es2.enter_context(sb("fsd", [1, 16], F32))
    P["cbf"] = es2.enter_context(sb("cbf", [1, 2], BF16))
    P["cbf2"] = es2.enter_context(sb("cbf2", [1, 1], BF16))
    P["nb_sb"] = es2.enter_context(sb("nb_sb", [128, 2], F32))
    P["dlog"] = es2.enter_context(sb("dlog", [1, 1024], F32))
    P["recip_bf"] = es2.enter_context(sb("recip_bf", [1, 512], BF16))
    P["ones_row"] = es2.enter_context(sb("ones_row", [1, 128], BF16))
    P["negscale_row"] = es2.enter_context(sb("negscale_row", [1, 128], BF16))
    P["onescol"] = es2.enter_context(sb("onescol", [128, 1], BF16))
    P["dmask"] = es2.enter_context(sb("dmask", [128, 32], F32))
    P["cfin"] = es2.enter_context(sb("cfin", [128, 512], F32))
    P["funit"] = [0]
    P["t_pm1free"] = {0: None, 1: None}
    P["t_exp_free"] = {}
    P["t_ctxfree"] = [None]
    P["t_denfree"] = [None]


def emit_flash_half(s, P, F, h, st, smm_cb, vbuf, v_dep, ctx_dst, scale,
                    n_q=512, causal=True, pv_fn=None, fin_fn=None):
    if causal:
        nfull = 16 * st
        kcs = list(range(nfull + 16))
    else:
        nfull = 10 ** 9
        kcs = list(range(32))
    K = len(kcs)
    u = P["funit"][0] & 1
    P["funit"][0] += 1
    sc = P["fsc"]
    base = u * 40
    mcol = sc[:, base:base + 32]
    cmax = sc[:, base + 32:base + 33]
    cb128 = sc[:, base + 33:base + 34]
    nb = P["nb_sb"][:, u:u + 1]
    dsc = P["fsd"]
    red_prev = {0: P["t_pm1free"][0], 1: P["t_pm1free"][1]}
    t_reds = []
    for i, kc in enumerate(kcs):
        sl = i % 2
        q0 = max(0, 32 * (kc - nfull))
        out_ap = P["pM1"][:, sl * 512 + q0:sl * 512 + 512]
        t_s = smm_cb(kc, out_ap, q0, [red_prev[sl]])
        if kc >= nfull:
            t_s = s.op("vector", lambda q, q0=q0, sl=sl: q.tensor_tensor(
                P["pM1"][:, sl * 512 + q0:sl * 512 + q0 + 32],
                P["pM1"][:, sl * 512 + q0:sl * 512 + q0 + 32],
                P["dmask"][:], op=ALU.add), deps=[t_s])
        t_red = s.op("vector", lambda q, i=i, out_ap=out_ap: q.reduce_max(
            mcol[:, i:i + 1], out_ap, axis=AX.X), deps=[t_s])
        red_prev[sl] = t_red
        t_reds.append(t_red)
    t_c = s.op("vector", lambda q, K=K: q.reduce_max(
        cmax, mcol[:, 0:K], axis=AX.X), deps=t_reds)
    t_tp = s.op("tensor", lambda q: q.transpose(
        P["pCt"][0:1, 0:128], cmax, P["id_f32"][:]),
        deps=[t_c, red_prev[0], red_prev[1], P.get("t_idf")])
    t_cf = s.op("vector", lambda q, u=u: q.reduce_max(
        dsc[0:1, 2 * u:2 * u + 1], P["pCt"][0:1, 0:128], axis=AX.X),
        deps=[t_tp])
    t_cb = s.op("vector", lambda q, u=u: q.tensor_copy(
        P["cbf"][0:1, u:u + 1], dsc[0:1, 2 * u:2 * u + 1]), deps=[t_cf])
    t_nb = s.op("tensor", lambda q, u=u: q.matmul(
        P["pCb"][:, u:u + 1], P["negscale_row"][:], P["cbf"][0:1, u:u + 1],
        start=True, stop=True), deps=[t_cb])
    t_nbs = s.op("vector", lambda q, u=u: q.tensor_copy(
        nb, P["pCb"][:, u:u + 1]), deps=[t_nb])
    P["t_pm1free"][0] = t_nbs
    P["t_pm1free"][1] = t_nbs
    t_exp_last = None
    t_den = None
    t_pv = None
    for i, kc in enumerate(kcs):
        par = (i // 2) % 2
        sl = i % 2
        q0 = max(0, 32 * (kc - nfull))
        pS = P["pS2"][par]
        Eb = P["Eb"][par]
        t_s = smm_cb(kc, pS[:, sl * 512 + q0:sl * 512 + 512], q0,
                     [P["t_exp_free"].get((par, sl))])
        if kc >= nfull:
            t_s = s.op("vector", lambda q, q0=q0, sl=sl, pS=pS: q.tensor_tensor(
                pS[:, sl * 512 + q0:sl * 512 + q0 + 32],
                pS[:, sl * 512 + q0:sl * 512 + q0 + 32],
                P["dmask"][:], op=ALU.add), deps=[t_s])
        t_exp = s.op("scalar", lambda q, q0=q0, sl=sl, pS=pS, Eb=Eb: q.activation(
            out=Eb[:, sl * 512 + q0:sl * 512 + 512],
            in_=pS[:, sl * 512 + q0:sl * 512 + 512],
            func=AF.Exp, scale=scale, bias=nb), deps=[t_s, t_nbs])
        P["t_exp_free"][(par, sl)] = t_exp
        t_exp_last = t_exp
        if pv_fn is None:
            t_den = s.op("tensor", lambda q, q0=q0, sl=sl, Eb=Eb,
                         first=(i == 0), last=(i == K - 1): q.matmul(
                P["pDen"][0:1, q0:512], P["onescol"][:],
                Eb[:, sl * 512 + q0:sl * 512 + 512],
                start=first, stop=last),
                deps=[t_exp, P["t_denfree"][0]])
            t_pv = s.op("tensor", lambda q, q0=q0, sl=sl, kc=kc, Eb=Eb,
                        first=(i == 0), last=(i == K - 1): q.matmul(
                P["pCtx"][:, q0:512], vbuf[:, kc * 128:(kc + 1) * 128],
                Eb[:, sl * 512 + q0:sl * 512 + 512],
                start=first, stop=last),
                deps=[t_exp, v_dep, P["t_ctxfree"][0]])
        else:
            t_pv = pv_fn(kc, q0, Eb[:, sl * 512 + q0:sl * 512 + 512],
                         i == 0, i == K - 1, [t_exp])
            t_den = t_pv
    if fin_fn is not None:
        t_fin = fin_fn(t_pv, t_exp_last)
        P["t_ctxfree"][0] = t_fin
        P["t_denfree"][0] = t_fin
        return t_fin
    t_l = s.op("scalar", lambda q: q.activation(
        out=P["dlog"][0:1, 0:512], in_=P["pDen"][0:1, 0:512], func=AF.Ln),
        deps=[t_den])
    t_r = s.op("scalar", lambda q: q.activation(
        out=P["dlog"][0:1, 512:1024], in_=P["dlog"][0:1, 0:512],
        func=AF.Exp, scale=-1.0), deps=[t_l])
    t_rb = s.op("vector", lambda q: q.tensor_copy(P["recip_bf"][:],
                                                  P["dlog"][0:1, 512:1024]),
                deps=[t_r])
    t_ex = s.op("tensor", lambda q: q.matmul(
        P["pDen"][:, 0:512], P["ones_row"][:], P["recip_bf"][:],
        start=True, stop=True), deps=[t_rb, t_l])
    t_cp = s.op("vector", lambda q: q.tensor_copy(P["cfin"][:], P["pCtx"][:, 0:512]),
                deps=[t_pv])
    t_fin = s.op("vector", lambda q: q.tensor_tensor(
        ctx_dst, P["cfin"][:], P["pDen"][:, 0:512], op=ALU.mult),
        deps=[t_ex, t_cp])
    P["t_ctxfree"][0] = t_cp
    P["t_denfree"][0] = t_fin
    return t_fin


def _bcast128(s, P, src11, dst1281, deps):
    t1 = s.op("vector", lambda q: q.tensor_copy(P["cbf2"][:], src11), deps=deps)
    t2 = s.op("tensor", lambda q: q.matmul(P["pCb"][:, 1:2], P["ones_row"][:],
                                           P["cbf2"][:], start=True, stop=True),
              deps=[t1])
    t3 = s.op("vector", lambda q: q.tensor_copy(dst1281, P["pCb"][:, 1:2]),
              deps=[t2])
    return t3


def _sc_to_p0(s, P, src1281, dst11, deps):
    return s.op("vector", lambda q: q.tensor_copy(dst11, src1281[0:1, :]),
                deps=deps)


def build_attn_proj_l1b():
    nc = bass.Bass("TRN2", target_bir_lowering=False)
    x = nc.dram_tensor("x", [PER, 4096], F32, kind="ExternalInput").ap()
    wt = nc.dram_tensor("wt", [96, 128, 4096], F32, kind="ExternalInput").ap()
    bias_t = nc.dram_tensor("bias_t", [128, 96], F32, kind="ExternalInput").ap()
    idf = nc.dram_tensor("ident", [128, 128], F32, kind="ExternalInput").ap()
    cosf = nc.dram_tensor("cosf", [64, 1024], F32, kind="ExternalInput").ap()
    sinf = nc.dram_tensor("sinf", [64, 1024], F32, kind="ExternalInput").ap()
    outs = {}
    for nm in ("q_sh", "k_sh", "v_sh"):
        outs[nm] = nc.dram_tensor(nm, [128, 32 * 1024], BF16,
                                  kind="ExternalOutput").ap()
    with ExitStack() as es:
        P = {}
        sb, ps = nc.sbuf_tensor, nc.psum_tensor
        alloc_ln_bufs(nc, es, P)
        P["id_bf"] = es.enter_context(sb("id_bf", [128, 128], BF16))
        P["id_f32"] = es.enter_context(sb("id_f32", [128, 128], F32))
        P["pLN"] = [es.enter_context(ps(f"pLN{i}", [128, 1024], BF16))
                    for i in range(2)]
        P["pD"] = [es.enter_context(ps(f"pD{i}", [128, 1024], F32))
                   for i in range(2)]
        P["wa"] = es.enter_context(sb("wa", [128, 4096], BF16))
        P["wb"] = es.enter_context(sb("wb", [128, 4096], BF16))
        P["bias_sb"] = es.enter_context(sb("bias_sb", [128, 96], F32))
        P["hb0"] = es.enter_context(sb("hb0", [128, 1024], F32))
        P["hb1"] = es.enter_context(sb("hb1", [128, 1024], F32))
        P["hc0"] = es.enter_context(sb("hc0", [128, 1024], BF16))
        P["hc1"] = es.enter_context(sb("hc1", [128, 1024], BF16))
        P["rt4"] = es.enter_context(sb("rt4", [64, 4096], F32))
        P["cosf"] = es.enter_context(sb("cosf_sb", [64, 1024], F32))
        P["sinf"] = es.enter_context(sb("sinf_sb", [64, 1024], F32))
        block = es.enter_context(nc.Block())
        s = Sched(nc, es, block)
        t_id = load_ident(s, P, idf)
        t_b = s.op("sync", lambda q: q.dma_start(out=P["bias_sb"][:], in_=bias_t[:]),
                   dma=True)
        t_cos = s.op("sync", lambda q: q.dma_start(out=P["cosf"][:], in_=cosf[:]),
                     dma=True)
        t_sin = s.op("sync", lambda q: q.dma_start(out=P["sinf"][:], in_=sinf[:]),
                     dma=True)
        kdep = emit_ln(s, P, x)
        state = {"dma": [None, None], "rope": [None, None]}

        def out_cb(m, p, toks):
            par = m % 2
            hb = (P["hb0"], P["hb1"])[par]
            hc = (P["hc0"], P["hc1"])[par]
            t_e = s.op("vector", lambda q, p=p, hb=hb, m=m: q.tensor_scalar(
                hb[:], p[:], P["bias_sb"][:, m:m + 1], None, op0=ALU.add),
                deps=toks + [t_b, state["dma"][par]])
            deps = [t_e, t_cos, t_sin]
            if m < 64:
                deps = emit_rope_inplace(s, P, hb, P["cosf"][:], P["sinf"][:],
                                         deps + [state["rope"][1 - par]])
                state["rope"][par] = deps[-1]
            t_c = s.op("vector", lambda q, hc=hc, hb=hb: q.tensor_copy(hc[:], hb[:]),
                       deps=deps)
            dst = (outs["q_sh"] if m < 32 else outs["k_sh"] if m < 64
                   else outs["v_sh"])
            t_d = s.op("sync", lambda q, dst=dst, hc=hc, m=m: q.dma_start(
                out=dst[:, (m % 32) * 1024:(m % 32 + 1) * 1024], in_=hc[:]),
                deps=[t_c], dma=True)
            state["dma"][par] = t_d
            return t_e
        emit_proj(s, P, wt, 96, 32, P["lnt"], kdep, out_cb,
                  (P["wa"], P["wb"]), first_deps=[t_id[1]])
        s.finish([state["dma"][0], state["dma"][1]])
    return nc


def build_attn_flash_mha(n_heads=32, group=1, scale=None,
                         with_bias=True, name="l1b", dbg=False):
    if scale is None:
        scale = 1.0 / float(np.sqrt(128.0))
    n_kv = n_heads // group
    KMAXB = 32
    nc = bass.Bass("TRN2", target_bir_lowering=False)
    x = nc.dram_tensor("x", [PER, 4096], F32, kind="ExternalInput").ap()
    q_sh = nc.dram_tensor("q_sh", [128, 32 * 1024], BF16, kind="ExternalInput").ap()
    kall = nc.dram_tensor("kall", [n_kv, 128, 4096], BF16, kind="ExternalInput").ap()
    vall = nc.dram_tensor("vall", [n_kv, 128, 4096], BF16, kind="ExternalInput").ap()
    wo = nc.dram_tensor("wo", [32, 128, 4096], F32, kind="ExternalInput").ap()
    bias_t = nc.dram_tensor("bias_t", [128, 32], F32, kind="ExternalInput").ap()
    tri = nc.dram_tensor("tri", [128, 32], F32, kind="ExternalInput").ap()
    idf = nc.dram_tensor("ident", [128, 128], F32, kind="ExternalInput").ap()
    y = nc.dram_tensor("y", [PER, 4096], F32, kind="ExternalOutput").ap()
    if dbg:
        ctx_d = nc.dram_tensor("ctx_d", [128, 32 * 1024], BF16,
                               kind="ExternalOutput").ap()
    with ExitStack() as es:
        P = {}
        sb, ps = nc.sbuf_tensor, nc.psum_tensor
        P["id_f32"] = es.enter_context(sb("id_f32", [128, 128], F32))
        P["ctx_all"] = es.enter_context(sb("ctx_all", [128, 32 * 1024], BF16))
        P["kbuf"] = [es.enter_context(sb(f"kbuf{i}", [128, KMAXB * 128], BF16))
                     for i in range(2)]
        P["vbuf"] = [es.enter_context(sb(f"vbuf{i}", [128, KMAXB * 128], BF16))
                     for i in range(2)]
        P["qbuf"] = [es.enter_context(sb(f"qbuf{i}", [128, 1024], BF16))
                     for i in range(2)]
        alloc_flash_bufs(nc, es, P)
        P["wa"] = es.enter_context(sb("wa", [128, 4096], BF16))
        P["wb"] = es.enter_context(sb("wb", [128, 4096], BF16))
        alloc_resid_bufs(nc, es, P)
        if with_bias:
            P["bias_sb"] = es.enter_context(sb("bias_sb", [128, 32], F32))
        block = es.enter_context(nc.Block())
        s = Sched(nc, es, block)
        t_id = s.op("sync", lambda q: q.dma_start(out=P["id_f32"][:], in_=idf[:]),
                    dma=True)
        P["t_idf"] = t_id
        ctx_done = [[None] * 2 for _ in range(n_heads)]
        with ExitStack() as es2:
            alloc_flash_psum(nc, es2, P)
            t_tri = s.op("sync", lambda q: q.dma_start(out=P["dmask"][:],
                                                       in_=tri[:]), dma=True)
            t_c1 = s.op("vector", lambda q: q.memset(P["ones_row"][:], 1.0))
            t_c2 = s.op("vector", lambda q: q.memset(P["negscale_row"][:], -scale))
            t_c3 = s.op("vector", lambda q: q.memset(P["onescol"][:], 1.0))
            P["t_ctxfree"] = [t_c1]
            P["t_denfree"] = [t_c3]
            P["t_pm1free"] = {0: t_c2, 1: t_c2}
            P["t_exp_free"] = {}
            kv_loaded = {}
            t_kv_prev = [None, None]
            t_q_prev = [None, None]
            for hh in range(n_heads):
                g = hh // group
                gpar = g % 2
                if g not in kv_loaded:
                    t_k = s.op("sync", lambda q, g=g, gpar=gpar: q.dma_start(
                        out=P["kbuf"][gpar][:], in_=kall[g, :, 0:KMAXB * 128]),
                        deps=[t_kv_prev[gpar]], dma=True)
                    t_v = s.op("gpsimd", lambda q, g=g, gpar=gpar: q.dma_start(
                        out=P["vbuf"][gpar][:], in_=vall[g, :, 0:KMAXB * 128]),
                        deps=[t_kv_prev[gpar]], dma=True)
                    kv_loaded[g] = (t_k, t_v)
                t_k, t_v = kv_loaded[g]
                hpar = hh % 2
                t_q = s.op("sync", lambda q, hh=hh, hpar=hpar: q.dma_start(
                    out=P["qbuf"][hpar][:],
                    in_=q_sh[:, hh * 1024:(hh + 1) * 1024]),
                    deps=[t_q_prev[hpar]], dma=True)
                kbuf, vbuf, qbuf = P["kbuf"][gpar], P["vbuf"][gpar], P["qbuf"][hpar]
                for st in range(2):

                    def smm(kc, out_ap, q0, deps, qbuf=qbuf, kbuf=kbuf,
                            st=st, t_k=t_k, t_q=t_q, t_tri=t_tri):
                        return s.op("tensor", lambda q: q.matmul(
                            out_ap, kbuf[:, kc * 128:(kc + 1) * 128],
                            qbuf[:, st * 512 + q0:st * 512 + 512],
                            start=True, stop=True),
                            deps=deps + [t_k, t_q, t_tri])
                    ctx_dst = P["ctx_all"][:, hh * 1024 + st * 512:
                                           hh * 1024 + (st + 1) * 512]
                    t_fin = emit_flash_half(s, P, None, hh, st, smm,
                                            vbuf, t_v, ctx_dst, scale)
                    ctx_done[hh][st] = t_fin
                t_kv_prev[gpar] = ctx_done[hh][1]
                t_q_prev[hpar] = ctx_done[hh][1]
        P["pD"] = [es.enter_context(ps(f"pD{i}", [128, 1024], F32))
                   for i in range(2)]
        P["pR"] = es.enter_context(ps("pR", [128, 1024], F32))
        bias_sb = None
        if with_bias:
            t_b = s.op("sync", lambda q: q.dma_start(out=P["bias_sb"][:],
                                                     in_=bias_t[:]), dma=True)
            bias_sb = P["bias_sb"]
        src_deps = {k: [ctx_done[hh][hf] for hh in range(n_heads)
                        for hf in range(2)] if k == 0 else
                    [ctx_done[k][0], ctx_done[k][1]] for k in range(32)}
        t_ctxd = None
        if dbg:
            t_ctxd = s.op("sync", lambda q: q.dma_start(
                out=ctx_d[:], in_=P["ctx_all"][:]),
                deps=[ctx_done[hh][hf] for hh in range(n_heads)
                      for hf in range(2)], dma=True)
        t_last = emit_out_proj_residual(s, P, wo, bias_sb, P["ctx_all"],
                                        src_deps, x, y, (P["wa"], P["wb"]))
        s.finish([t_last, t_ctxd])
    return nc


def as_bf16(a):
    a = np.asarray(a)
    if a.dtype == np.uint16:
        return a.view(ml_dtypes.bfloat16)
    return a


def rope_tables_core(j):
    pos = core_positions(j).astype(np.float64)
    f = np.arange(64, dtype=np.float64)
    inv = 1.0 / (10000.0 ** (f / 64.0))
    ang = inv[:, None] * pos[None, :]
    return (np.ascontiguousarray(np.cos(ang).astype(np.float32)),
            np.ascontiguousarray(np.sin(ang).astype(np.float32)))


def diag_mask_core(j):
    kr = np.arange(128)[:, None]
    qc = np.arange(32)[None, :]
    return np.ascontiguousarray(
        np.where(kr > 32 * j + qc, -1e9, 0.0).astype(np.float32))


def reshard_feat_heads(shards, batch, n_heads=32, hd=1024):
    out = np.empty((n_heads, 128, 4096), dtype=shards[0].dtype)
    for j in range(4):
        kf = np.asarray(shards[batch * 4 + j]).reshape(128, n_heads, 1024)
        out[:, :, _POS_IDX[j]] = kf.transpose(1, 0, 2)
    return np.ascontiguousarray(out)


def reshard_v_tokmajor(shards, batch, n_heads=32):
    vfull = np.empty((n_heads, 4096, 128), dtype=shards[0].dtype)
    for j in range(4):
        vf = np.asarray(shards[batch * 4 + j]).reshape(128, n_heads, 1024)
        vfull[:, _POS_IDX[j], :] = vf.transpose(1, 2, 0)
    return np.ascontiguousarray(
        vfull.reshape(n_heads, 32, 128, 128).transpose(0, 2, 1, 3)
        .reshape(n_heads, 128, 4096))


def run_attn_l1b(x_shards, inp):
    qkvw = np.asarray(inp["L1_attn_b_qkv_w"], np.float32)
    ident = np.eye(128, dtype=np.float32)
    wt = tile_w(np.ascontiguousarray(qkvw.T))
    bias96 = bias_cols(np.asarray(inp["L1_attn_b_qkv_b"], np.float32))
    nc = build_attn_proj_l1b()
    in_maps = []
    for c in range(8):
        j = c % 4
        cosf, sinf = rope_tables_core(j)
        in_maps.append({"x": x_shards[c], "wt": wt, "bias_t": bias96,
                        "ident": ident, "cosf": cosf, "sinf": sinf})
    r = run_bass_kernel_spmd(nc, in_maps, core_ids=list(range(8)))
    q_sh = [as_bf16(r.results[c]["q_sh"]) for c in range(8)]
    k_sh = [as_bf16(r.results[c]["k_sh"]) for c in range(8)]
    v_sh = [as_bf16(r.results[c]["v_sh"]) for c in range(8)]
    kall = [reshard_feat_heads(k_sh, b) for b in range(2)]
    vall = [reshard_v_tokmajor(v_sh, b) for b in range(2)]
    wo = tile_w(np.ascontiguousarray(np.asarray(inp["L1_attn_b_out_w"],
                                                np.float32).T))
    bias_o = bias_cols(np.asarray(inp["L1_attn_b_out_b"], np.float32))
    nc2 = build_attn_flash_mha(n_heads=32, group=1, with_bias=True)
    in_maps2 = []
    for c in range(8):
        b, j = divmod(c, 4)
        in_maps2.append({"x": x_shards[c], "q_sh": q_sh[c], "kall": kall[b],
                         "vall": vall[b], "wo": wo, "bias_t": bias_o,
                         "tri": diag_mask_core(j), "ident": ident})
    r2 = run_bass_kernel_spmd(nc2, in_maps2, core_ids=list(range(8)))
    return [r2.results[c]["y"] for c in range(8)]


def build_attn_proj_l0b():
    nc = bass.Bass("TRN2", target_bir_lowering=False)
    x = nc.dram_tensor("x", [PER, 4096], F32, kind="ExternalInput").ap()
    wq = nc.dram_tensor("wq", [32, 128, 4096], F32, kind="ExternalInput").ap()
    wkr = nc.dram_tensor("wkr", [4, 128, 4096], F32, kind="ExternalInput").ap()
    wdk = nc.dram_tensor("wdk", [4, 128, 4096], F32, kind="ExternalInput").ap()
    wupk = nc.dram_tensor("wupk", [4, 128, 512], F32, kind="ExternalInput").ap()
    wupv = nc.dram_tensor("wupv", [8, 128, 512], F32, kind="ExternalInput").ap()
    idf = nc.dram_tensor("ident", [128, 128], F32, kind="ExternalInput").ap()
    cosf = nc.dram_tensor("cosf", [64, 1024], F32, kind="ExternalInput").ap()
    sinf = nc.dram_tensor("sinf", [64, 1024], F32, kind="ExternalInput").ap()
    outs = {}
    for nm, w in (("q_sh", 32), ("kr_sh", 4), ("kc_sh", 4), ("v_sh", 8)):
        outs[nm] = nc.dram_tensor(nm, [128, w * 1024], BF16,
                                  kind="ExternalOutput").ap()
    with ExitStack() as es:
        P = {}
        sb, ps = nc.sbuf_tensor, nc.psum_tensor
        alloc_ln_bufs(nc, es, P)
        P["id_bf"] = es.enter_context(sb("id_bf", [128, 128], BF16))
        P["id_f32"] = es.enter_context(sb("id_f32", [128, 128], F32))
        P["pLN"] = [es.enter_context(ps(f"pLN{i}", [128, 1024], BF16))
                    for i in range(2)]
        P["pD"] = [es.enter_context(ps(f"pD{i}", [128, 1024], F32))
                   for i in range(2)]
        P["wa"] = es.enter_context(sb("wa", [128, 4096], BF16))
        P["wb"] = es.enter_context(sb("wb", [128, 4096], BF16))
        P["hb0"] = es.enter_context(sb("hb0", [128, 1024], F32))
        P["hb1"] = es.enter_context(sb("hb1", [128, 1024], F32))
        P["hc0"] = es.enter_context(sb("hc0", [128, 1024], BF16))
        P["hc1"] = es.enter_context(sb("hc1", [128, 1024], BF16))
        P["rt4"] = es.enter_context(sb("rt4", [64, 4096], F32))
        P["cosf"] = es.enter_context(sb("cosf_sb", [64, 1024], F32))
        P["sinf"] = es.enter_context(sb("sinf_sb", [64, 1024], F32))
        P["ckv_sb"] = es.enter_context(sb("ckv_sb", [128, 4 * 1024], BF16))
        block = es.enter_context(nc.Block())
        s = Sched(nc, es, block)
        t_id = load_ident(s, P, idf)
        t_cos = s.op("sync", lambda q: q.dma_start(out=P["cosf"][:], in_=cosf[:]),
                     dma=True)
        t_sin = s.op("sync", lambda q: q.dma_start(out=P["sinf"][:], in_=sinf[:]),
                     dma=True)
        kdep = emit_ln(s, P, x)
        state = {"dma": [None, None], "rope": [None, None], "ckv": {}}

        def mk_out_cb(dst, kind):
            def out_cb(m, p, toks):
                par = m % 2
                hb = (P["hb0"], P["hb1"])[par]
                hc = (P["hc0"], P["hc1"])[par]
                t_e = s.op("vector", lambda q, p=p, hb=hb: q.tensor_copy(hb[:], p[:]),
                           deps=toks + [state["dma"][par]])
                deps = [t_e, t_cos, t_sin]
                if kind == "q":
                    deps = emit_rope_inplace(s, P, hb, P["cosf"][:], P["sinf"][:],
                                             deps + [state["rope"][1 - par]])
                    state["rope"][par] = deps[-1]
                elif kind == "kr":
                    deps = emit_rope_inplace(s, P, hb, P["cosf"][:], P["sinf"][:],
                                             deps + [state["rope"][1 - par]],
                                             half_rows=32, base=0)
                    deps = emit_rope_inplace(s, P, hb, P["cosf"][:], P["sinf"][:],
                                             deps, half_rows=32, base=64)
                    state["rope"][par] = deps[-1]
                t_c = s.op("vector", lambda q, hc=hc, hb=hb: q.tensor_copy(hc[:], hb[:]),
                           deps=deps)
                if kind == "ckv":
                    t_c2 = s.op("vector", lambda q, m=m, hc=hc: q.tensor_copy(
                        P["ckv_sb"][:, m * 1024:(m + 1) * 1024], hc[:]), deps=[t_c])
                    state["ckv"][m] = t_c2
                    state["dma"][par] = t_c2
                else:
                    t_d = s.op("sync", lambda q, dst=dst, hc=hc, m=m: q.dma_start(
                        out=dst[:, m * 1024:(m + 1) * 1024], in_=hc[:]),
                        deps=[t_c], dma=True)
                    state["dma"][par] = t_d
                return t_e
            return out_cb

        emit_proj(s, P, wq, 32, 32, P["lnt"], kdep, mk_out_cb(outs["q_sh"], "q"),
                  (P["wa"], P["wb"]), first_deps=[t_id[1]])
        emit_proj(s, P, wkr, 4, 32, P["lnt"], kdep, mk_out_cb(outs["kr_sh"], "kr"),
                  (P["wa"], P["wb"]))
        emit_proj(s, P, wdk, 4, 32, P["lnt"], kdep, mk_out_cb(outs["kc_sh"], "ckv"),
                  (P["wa"], P["wb"]))
        ckvdep = {k: [state["ckv"][k]] for k in range(4)}
        emit_proj(s, P, wupk, 4, 4, P["ckv_sb"], ckvdep,
                  mk_out_cb(outs["kc_sh"], "plain"), (P["wa"], P["wb"]),
                  rhs_stride=1024)
        emit_proj(s, P, wupv, 8, 4, P["ckv_sb"], ckvdep,
                  mk_out_cb(outs["v_sh"], "plain"), (P["wa"], P["wb"]),
                  rhs_stride=1024)
        s.finish([state["dma"][0], state["dma"][1]])
    return nc


def run_attn_l0b(x_shards, inp):
    ident = np.eye(128, dtype=np.float32)
    args = {
        "wq": tile_w(np.asarray(inp["L0_attn_b_Wq"], np.float32)),
        "wkr": tile_w(np.asarray(inp["L0_attn_b_Wkr"], np.float32)),
        "wdk": tile_w(np.asarray(inp["L0_attn_b_Wdk"], np.float32)),
        "wupk": tile_w(np.asarray(inp["L0_attn_b_Wupk"], np.float32)),
        "wupv": tile_w(np.asarray(inp["L0_attn_b_Wupv"], np.float32)),
        "ident": ident,
    }
    nc = build_attn_proj_l0b()
    in_maps = []
    for c in range(8):
        j = c % 4
        cosf, sinf = rope_tables_core(j)
        m = dict(args)
        m.update({"x": x_shards[c], "cosf": cosf, "sinf": sinf})
        in_maps.append(m)
    r = run_bass_kernel_spmd(nc, in_maps, core_ids=list(range(8)))
    q_sh = [as_bf16(r.results[c]["q_sh"]) for c in range(8)]
    kr_sh = [as_bf16(r.results[c]["kr_sh"]) for c in range(8)]
    kc_sh = [as_bf16(r.results[c]["kc_sh"]) for c in range(8)]
    v_sh = [as_bf16(r.results[c]["v_sh"]) for c in range(8)]
    krall = [reshard_feat_heads(kr_sh, b, n_heads=4) for b in range(2)]
    kcall = [reshard_feat_heads(kc_sh, b, n_heads=4) for b in range(2)]
    kall = []
    for b in range(2):
        kf = np.empty((8, 128, 4096), dtype=krall[0].dtype)
        krb = krall[b].reshape(4 * 128, 4096)
        kcb = kcall[b].reshape(4 * 128, 4096)
        for g in range(8):
            kf[g, 0:32] = krb[64 * g:64 * g + 32]
            kf[g, 32:64] = kcb[64 * g:64 * g + 32]
            kf[g, 64:96] = krb[64 * g + 32:64 * g + 64]
            kf[g, 96:128] = kcb[64 * g + 32:64 * g + 64]
        kall.append(np.ascontiguousarray(kf))
    vall = [reshard_v_tokmajor(v_sh, b, n_heads=8) for b in range(2)]
    wo = tile_w(np.asarray(inp["L0_attn_b_Wo"], np.float32))
    zb = np.zeros((128, 32), np.float32)
    nc2 = build_attn_flash_mha(n_heads=32, group=4, with_bias=False)
    in_maps2 = []
    for c in range(8):
        b, j = divmod(c, 4)
        in_maps2.append({"x": x_shards[c], "q_sh": q_sh[c], "kall": kall[b],
                         "vall": vall[b], "wo": wo, "bias_t": zb,
                         "tri": diag_mask_core(j), "ident": ident})
    r2 = run_bass_kernel_spmd(nc2, in_maps2, core_ids=list(range(8)))
    return [r2.results[c]["y"] for c in range(8)]


def permute_wqb(wq_b):
    out = np.empty_like(wq_b)
    for h in range(32):
        out[128 * h:128 * (h + 1)] = wq_b[192 * h:192 * h + 128]
        for j in range(32):
            out[4096 + 64 * h + j] = wq_b[192 * h + 128 + 2 * j]
            out[4096 + 64 * h + 32 + j] = wq_b[192 * h + 128 + 2 * j + 1]
    return out


def permute_wkva(wkv_a):
    out = np.zeros((640, 4096), wkv_a.dtype)
    out[0:512] = wkv_a[0:512]
    for j in range(32):
        out[512 + j] = wkv_a[512 + 2 * j]
        out[544 + j] = wkv_a[512 + 2 * j + 1]
    return out


def rope_tables_pe_core(j):
    pos = core_positions(j).astype(np.float64)
    f = np.arange(32, dtype=np.float64)
    inv = 1.0 / (10000.0 ** (2.0 * f / 64.0))
    ang = inv[:, None] * pos[None, :]
    return (np.ascontiguousarray(np.cos(ang).astype(np.float32)),
            np.ascontiguousarray(np.sin(ang).astype(np.float32)))


def build_attn_proj_l1a():
    nc = bass.Bass("TRN2", target_bir_lowering=False)
    x = nc.dram_tensor("x", [PER, 4096], F32, kind="ExternalInput").ap()
    wqa = nc.dram_tensor("wqa", [12, 128, 4096], F32, kind="ExternalInput").ap()
    wqb = nc.dram_tensor("wqb", [48, 128, 1536], F32, kind="ExternalInput").ap()
    wkva = nc.dram_tensor("wkva", [5, 128, 4096], F32, kind="ExternalInput").ap()
    gq = nc.dram_tensor("gq", [128, 12], F32, kind="ExternalInput").ap()
    gk = nc.dram_tensor("gk", [128, 4], F32, kind="ExternalInput").ap()
    idf = nc.dram_tensor("ident", [128, 128], F32, kind="ExternalInput").ap()
    cpe = nc.dram_tensor("cpe", [32, 1024], F32, kind="ExternalInput").ap()
    spe = nc.dram_tensor("spe", [32, 1024], F32, kind="ExternalInput").ap()
    q2_sh = nc.dram_tensor("q2_sh", [128, 48 * 1024], BF16, kind="ExternalOutput").ap()
    kvn_sh = nc.dram_tensor("kvn_sh", [128, 4 * 1024], BF16, kind="ExternalOutput").ap()
    kpe_sh = nc.dram_tensor("kpe_sh", [128, 1024], BF16, kind="ExternalOutput").ap()
    with ExitStack() as es:
        P = {}
        sb, ps = nc.sbuf_tensor, nc.psum_tensor
        alloc_ln_bufs(nc, es, P)
        P["id_bf"] = es.enter_context(sb("id_bf", [128, 128], BF16))
        P["id_f32"] = es.enter_context(sb("id_f32", [128, 128], F32))
        P["pLN"] = [es.enter_context(ps(f"pLN{i}", [128, 1024], BF16))
                    for i in range(2)]
        P["pD"] = [es.enter_context(ps(f"pD{i}", [128, 1024], F32))
                   for i in range(2)]
        P["pR"] = es.enter_context(ps("pR", [128, 1024], F32))
        P["wa"] = es.enter_context(sb("wa", [128, 4096], BF16))
        P["wb"] = es.enter_context(sb("wb", [128, 4096], BF16))
        P["hb0"] = es.enter_context(sb("hb0", [128, 1024], F32))
        P["hb1"] = es.enter_context(sb("hb1", [128, 1024], F32))
        P["hc0"] = es.enter_context(sb("hc0", [128, 1024], BF16))
        P["hc1"] = es.enter_context(sb("hc1", [128, 1024], BF16))
        P["rt4"] = es.enter_context(sb("rt4", [64, 4096], F32))
        P["cpe"] = es.enter_context(sb("cpe_sb", [32, 1024], F32))
        P["spe"] = es.enter_context(sb("spe_sb", [32, 1024], F32))
        P["cq_sb"] = es.enter_context(sb("cq_sb", [128, 12 * 1024], BF16))
        P["qn_sb"] = P["cq_sb"]
        P["ckv_sb"] = es.enter_context(sb("ckv_sb", [128, 4 * 1024], BF16))
        P["sqt"] = es.enter_context(sb("sqt", [128, 1024], BF16))
        P["gq_sb"] = es.enter_context(sb("gq_sb", [128, 12], F32))
        P["gk_sb"] = es.enter_context(sb("gk_sb", [128, 4], F32))
        P["onescol"] = es.enter_context(sb("onescol", [128, 1], BF16))
        P["ones_row"] = es.enter_context(sb("ones_row", [1, 128], BF16))
        P["rs_sb"] = es.enter_context(sb("rs_sb", [1, 2048], F32))
        P["rs_bf"] = es.enter_context(sb("rs_bf", [1, 1024], BF16))
        P["eps2"] = es.enter_context(sb("eps2", [1, 1], F32))
        block = es.enter_context(nc.Block())
        s = Sched(nc, es, block)
        t_id = load_ident(s, P, idf)
        t_cos = s.op("sync", lambda q: q.dma_start(out=P["cpe"][:], in_=cpe[:]),
                     dma=True)
        t_sin = s.op("sync", lambda q: q.dma_start(out=P["spe"][:], in_=spe[:]),
                     dma=True)
        t_gq = s.op("sync", lambda q: q.dma_start(out=P["gq_sb"][:], in_=gq[:]),
                    dma=True)
        t_gk = s.op("sync", lambda q: q.dma_start(out=P["gk_sb"][:], in_=gk[:]),
                    dma=True)
        t_oc = s.op("vector", lambda q: q.memset(P["onescol"][:], 1.0))
        t_or = s.op("vector", lambda q: q.memset(P["ones_row"][:], 1.0))
        t_ep = s.op("vector", lambda q: q.memset(P["eps2"][:], 1e-6))
        kdep = emit_ln(s, P, x)
        state = {"dma": [None, None], "rope": [None, None]}

        sq_toks = []

        def cq_cb(m, p, toks):
            t_sq = s.op("scalar", lambda q, p=p: q.activation(
                out=P["sqt"][:], in_=p[:], func=AF.Square),
                deps=toks + (sq_toks[-1:] if sq_toks else []))
            t_e = s.op("vector", lambda q, m=m, p=p: q.tensor_copy(
                P["cq_sb"][:, m * 1024:(m + 1) * 1024], p[:]),
                deps=toks + [state["dma"][m % 2], t_sq])
            for hf in range(2):
                t_mm = s.op("tensor", lambda q, m=m, hf=hf: q.matmul(
                    P["pR"][0:1, hf * 512:(hf + 1) * 512], P["onescol"][:],
                    P["sqt"][:, hf * 512:(hf + 1) * 512],
                    start=(m == 0), stop=(m == 11)),
                    deps=[t_sq, t_oc])
            sq_toks.append(t_mm)
            state["dma"][m % 2] = t_sq
            return t_e
        emit_proj(s, P, wqa, 12, 32, P["lnt"], kdep, cq_cb,
                  (P["wa"], P["wb"]), first_deps=[t_id[1]])
        t_l = s.op("scalar", lambda q: q.activation(
            out=P["rs_sb"][0:1, 0:1024], in_=P["pR"][0:1, 0:1024], func=AF.Ln,
            scale=1.0 / 1536.0, bias=P["eps2"][:]), deps=[sq_toks[-1], t_ep])
        t_r = s.op("scalar", lambda q: q.activation(
            out=P["rs_sb"][0:1, 1024:2048], in_=P["rs_sb"][0:1, 0:1024],
            func=AF.Exp, scale=-0.5), deps=[t_l])
        t_rb = s.op("vector", lambda q: q.tensor_copy(P["rs_bf"][:],
                                                      P["rs_sb"][0:1, 1024:2048]),
                    deps=[t_r])
        bct = []
        for hf in range(2):
            bct.append(s.op("tensor", lambda q, hf=hf: q.matmul(
                P["pD"][0][:, hf * 512:(hf + 1) * 512], P["ones_row"][:],
                P["rs_bf"][0:1, hf * 512:(hf + 1) * 512], start=True, stop=True),
                deps=[t_rb, t_or, P["proj_state"]["out"][0], P["proj_state"]["out"][1]]))
        qn_toks = {}
        for k in range(12):
            qn_toks[k] = [s.op("vector", lambda q, k=k: q.scalar_tensor_tensor(
                P["qn_sb"][:, k * 1024:(k + 1) * 1024],
                P["cq_sb"][:, k * 1024:(k + 1) * 1024],
                P["gq_sb"][:, k:k + 1], P["pD"][0][:, 0:1024],
                op0=ALU.mult, op1=ALU.mult), deps=bct + [t_gq])]
        allqn = [qn_toks[k][0] for k in range(12)]

        def q2_cb(m, p, toks):
            par = m % 2
            hb = (P["hb0"], P["hb1"])[par]
            hc = (P["hc0"], P["hc1"])[par]
            t_e = s.op("vector", lambda q, p=p, hb=hb: q.tensor_copy(hb[:], p[:]),
                       deps=toks + [state["dma"][par]])
            deps = [t_e]
            if m >= 32:
                deps = emit_rope_inplace(s, P, hb, P["cpe"][:], P["spe"][:],
                                         deps + [state["rope"][1 - par], t_cos,
                                                 t_sin],
                                         half_rows=32, base=0)
                deps = emit_rope_inplace(s, P, hb, P["cpe"][:], P["spe"][:],
                                         deps, half_rows=32, base=64)
                state["rope"][par] = deps[-1]
            t_c = s.op("vector", lambda q, hc=hc, hb=hb: q.tensor_copy(hc[:], hb[:]),
                       deps=deps)
            t_d = s.op("sync", lambda q, hc=hc, m=m: q.dma_start(
                out=q2_sh[:, m * 1024:(m + 1) * 1024], in_=hc[:]),
                deps=[t_c], dma=True)
            state["dma"][par] = t_d
            return t_e
        emit_proj(s, P, wqb, 48, 12, P["qn_sb"], qn_toks, q2_cb,
                  (P["wa"], P["wb"]), rhs_stride=1024, first_deps=allqn)

        sq2_toks = []
        ckv_toks = {}

        def kv_cb(m, p, toks):
            par = m % 2
            if m < 4:
                t_sq = s.op("scalar", lambda q, p=p: q.activation(
                    out=P["sqt"][:], in_=p[:], func=AF.Square),
                    deps=toks + (sq2_toks[-1:] if sq2_toks else [t_l]))
                t_e = s.op("vector", lambda q, m=m, p=p: q.tensor_copy(
                    P["ckv_sb"][:, m * 1024:(m + 1) * 1024], p[:]),
                    deps=toks + [state["dma"][par], t_sq])
                for hf in range(2):
                    t_mm = s.op("tensor", lambda q, m=m, hf=hf: q.matmul(
                        P["pR"][0:1, hf * 512:(hf + 1) * 512], P["onescol"][:],
                        P["sqt"][:, hf * 512:(hf + 1) * 512],
                        start=(m == 0), stop=(m == 3)),
                        deps=[t_sq, t_l])
                sq2_toks.append(t_mm)
                ckv_toks[m] = t_e
                state["dma"][par] = t_sq
                return t_e
            hb = (P["hb0"], P["hb1"])[par]
            hc = (P["hc0"], P["hc1"])[par]
            t_e = s.op("vector", lambda q, p=p, hb=hb: q.tensor_copy(hb[:], p[:]),
                       deps=toks + [state["dma"][par]])
            deps = emit_rope_inplace(s, P, hb, P["cpe"][:], P["spe"][:],
                                     [t_e, state["rope"][1 - par], t_cos, t_sin],
                                     half_rows=32, base=0)
            state["rope"][par] = deps[-1]
            t_c = s.op("vector", lambda q, hc=hc, hb=hb: q.tensor_copy(hc[:], hb[:]),
                       deps=deps)
            t_d = s.op("sync", lambda q, hc=hc: q.dma_start(
                out=kpe_sh[:], in_=hc[:]), deps=[t_c], dma=True)
            state["dma"][par] = t_d
            return t_e
        emit_proj(s, P, wkva, 5, 32, P["lnt"], kdep, kv_cb, (P["wa"], P["wb"]))
        t_l2 = s.op("scalar", lambda q: q.activation(
            out=P["rs_sb"][0:1, 0:1024], in_=P["pR"][0:1, 0:1024], func=AF.Ln,
            scale=1.0 / 512.0, bias=P["eps2"][:]), deps=[sq2_toks[-1], t_r])
        t_r2 = s.op("scalar", lambda q: q.activation(
            out=P["rs_sb"][0:1, 1024:2048], in_=P["rs_sb"][0:1, 0:1024],
            func=AF.Exp, scale=-0.5), deps=[t_l2])
        t_rb2 = s.op("vector", lambda q: q.tensor_copy(P["rs_bf"][:],
                                                       P["rs_sb"][0:1, 1024:2048]),
                     deps=[t_r2, allqn[-1]])
        bct2 = []
        for hf in range(2):
            bct2.append(s.op("tensor", lambda q, hf=hf: q.matmul(
                P["pD"][0][:, hf * 512:(hf + 1) * 512], P["ones_row"][:],
                P["rs_bf"][0:1, hf * 512:(hf + 1) * 512], start=True, stop=True),
                deps=[t_rb2, P["proj_state"]["out"][0], P["proj_state"]["out"][1]]
                     + allqn))
        t_kvns = []
        for k in range(4):
            t_kvn = s.op("vector", lambda q, k=k: q.scalar_tensor_tensor(
                P["qn_sb"][:, k * 1024:(k + 1) * 1024],
                P["ckv_sb"][:, k * 1024:(k + 1) * 1024],
                P["gk_sb"][:, k:k + 1], P["pD"][0][:, 0:1024],
                op0=ALU.mult, op1=ALU.mult), deps=bct2 + [t_gk, ckv_toks[k]])
            t_kvns.append(s.op("sync", lambda q, k=k: q.dma_start(
                out=kvn_sh[:, k * 1024:(k + 1) * 1024],
                in_=P["qn_sb"][:, k * 1024:(k + 1) * 1024]), deps=[t_kvn],
                dma=True))
        s.finish([state["dma"][0], state["dma"][1], t_kvns[-1]])
    return nc


def build_attn_flash_l1a():
    scale = 1.0 / float(np.sqrt(192.0))
    nc = bass.Bass("TRN2", target_bir_lowering=False)
    x = nc.dram_tensor("x", [PER, 4096], F32, kind="ExternalInput").ap()
    q2_sh = nc.dram_tensor("q2_sh", [128, 48 * 1024], BF16, kind="ExternalInput").ap()
    kvnall = nc.dram_tensor("kvnall", [4, 128, 4096], BF16, kind="ExternalInput").ap()
    kpeall = nc.dram_tensor("kpeall", [1, 128, 4096], BF16, kind="ExternalInput").ap()
    wkvb = nc.dram_tensor("wkvb", [64, 128, 512], F32, kind="ExternalInput").ap()
    wo = nc.dram_tensor("wo", [32, 128, 4096], F32, kind="ExternalInput").ap()
    tri = nc.dram_tensor("tri", [128, 32], F32, kind="ExternalInput").ap()
    idf = nc.dram_tensor("ident", [128, 128], F32, kind="ExternalInput").ap()
    y = nc.dram_tensor("y", [PER, 4096], F32, kind="ExternalOutput").ap()
    with ExitStack() as es:
        P = {}
        sb, ps = nc.sbuf_tensor, nc.psum_tensor
        P["id_f32"] = es.enter_context(sb("id_f32", [128, 128], F32))
        P["ctx_all"] = es.enter_context(sb("ctx_all", [128, 32 * 1024], BF16))
        P["kvn_sb"] = es.enter_context(sb("kvn_sb", [128, 4 * 4096], BF16))
        P["kpe_sb"] = es.enter_context(sb("kpe_sb", [128, 4096], BF16))
        P["kbuf1"] = es.enter_context(sb("kbuf1", [128, 4096], BF16))
        P["vbuf1"] = es.enter_context(sb("vbuf1", [128, 4096], BF16))
        P["vtmp"] = es.enter_context(sb("vtmp", [128, 512], F32))
        P["qbuf"] = [es.enter_context(sb(f"qbuf{i}", [128, 1024], BF16))
                     for i in range(2)]
        P["qpebuf"] = es.enter_context(sb("qpebuf", [128, 1024], BF16))
        P["wexp"] = [es.enter_context(sb(f"wexp{i}", [128, 1024], BF16))
                     for i in range(2)]
        alloc_flash_bufs(nc, es, P)
        P["wa"] = es.enter_context(sb("wa", [128, 4096], BF16))
        P["wb"] = es.enter_context(sb("wb", [128, 4096], BF16))
        alloc_resid_bufs(nc, es, P, gsz=4)
        block = es.enter_context(nc.Block())
        s = Sched(nc, es, block)
        t_id = s.op("sync", lambda q: q.dma_start(out=P["id_f32"][:], in_=idf[:]),
                    dma=True)
        P["t_idf"] = t_id
        t_kvn = [s.op("sync", lambda q, k=k: q.dma_start(
            out=P["kvn_sb"][:, k * 4096:(k + 1) * 4096], in_=kvnall[k, :, :]),
            dma=True) for k in range(4)]
        t_kpe = s.op("sync", lambda q: q.dma_start(out=P["kpe_sb"][:],
                                                   in_=kpeall[0, :, :]), dma=True)
        ctx_done = [[None] * 2 for _ in range(32)]
        with ExitStack() as es2:
            alloc_flash_psum(nc, es2, P)
            t_tri = s.op("sync", lambda q: q.dma_start(out=P["dmask"][:],
                                                       in_=tri[:]), dma=True)
            t_c1 = s.op("vector", lambda q: q.memset(P["ones_row"][:], 1.0))
            t_c2 = s.op("vector", lambda q: q.memset(P["negscale_row"][:], -scale))
            t_c3 = s.op("vector", lambda q: q.memset(P["onescol"][:], 1.0))
            P["t_ctxfree"] = [t_c1]
            P["t_denfree"] = [t_c3]
            P["t_pm1free"] = {0: t_c2, 1: t_c2}
            P["t_exp_free"] = {}
            t_w_prev = [None, None]
            t_q_prev = [None, None]
            t_qpe_prev = [None]
            t_kv_use_prev = [None]
            for hh in range(32):
                hpar = hh % 2
                wsl = wkvb[2 * hh:2 * hh + 2].rearrange("m p j -> p m j")
                wex = P["wexp"][hpar]
                t_w = s.op("gpsimd", lambda q, wsl=wsl, wex=wex: q.dma_start(
                    out=wex[:].rearrange("p (m j) -> p m j", j=512), in_=wsl),
                    deps=[t_w_prev[hpar]], dma=True)
                guard = [P["t_pm1free"][0], P["t_pm1free"][1],
                         P["t_exp_free"].get((1, 1)), t_kv_use_prev[0]]
                t_kb = []
                t_vb = []
                for c5 in range(8):
                    last = None
                    for ki in range(4):
                        last = s.op("tensor", lambda q, c5=c5, ki=ki, wex=wex:
                                    q.matmul(
                            P["pM1"][:, 0:512],
                            wex[:, ki * 128:(ki + 1) * 128],
                            P["kvn_sb"][:, ki * 4096 + c5 * 512:
                                        ki * 4096 + (c5 + 1) * 512],
                            start=(ki == 0), stop=(ki == 3)),
                            deps=[t_w] + guard + t_kvn + t_kb[-1:])
                    t_ke = s.op("vector", lambda q, c5=c5: q.tensor_copy(
                        P["kbuf1"][:, c5 * 512:(c5 + 1) * 512], P["pM1"][:, 0:512]),
                        deps=[last])
                    t_kb.append(t_ke)
                    last = None
                    for ki in range(4):
                        last = s.op("tensor", lambda q, c5=c5, ki=ki, wex=wex:
                                    q.matmul(
                            P["pM1"][:, 512:1024],
                            wex[:, 512 + ki * 128:512 + (ki + 1) * 128],
                            P["kvn_sb"][:, ki * 4096 + c5 * 512:
                                        ki * 4096 + (c5 + 1) * 512],
                            start=(ki == 0), stop=(ki == 3)),
                            deps=[t_w] + guard + t_kvn + t_vb[-1:])
                    t_ve = s.op("vector", lambda q: q.tensor_copy(
                        P["vtmp"][:], P["pM1"][:, 512:1024]), deps=[last])
                    tps = []
                    for i in range(4):
                        tps.append(s.op("tensor", lambda q, i=i: q.transpose(
                            P["pS2"][1][:, 512 + i * 128:512 + (i + 1) * 128],
                            P["vtmp"][:, i * 128:(i + 1) * 128], P["id_f32"][:]),
                            deps=[t_ve, t_id]))
                    t_vv = s.op("vector", lambda q, c5=c5: q.tensor_copy(
                        P["vbuf1"][:, c5 * 512:(c5 + 1) * 512],
                        P["pS2"][1][:, 512:1024]), deps=tps)
                    t_vb.append(t_vv)
                t_kdone = t_kb[-1]
                t_vdone = t_vb[-1]
                P["t_pm1free"][0] = t_kb[-1]
                P["t_pm1free"][1] = t_vb[-1]
                P["t_exp_free"][(1, 1)] = t_vb[-1]
                t_w_prev[hpar] = t_kb[-1]
                t_q = s.op("sync", lambda q, hh=hh, hpar=hpar: q.dma_start(
                    out=P["qbuf"][hpar][:],
                    in_=q2_sh[:, hh * 1024:(hh + 1) * 1024]),
                    deps=[t_q_prev[hpar]], dma=True)
                if hh % 2 == 0:
                    t_qpe = s.op("sync", lambda q, hh=hh: q.dma_start(
                        out=P["qpebuf"][:],
                        in_=q2_sh[:, (32 + hh // 2) * 1024:(33 + hh // 2) * 1024]),
                        deps=[t_qpe_prev[0]], dma=True)
                    P["t_qpe"] = t_qpe
                t_qpe = P["t_qpe"]
                qrow = (hh % 2) * 64
                qbuf = P["qbuf"][hpar]
                for st in range(2):

                    def smm(kc, out_ap, q0, deps, qbuf=qbuf, st=st, qrow=qrow,
                            t_q=t_q, t_qpe=t_qpe, t_kdone=t_kdone):
                        t1 = s.op("tensor", lambda q: q.matmul(
                            out_ap, P["kbuf1"][:, kc * 128:(kc + 1) * 128],
                            qbuf[:, st * 512 + q0:st * 512 + 512],
                            start=True, stop=False),
                            deps=deps + [t_q, t_kdone, t_tri])
                        return s.op("tensor", lambda q: q.matmul(
                            out_ap,
                            P["kpe_sb"][qrow:qrow + 64, kc * 128:(kc + 1) * 128],
                            P["qpebuf"][qrow:qrow + 64,
                                        st * 512 + q0:st * 512 + 512],
                            start=False, stop=True),
                            deps=[t_qpe, t_kpe])
                    ctx_dst = P["ctx_all"][:, hh * 1024 + st * 512:
                                           hh * 1024 + (st + 1) * 512]
                    t_fin = emit_flash_half(s, P, None, hh, st, smm,
                                            P["vbuf1"], t_vdone, ctx_dst, scale)
                    ctx_done[hh][st] = t_fin
                t_q_prev[hpar] = ctx_done[hh][1]
                t_qpe_prev[0] = ctx_done[hh][1]
                t_kv_use_prev[0] = ctx_done[hh][1]
        P["pD"] = [es.enter_context(ps(f"pD{i}", [128, 1024], F32))
                   for i in range(2)]
        P["pR"] = es.enter_context(ps("pR", [128, 1024], F32))
        src_deps = {k: [ctx_done[k][0], ctx_done[k][1]] if k > 0 else
                    [ctx_done[hh][hf] for hh in range(32) for hf in range(2)]
                    for k in range(32)}
        t_last = emit_out_proj_residual(s, P, wo, None, P["ctx_all"],
                                        src_deps, x, y, (P["wa"], P["wb"]),
                                        gsz=4)
        s.finish([t_last])
    return nc


def run_attn_l1a(x_shards, inp):
    ident = np.eye(128, dtype=np.float32)
    args = {
        "wqa": tile_w(np.asarray(inp["L1_attn_a_wq_a"], np.float32)),
        "wqb": tile_w(permute_wqb(np.asarray(inp["L1_attn_a_wq_b"], np.float32))),
        "wkva": tile_w(permute_wkva(np.asarray(inp["L1_attn_a_wkv_a"], np.float32))),
        "gq": bias_cols(np.asarray(inp["L1_attn_a_q_norm_w"], np.float32)),
        "gk": bias_cols(np.asarray(inp["L1_attn_a_kv_norm_w"], np.float32)),
        "ident": ident,
    }
    nc = build_attn_proj_l1a()
    in_maps = []
    for c in range(8):
        j = c % 4
        cpe, spe = rope_tables_pe_core(j)
        m = dict(args)
        m.update({"x": x_shards[c], "cpe": cpe, "spe": spe})
        in_maps.append(m)
    r = run_bass_kernel_spmd(nc, in_maps, core_ids=list(range(8)))
    q2_sh = [as_bf16(r.results[c]["q2_sh"]) for c in range(8)]
    kvn_sh = [as_bf16(r.results[c]["kvn_sh"]) for c in range(8)]
    kpe_sh = [as_bf16(r.results[c]["kpe_sh"]) for c in range(8)]
    kvnall = [reshard_feat_heads(kvn_sh, b, n_heads=4) for b in range(2)]
    kpeall = []
    for b in range(2):
        kp = reshard_feat_heads(kpe_sh, b, n_heads=1)
        kp[0, 64:128] = kp[0, 0:64]
        kpeall.append(kp)
    wkvb = tile_w(np.asarray(inp["L1_attn_a_wkv_b"], np.float32))
    wo = tile_w(np.asarray(inp["L1_attn_a_wo"], np.float32))
    nc2 = build_attn_flash_l1a()
    in_maps2 = []
    for c in range(8):
        b, j = divmod(c, 4)
        in_maps2.append({"x": x_shards[c], "q2_sh": q2_sh[c], "kvnall": kvnall[b],
                         "kpeall": kpeall[b], "wkvb": wkvb, "wo": wo,
                         "tri": diag_mask_core(j), "ident": ident})
    r2 = run_bass_kernel_spmd(nc2, in_maps2, core_ids=list(range(8)))
    return [r2.results[c]["y"] for c in range(8)]


def build_attn_proj_l0a():
    nc = bass.Bass("TRN2", target_bir_lowering=False)
    x = nc.dram_tensor("x", [PER, 4096], F32, kind="ExternalInput").ap()
    wqd = nc.dram_tensor("wqd", [128, 32 * 12], F32, kind="ExternalInput").ap()
    wkvd = nc.dram_tensor("wkvd", [128, 32 * 4], F32, kind="ExternalInput").ap()
    wqk = nc.dram_tensor("wqk", [128, 128], F32, kind="ExternalInput").ap()
    bqd = nc.dram_tensor("bqd", [128, 1], F32, kind="ExternalInput").ap()
    bkvd = nc.dram_tensor("bkvd", [128, 1], F32, kind="ExternalInput").ap()
    bqk = nc.dram_tensor("bqk", [128, 8], F32, kind="ExternalInput").ap()
    idf = nc.dram_tensor("ident", [128, 128], F32, kind="ExternalInput").ap()
    q_sh = nc.dram_tensor("q_sh", [128, 8 * 1024], BF16, kind="ExternalOutput").ap()
    ckv_sh = nc.dram_tensor("ckv_sh", [128, 1024], BF16, kind="ExternalOutput").ap()
    with ExitStack() as es:
        P = {}
        sb, ps = nc.sbuf_tensor, nc.psum_tensor
        alloc_ln_bufs(nc, es, P)
        P["id_bf"] = es.enter_context(sb("id_bf", [128, 128], BF16))
        P["id_f32"] = es.enter_context(sb("id_f32", [128, 128], F32))
        P["pLN"] = [es.enter_context(ps(f"pLN{i}", [128, 1024], BF16))
                    for i in range(2)]
        P["pD"] = [es.enter_context(ps(f"pD{i}", [128, 1024], F32))
                   for i in range(2)]
        P["pR"] = es.enter_context(ps("pR", [128, 1024], F32))
        P["wqd_sb"] = es.enter_context(sb("wqd_sb", [128, 32 * 12], BF16))
        P["wkvd_sb"] = es.enter_context(sb("wkvd_sb", [128, 32 * 4], BF16))
        P["wqk_sb"] = es.enter_context(sb("wqk_sb", [128, 128], BF16))
        P["bqd_sb"] = es.enter_context(sb("bqd_sb", [128, 1], F32))
        P["bkvd_sb"] = es.enter_context(sb("bkvd_sb", [128, 1], F32))
        P["bqk_sb"] = es.enter_context(sb("bqk_sb", [128, 8], F32))
        P["cq_sb"] = es.enter_context(sb("cq_sb", [128, 1024], BF16))
        P["ckv_b"] = es.enter_context(sb("ckv_b", [128, 1024], BF16))
        P["qp_sb"] = es.enter_context(sb("qp_sb", [128, 1024], BF16))
        block = es.enter_context(nc.Block())
        s = Sched(nc, es, block)
        t_id = load_ident(s, P, idf)
        lds = []
        for nm, src_t in (("wqd_sb", wqd), ("wkvd_sb", wkvd), ("wqk_sb", wqk)):
            lds.append(s.op("gpsimd", lambda q, nm=nm, src_t=src_t: q.dma_start(
                out=P[nm][:], in_=src_t[:]), dma=True))
        for nm, src_t in (("bqd_sb", bqd), ("bkvd_sb", bkvd), ("bqk_sb", bqk)):
            lds.append(s.op("sync", lambda q, nm=nm, src_t=src_t: q.dma_start(
                out=P[nm][:], in_=src_t[:]), dma=True))
        kdep = emit_ln(s, P, x)
        t_cq = None
        for hf in range(2):
            for k in range(32):
                t_cq = s.op("tensor", lambda q, hf=hf, k=k: q.matmul(
                    P["pR"][0:12, hf * 512:(hf + 1) * 512],
                    P["wqd_sb"][:, k * 12:(k + 1) * 12],
                    P["lnt"][:, k * PER + hf * 512:k * PER + (hf + 1) * 512],
                    start=(k == 0), stop=(k == 31)),
                    deps=lds[:1] + kdep[k])
        t_cqe = s.op("vector", lambda q: q.tensor_scalar(
            P["cq_sb"][0:12, :], P["pR"][0:12, 0:1024], P["bqd_sb"][0:12, :],
            None, op0=ALU.add), deps=[t_cq, lds[3]])
        t_ck = None
        for hf in range(2):
            for k in range(32):
                t_ck = s.op("tensor", lambda q, hf=hf, k=k: q.matmul(
                    P["pD"][0][0:4, hf * 512:(hf + 1) * 512],
                    P["wkvd_sb"][:, k * 4:(k + 1) * 4],
                    P["lnt"][:, k * PER + hf * 512:k * PER + (hf + 1) * 512],
                    start=(k == 0), stop=(k == 31)),
                    deps=lds[1:2] + kdep[k])
        t_cke = s.op("vector", lambda q: q.tensor_scalar(
            P["ckv_b"][0:4, :], P["pD"][0][0:4, 0:1024], P["bkvd_sb"][0:4, :],
            None, op0=ALU.add), deps=[t_ck, lds[4]])
        t_ckd = s.op("sync", lambda q: q.dma_start(out=ckv_sh[:], in_=P["ckv_b"][:]),
                     deps=[t_cke], dma=True)
        t_qd = None
        t_qe_prev = None
        for g in range(8):
            mms = []
            for sl in range(4):
                h = 4 * g + sl
                for hf in range(2):
                    mms.append(s.op("tensor", lambda q, sl=sl, h=h, hf=hf: q.matmul(
                        P["pD"][1][32 * sl:32 * sl + 4, hf * 512:(hf + 1) * 512],
                        P["wqk_sb"][0:12, 4 * h:4 * h + 4],
                        P["cq_sb"][0:12, hf * 512:(hf + 1) * 512],
                        start=True, stop=True, tile_position=(0, 32 * sl)),
                        deps=[t_cqe, lds[2], t_qe_prev]))
            t_qe = s.op("vector", lambda q, g=g: q.tensor_scalar(
                P["qp_sb"][:], P["pD"][1][:, 0:1024], P["bqk_sb"][:, g:g + 1],
                None, op0=ALU.add), deps=mms + [lds[5], t_qd])
            t_qe_prev = t_qe
            t_qd = s.op("sync", lambda q, g=g: q.dma_start(
                out=q_sh[:, g * 1024:(g + 1) * 1024], in_=P["qp_sb"][:]),
                deps=[t_qe], dma=True)
        s.finish([t_qd, t_ckd])
    return nc


def build_attn_flash_l0a():
    scale = 0.5
    nc = bass.Bass("TRN2", target_bir_lowering=False)
    x = nc.dram_tensor("x", [PER, 4096], F32, kind="ExternalInput").ap()
    q_sh = nc.dram_tensor("q_sh", [128, 8 * 1024], BF16, kind="ExternalInput").ap()
    ckvf = nc.dram_tensor("ckvf", [128, 4096], BF16, kind="ExternalInput").ap()
    ckvt = nc.dram_tensor("ckvt", [128, 32 * 36], BF16, kind="ExternalInput").ap()
    wvu = nc.dram_tensor("wvu", [128, 4096], F32, kind="ExternalInput").ap()
    bvu = nc.dram_tensor("bvu", [128, 32], F32, kind="ExternalInput").ap()
    wo = nc.dram_tensor("wo", [32, 128, 4096], F32, kind="ExternalInput").ap()
    bo = nc.dram_tensor("bo", [128, 32], F32, kind="ExternalInput").ap()
    idf = nc.dram_tensor("ident", [128, 128], F32, kind="ExternalInput").ap()
    y = nc.dram_tensor("y", [PER, 4096], F32, kind="ExternalOutput").ap()
    with ExitStack() as es:
        P = {}
        sb, ps = nc.sbuf_tensor, nc.psum_tensor
        P["id_f32"] = es.enter_context(sb("id_f32", [128, 128], F32))
        P["ctx_all"] = es.enter_context(sb("ctx_all", [128, 32 * 1024], BF16))
        P["qp_sb"] = es.enter_context(sb("qp_sb", [128, 8 * 1024], BF16))
        P["ckvf_sb"] = es.enter_context(sb("ckvf_sb", [128, 4096], BF16))
        P["ckvt_sb"] = es.enter_context(sb("ckvt_sb", [128, 32 * 36], BF16))
        P["wvu_sb"] = es.enter_context(sb("wvu_sb", [128, 4096], BF16))
        P["bvu_sb"] = es.enter_context(sb("bvu_sb", [128, 32], F32))
        P["M_all"] = es.enter_context(sb("M_all", [128, 8 * 1024], BF16))
        P["m_run"] = es.enter_context(sb("m_run", [36, 512], F32))
        alloc_flash_bufs(nc, es, P)
        P["wa"] = es.enter_context(sb("wa", [128, 4096], BF16))
        P["wb"] = es.enter_context(sb("wb", [128, 4096], BF16))
        alloc_resid_bufs(nc, es, P)
        P["bias_sb"] = es.enter_context(sb("bias_sb", [128, 32], F32))
        block = es.enter_context(nc.Block())
        s = Sched(nc, es, block)
        t_id = s.op("sync", lambda q: q.dma_start(out=P["id_f32"][:], in_=idf[:]),
                    dma=True)
        P["t_idf"] = t_id
        t_q = s.op("sync", lambda q: q.dma_start(out=P["qp_sb"][:], in_=q_sh[:]),
                   dma=True)
        t_cf = s.op("sync", lambda q: q.dma_start(out=P["ckvf_sb"][:], in_=ckvf[:]),
                    dma=True)
        t_ct = s.op("sync", lambda q: q.dma_start(out=P["ckvt_sb"][:], in_=ckvt[:]),
                    dma=True)
        t_wv = s.op("gpsimd", lambda q: q.dma_start(out=P["wvu_sb"][:], in_=wvu[:]),
                    dma=True)
        t_bv = s.op("sync", lambda q: q.dma_start(out=P["bvu_sb"][:], in_=bvu[:]),
                    dma=True)
        t_bo = s.op("sync", lambda q: q.dma_start(out=P["bias_sb"][:], in_=bo[:]),
                    dma=True)
        M_done = {}
        with ExitStack() as es2:
            alloc_flash_psum(nc, es2, P)
            t_c1 = s.op("vector", lambda q: q.memset(P["ones_row"][:], 1.0))
            t_c2 = s.op("vector", lambda q: q.memset(P["negscale_row"][:], -scale))
            t_c3 = s.op("vector", lambda q: q.memset(P["onescol"][:], 1.0))
            P["t_ctxfree"] = [t_c1]
            P["t_denfree"] = [t_c3]
            P["t_pm1free"] = {0: t_c2, 1: t_c2}
            P["t_exp_free"] = {}
            for g in range(8):
                for sl in range(4):
                    h = 4 * g + sl
                    for st in range(2):

                        def smm(kc, out_ap, q0, deps, g=g, sl=sl, st=st):
                            return s.op("tensor", lambda q: q.matmul(
                                out_ap,
                                P["ckvf_sb"][32 * sl:32 * sl + 4,
                                             kc * 128:(kc + 1) * 128],
                                P["qp_sb"][32 * sl:32 * sl + 4,
                                           g * 1024 + st * 512:
                                           g * 1024 + st * 512 + 512],
                                start=True, stop=True,
                                tile_position=(32 * sl, 0)),
                                deps=deps + [t_q, t_cf])

                        def pv_fn(kc, q0, eb_ap, first, last, deps):
                            return s.op("tensor", lambda q, kc=kc, eb_ap=eb_ap,
                                        first=first, last=last: q.matmul(
                                P["pCtx"][0:36, 0:512],
                                P["ckvt_sb"][:, kc * 36:(kc + 1) * 36],
                                eb_ap, start=first, stop=last),
                                deps=deps + [t_ct, P["t_ctxfree"][0]])

                        def fin_fn(t_pv, t_exp_last, g=g, sl=sl, st=st):
                            t_l = s.op("scalar", lambda q: q.activation(
                                out=P["dlog"][0:1, 0:512],
                                in_=P["pCtx"][0:1, 0:512], func=AF.Ln),
                                deps=[t_pv])
                            t_r = s.op("scalar", lambda q: q.activation(
                                out=P["dlog"][0:1, 512:1024],
                                in_=P["dlog"][0:1, 0:512],
                                func=AF.Exp, scale=-1.0), deps=[t_l])
                            t_rb = s.op("vector", lambda q: q.tensor_copy(
                                P["recip_bf"][:], P["dlog"][0:1, 512:1024]),
                                deps=[t_r])
                            t_ex = s.op("tensor", lambda q: q.matmul(
                                P["pDen"][:, 0:512], P["ones_row"][:],
                                P["recip_bf"][:], start=True, stop=True),
                                deps=[t_rb, P["t_denfree"][0]])
                            dst = P["M_all"][32 * sl:32 * sl + 4,
                                             g * 1024 + st * 512:
                                             g * 1024 + st * 512 + 512]
                            t_cp = s.op("vector", lambda q: q.tensor_copy(
                                P["cfin"][0:36, :], P["pCtx"][0:36, 0:512]),
                                deps=[t_pv])
                            return s.op("vector", lambda q: q.tensor_tensor(
                                dst, P["cfin"][32:36, :],
                                P["pDen"][32:36, 0:512],
                                op=ALU.mult), deps=[t_ex, t_cp])
                        M_done[(h, st)] = emit_flash_half(
                            s, P, None, h, st, smm, None, None, None, scale,
                            causal=False, pv_fn=pv_fn, fin_fn=fin_fn)
        P["pD"] = [es.enter_context(ps(f"pD{i}", [128, 1024], F32))
                   for i in range(2)]
        P["pR"] = es.enter_context(ps("pR", [128, 1024], F32))
        all_M = list(M_done.values())
        t_ue_prev = [all_M[-1], all_M[-1]]
        ctx_done = {}
        for h in range(32):
            g, sl = h // 4, h % 4
            p = P["pD"][h % 2]
            mms = []
            for hf in range(2):
                mms.append(s.op("tensor", lambda q, h=h, g=g, sl=sl, hf=hf:
                                q.matmul(
                    p[:, hf * 512:(hf + 1) * 512],
                    P["wvu_sb"][32 * sl:32 * sl + 4, h * 128:(h + 1) * 128],
                    P["M_all"][32 * sl:32 * sl + 4,
                               g * 1024 + hf * 512:g * 1024 + (hf + 1) * 512],
                    start=True, stop=True, tile_position=(32 * sl, 0)),
                    deps=[t_wv, M_done[(h, 0)], M_done[(h, 1)],
                          t_ue_prev[h % 2]] + all_M))
            t_ue = s.op("vector", lambda q, h=h, p=p: q.tensor_scalar(
                P["ctx_all"][:, h * 1024:(h + 1) * 1024], p[:, 0:1024],
                P["bvu_sb"][:, h:h + 1], None, op0=ALU.add),
                deps=mms + [t_bv])
            t_ue_prev[h % 2] = t_ue
            ctx_done[h] = t_ue
        src_deps = {k: [ctx_done[k]] if k > 0 else list(ctx_done.values())
                    for k in range(32)}
        t_last = emit_out_proj_residual(s, P, wo, P["bias_sb"], P["ctx_all"],
                                        src_deps, x, y, (P["wa"], P["wb"]))
        s.finish([t_last])
    return nc


def run_attn_l0a(x_shards, inp):
    ident = np.eye(128, dtype=np.float32)
    Wq_d = np.asarray(inp["L0_attn_a_Wq_d"], np.float32)
    Wkv_d = np.asarray(inp["L0_attn_a_Wkv_d"], np.float32)
    W_qk = np.asarray(inp["L0_attn_a_W_qk"], np.float32)
    wqd = np.ascontiguousarray(
        Wq_d.reshape(32, 128, 12).transpose(1, 0, 2).reshape(128, 32 * 12))
    wkvd = np.ascontiguousarray(
        Wkv_d.reshape(32, 128, 4).transpose(1, 0, 2).reshape(128, 32 * 4))
    wqk = np.zeros((128, 128), np.float32)
    wqk[0:12] = W_qk
    def padcol(v):
        o = np.zeros((128, 1), np.float32)
        o[:len(v), 0] = v
        return o
    bqk = np.zeros((128, 8), np.float32)
    bqkv = np.asarray(inp["L0_attn_a_b_qk"], np.float32)
    for g in range(8):
        for sq in range(4):
            bqk[32 * sq:32 * sq + 4, g] = bqkv[4 * (4 * g + sq):4 * (4 * g + sq) + 4]
    args = {"wqd": wqd, "wkvd": wkvd, "wqk": wqk,
            "bqd": padcol(np.asarray(inp["L0_attn_a_bq_d"], np.float32)),
            "bkvd": padcol(np.asarray(inp["L0_attn_a_bkv_d"], np.float32)),
            "bqk": bqk, "ident": ident}
    nc = build_attn_proj_l0a()
    in_maps = [dict(args, x=x_shards[c]) for c in range(8)]
    r = run_bass_kernel_spmd(nc, in_maps, core_ids=list(range(8)))
    q_sh = [as_bf16(r.results[c]["q_sh"]) for c in range(8)]
    ckv_sh = [as_bf16(r.results[c]["ckv_sh"]) for c in range(8)]
    ckvg = []
    for b in range(2):
        g = np.empty((4, 4096), dtype=ckv_sh[0].dtype)
        for j in range(4):
            g[:, _POS_IDX[j]] = np.asarray(ckv_sh[b * 4 + j])[0:4, :]
        ckvg.append(g)
    ckvf, ckvt = [], []
    for b in range(2):
        cf = np.zeros((128, 4096), np.float32)
        for sq in range(4):
            cf[32 * sq:32 * sq + 4] = ckvg[b].astype(np.float32)
        ckvf.append(np.ascontiguousarray(cf.astype(ml_dtypes.bfloat16)))
        ct = np.zeros((128, 32 * 36), np.float32)
        cg = ckvg[b].astype(np.float32)
        for kc in range(32):
            ct[:, kc * 36] = 1.0
            ct[:, kc * 36 + 32:kc * 36 + 36] = cg[:, kc * 128:(kc + 1) * 128].T
        ckvt.append(np.ascontiguousarray(ct.astype(ml_dtypes.bfloat16)))
    Wv_u = np.asarray(inp["L0_attn_a_Wv_u"], np.float32)
    wvu = np.zeros((128, 4096), np.float32)
    for sq in range(4):
        wvu[32 * sq:32 * sq + 4] = Wv_u
    bvu = K_bias32(np.asarray(inp["L0_attn_a_bv_u"], np.float32))
    bo = K_bias32(np.asarray(inp["L0_attn_a_bo"], np.float32))
    wo = tile_w(np.ascontiguousarray(np.asarray(inp["L0_attn_a_Wo"],
                                                np.float32).T))
    nc2 = build_attn_flash_l0a()
    in_maps2 = []
    for c in range(8):
        b = c // 4
        in_maps2.append({"x": x_shards[c], "q_sh": q_sh[c], "ckvf": ckvf[b],
                         "ckvt": ckvt[b], "wvu": wvu, "bvu": bvu, "wo": wo,
                         "bo": bo, "ident": ident})
    r2 = run_bass_kernel_spmd(nc2, in_maps2, core_ids=list(range(8)))
    return [r2.results[c]["y"] for c in range(8)]


def K_bias32(v):
    return bias_cols(v)
